# version 09
# baseline (speedup 1.0000x reference)
import numpy as np
import ml_dtypes
from contextlib import ExitStack

import concourse.bass as bass
import concourse.mybir as mybir
from concourse.bass_utils import run_bass_kernel_spmd

F32 = mybir.dt.float32
F32R = mybir.dt.float32r
BF16 = mybir.dt.bfloat16
AF = mybir.ActivationFunctionType
ALU = mybir.AluOpType
AX = mybir.AxisListType

T = 2048
D = 1024
NT = T // 128
ND = D // 128
C0 = 128
NE0 = 32
H0 = 256
NH = 16
KQ = 64
NE1 = 8
F1 = 2752
NF1 = 22

ENGSEM = {"sync": "sy", "tensor": "pe", "vector": "dv", "scalar": "ac", "gpsimd": "pl"}


class Prog:

    def __init__(self, nc):
        self.nc = nc
        self.ops = {e: [] for e in ("sync", "tensor", "vector", "scalar", "gpsimd")}
        self.cnt = {}
        self.sems = {}

    def _sem(self, name):
        if name not in self.sems:
            self.sems[name] = self.nc.alloc_semaphore(name)
            self.cnt[name] = 0
        return self.sems[name]

    def emit(self, eng, fn, waits=(), incs=()):
        toks = {}
        incs2 = []
        for s, n in incs:
            self._sem(s)
            self.cnt[s] += n
            incs2.append((s, n, self.cnt[s]))
            toks[s] = self.cnt[s]
        waits2 = []
        for s, v in waits:
            self._sem(s)
            waits2.append((s, v))
        self.ops[eng].append((fn, waits2, incs2))
        return toks

    def val(self, sem):
        self._sem(sem)
        return self.cnt[sem]

    def play(self):
        with self.nc.Block() as blk:
            for eng_name, reg in (("sync", blk.sync), ("tensor", blk.tensor),
                                  ("vector", blk.vector), ("scalar", blk.scalar),
                                  ("gpsimd", blk.gpsimd)):
                def body(eng, eng_name=eng_name):
                    for fn, waits, incs in self.ops[eng_name]:
                        for s, v in waits:
                            eng.wait_ge(self.sems[s], v)
                        instr = fn(eng)
                        for s, n, _v in incs:
                            if instr is not None:
                                instr.then_inc(self.sems[s], n)
                            else:
                                eng.sem_inc(self.sems[s], n)
                reg(body)


class Chain:

    def __init__(self, p):
        self.p = p
        self.prev = None

    def step(self, eng, fn, extra_waits=()):
        sem = ENGSEM[eng]
        waits = list(extra_waits)
        if self.prev is not None:
            waits.append(self.prev)
        toks = self.p.emit(eng, fn, waits, [(sem, 1)])
        self.prev = (sem, toks[sem])
        return (sem, toks[sem])

    def wait_also(self, tok):
        if tok is not None:
            self._extra = tok



def prep_host(inputs):
    f = np.float32
    com = {}
    x = np.ascontiguousarray(np.asarray(inputs["L0_attn_x"], f))

    com["wqkvg"] = np.ascontiguousarray(np.concatenate(
        [inputs["L0_attn_Wq"], inputs["L0_attn_Wk"], inputs["L0_attn_Wv"],
         inputs["L0_attn_Wg"]], axis=1).astype(f))
    com["wab"] = np.ascontiguousarray(np.concatenate(
        [inputs["L0_attn_Wa"], inputs["L0_attn_Wb"]], axis=1).astype(f))
    com["bqkvg"] = np.ascontiguousarray(np.concatenate(
        [inputs["L0_attn_bq"], inputs["L0_attn_bk"], inputs["L0_attn_bv"],
         inputs["L0_attn_bg"]]).astype(f)[None, :])
    com["bab"] = np.ascontiguousarray(np.concatenate(
        [inputs["L0_attn_ba"], inputs["L0_attn_bb"]]).astype(f)[None, :])
    com["wo0"] = np.ascontiguousarray(np.asarray(inputs["L0_attn_Wo"], f))
    com["bo0"] = np.ascontiguousarray(np.asarray(inputs["L0_attn_bo"], f)[None, :])

    ident = np.eye(128, dtype=f)
    com["ident"] = ident
    com["ident_b"] = ident.astype(ml_dtypes.bfloat16)
    com["twoI"] = (2.0 * ident)
    r = np.arange(128)
    com["trilcum"] = (r[None, :] >= r[:, None]).astype(f)
    com["smask"] = (r[None, :] < r[:, None]).astype(f)
    com["umask"] = (r[None, :] >= r[:, None]).astype(f)
    com["ones1x512"] = np.ones((1, 128), f)
    com["ones1x512f"] = np.ones((1, 128), f)
    com["ones1b"] = np.ones((1, 128), ml_dtypes.bfloat16)
    com["eyeI"] = ident

    rwt = np.asarray(inputs["L0_ffn_router_w"], f).T
    com["rwT"] = np.ascontiguousarray(
        rwt.reshape(8, 128, 32).transpose(1, 0, 2).reshape(128, 8 * 32))
    com["rb"] = np.ascontiguousarray(np.asarray(inputs["L0_ffn_router_b"], f)[None, :])
    b1 = np.asarray(inputs["L0_ffn_bias1"], f)
    com["b1c"] = np.ascontiguousarray(b1.reshape(32, 2, 128).transpose(2, 0, 1).reshape(128, 64))
    w1 = np.asarray(inputs["L0_ffn_weights1"], f)
    com["w1"] = np.ascontiguousarray(
        w1.reshape(32, 8, 128, 256).transpose(0, 2, 1, 3).reshape(32, 128, 8 * 256))
    w2 = np.asarray(inputs["L0_ffn_weights2"], f)
    com["w2"] = np.ascontiguousarray(
        w2.reshape(32, 2, 128, 1024).transpose(0, 2, 1, 3).reshape(32, 128, 2 * 1024))
    com["b2"] = np.ascontiguousarray(np.asarray(inputs["L0_ffn_bias2"], f))

    def pairtile(w):
        w2 = np.ascontiguousarray(np.asarray(w, f).reshape(1024, 1024))
        return np.ascontiguousarray(
            w2.reshape(8, 128, 8, 128).transpose(2, 1, 0, 3).reshape(8, 128, 1024))
    com["wq1"] = pairtile(inputs["L1_attn_w_q"])
    com["wk1"] = pairtile(inputs["L1_attn_w_k"])
    com["wv1"] = pairtile(inputs["L1_attn_w_v"])
    com["wo1T"] = np.ascontiguousarray(
        np.asarray(inputs["L1_attn_w_o"], f).reshape(1024, 1024).T)
    r512 = np.arange(512)[None, :]
    m128 = np.arange(128)[:, None]
    com["cmask"] = np.ascontiguousarray(np.stack(
        [(r512 >= (m128 + o * 128)).astype(ml_dtypes.bfloat16) for o in range(4)]))
    com["onescol_b"] = np.ones((128, 1), ml_dtypes.bfloat16)

    com["rw1"] = np.ascontiguousarray(
        np.asarray(inputs["L1_ffn_Ws"], f).reshape(8, 128, 8).transpose(1, 0, 2).reshape(128, 64))
    com["rb1"] = np.ascontiguousarray(np.asarray(inputs["L1_ffn_bs"], f)[None, :])
    def ffntile(w):
        wp = np.zeros((8, 1024, NF1 * 128), f)
        wp[:, :, 0:2752] = np.asarray(w, f)
        return np.ascontiguousarray(
            wp.reshape(8, 8, 128, NF1, 128).transpose(0, 3, 2, 1, 4).reshape(8, NF1, 128, 1024))
    com["wg1"] = ffntile(inputs["L1_ffn_Wg"])
    com["wu1"] = ffntile(inputs["L1_ffn_Wu"])
    wd = np.zeros((8, NF1 * 128, 1024), f)
    wd[:, 0:2752, :] = np.asarray(inputs["L1_ffn_Wd"], f)
    com["wd1"] = np.ascontiguousarray(wd.reshape(8, NF1, 128, 1024))

    percore = []
    for b in range(8):
        d = {"x": np.ascontiguousarray(x[b]),
             "xT": np.ascontiguousarray(x[b].T)}
        percore.append(d)
    return com, percore



def alloc_psum(ctx, nc):
    ps = {}
    for i in range(6):
        ps[i] = ctx.enter_context(nc.psum_tensor("ps%d" % i, [128, 512], F32))
    ps["tr"] = ctx.enter_context(nc.psum_tensor("ps_tr", [128, 128], F32))
    ps["trb"] = ctx.enter_context(nc.psum_tensor("ps_trb", [128, 1024], BF16))
    return ps


def stage_a(p, ctx, env, dbg=None):
    nc = p.nc
    ps = env["ps"]
    XT, XFMR = env["XT"], env["XFMR"]
    cst = env["cst"]

    sb = lambda name, shape, dt: ctx.enter_context(nc.sbuf_tensor(name, shape, dt))
    wqkvg = sb("a_wqkvg", [128, 8 * 512], F32R)
    wab = sb("a_wab", [128, 8 * 2], F32R)
    wo = sb("a_wo", [128, 1024], BF16)
    bqkvg = sb("a_bqkvg", [1, 512], F32R)
    bab = sb("a_bab", [1, 2], F32R)
    bo = sb("a_bo", [1, 1024], BF16)
    def db(name, shape, dt):
        return [sb(name + "0", shape, dt), sb(name + "1", shape, dt)]
    qf = db("a_qf", [128, 128], F32)
    kf = db("a_kf", [128, 128], F32)
    kt = db("a_kt", [128, 128], F32)
    vt = db("a_vt", [128, 128], F32)
    gtm = db("a_gtm", [128, 128], F32)
    ELt = db("a_elt", [128, 128], F32)
    Zt = db("a_z", [128, 128], F32)
    colsb = db("a_cols", [128, 24], F32)
    lar2 = db("a_lar", [128, 2], F32)
    lrow2 = db("a_lrow", [1, 128], F32)
    lcol2 = db("a_lcolr", [128, 1], F32)
    qtm = sb("a_qtm", [128, 128], F32)
    scr = sb("a_scr", [128, 128], F32)
    ELl = sb("a_ell", [128, 128], F32)
    Ast = sb("a_ast", [128, 128], F32)
    ZTt = sb("a_zt", [128, 128], F32)
    Vn = sb("a_vn", [128, 128], F32)
    m1 = sb("a_m1", [128, 128], F32)
    Bst = sb("a_bst", [128, 128], F32)
    Wfm = sb("a_wfm", [128, 128], F32)
    Wtm = sb("a_wtm", [128, 128], F32)
    Wp = sb("a_wp", [128, 128], F32)
    ktw = sb("a_ktw", [128, 128], F32)
    Pt = sb("a_pt", [128, 128], F32)
    Otm = sb("a_otm", [128, 128], BF16)
    Ofm = sb("a_ofm", [128, 128], BF16)
    Sst = sb("a_s", [128, 128], F32)
    Rst = sb("a_r", [128, 128], F32)
    m2 = sb("a_m2", [128, 128], F32)
    zres = sb("a_zres", [128, 1024], F32)
    lnt = alloc_lnt(ctx, nc, "a")
    rstd = lnt["rstd"]

    p._sem("pe"); p._sem("ac"); p._sem("dv"); p._sem("w")

    def loads(eng):
        last = None
        for dc in range(ND):
            last = eng.dma_start(wqkvg[:, dc * 512:(dc + 1) * 512],
                                 env["d_wqkvg"][dc * 128:(dc + 1) * 128, :])
            last.then_inc(p.sems["w"], 16)
            last = eng.dma_start(wab[:, dc * 2:(dc + 1) * 2],
                                 env["d_wab"][dc * 128:(dc + 1) * 128, :])
            last.then_inc(p.sems["w"], 16)
        for nm, tile in (("d_bqkvg", bqkvg), ("d_bab", bab)):
            last = eng.dma_start(tile[:], env[nm])
            last.then_inc(p.sems["w"], 16)
        return None
    pload = env.get("pload", p)
    if pload is not p:
        pload.sems["w"] = p.sems["w"]
        pload.cnt["w"] = 0
    pload.emit("sync", loads, waits=(), incs=())
    def loads_cast(eng):
        eng.dma_start(wo[:], env["d_wo0"]).then_inc(p.sems["w"], 16)
        eng.dma_start(bo[:], env["d_bo0"]).then_inc(p.sems["w"], 16)
        return None
    pload.emit("gpsimd", loads_cast, waits=(), incs=())
    n_w = 20 * 16
    p.cnt["w"] += n_w
    if pload is not p:
        def wwait(eng):
            eng.wait_ge(p.sems["w"], n_w)
            return None
        pload.emit("sync", wwait)
    wtok = ("w", p.val("w"))
    ld_x = env["xload_tok"]

    tk = p.emit("vector", lambda e: e.tensor_scalar(Sst[:], cst["eyeI"][:], 0.0, None, ALU.mult),
                waits=[ld_x] if ld_x else [], incs=[("dv", 1)])
    tk = p.emit("vector", lambda e: e.tensor_scalar(Rst[:], cst["eyeI"][:], 0.0, None, ALU.mult),
                waits=(), incs=[("dv", 1)])
    tk = p.emit("vector", lambda e: e.tensor_copy(lar2[0][:, 1:2], cst["eyeI"][:, 0:1]),
                waits=(), incs=[("dv", 1)])
    tk = p.emit("vector", lambda e: e.tensor_copy(lar2[1][:, 1:2], cst["eyeI"][:, 0:1]),
                waits=(), incs=[("dv", 1)])
    init_tok = ("dv", tk["dv"])

    sec1_end = {}
    sec2_end = {-1: init_tok}

    def emit_sec1(ci):
        pr = ci % 2
        cols = colsb[pr]
        col = lambda i: cols[:, i:i + 1]
        lar, lrow, lcolr = lar2[pr], lrow2[pr], lcol2[pr]
        ch = Chain(p)
        w0 = [t for t in (sec2_end.get(ci - 2), sec1_end.get(ci - 1), wtok, ld_x) if t]
        tok0 = ci * 128
        xv = lambda dc, tok0=tok0: XFMR[:, dc * 2048 + tok0: dc * 2048 + tok0 + 128]

        def proj(eng, xv=xv):
            for dc in range(ND):
                eng.matmul(ps[0][:], xv(dc), wqkvg[:, dc * 512:(dc + 1) * 512],
                           start=(dc == 0), stop=False)
            return eng.matmul(ps[0][:], cst["ones1x512"][:1, 0:128], bqkvg[:],
                              start=False, stop=True)
        ch.step("tensor", proj, extra_waits=w0)
        def proj2(eng, xv=xv):
            for dc in range(ND):
                eng.matmul(ps[1][:, 0:2], xv(dc), wab[:, dc * 2:(dc + 1) * 2],
                           start=(dc == 0), stop=False)
            return eng.matmul(ps[1][:, 0:2], cst["ones1x512"][:1, 0:128], bab[:],
                              start=False, stop=True)
        ch.step("tensor", proj2)
        ch.step("vector", lambda e: e.tensor_copy(qtm[:], ps[0][:, 0:128]))
        ch.step("scalar", lambda e: e.activation(scr[:], ps[0][:, 128:256], AF.Square,
                                                 accum_out=col(0)))
        ch.step("scalar", lambda e: e.activation(col(1), col(0), AF.Ln))
        ch.step("scalar", lambda e: e.activation(col(2), col(1), AF.Exp, scale=-0.5))
        ch.step("vector", lambda e: e.tensor_scalar_min(col(2), col(2), 1e12))
        ch.step("vector", lambda e, pr=pr: e.tensor_scalar(kt[pr][:], ps[0][:, 128:256],
                                                           col(2), None, ALU.mult))
        ch.step("vector", lambda e, pr=pr: e.tensor_copy(vt[pr][:], ps[0][:, 256:384]))
        ch.step("scalar", lambda e: e.activation(gtm[pr][:], ps[0][:, 384:512], AF.Exp,
                                                 scale=-1.0) if False else
                e.activation(gtm[pr][:], ps[0][:, 384:512], AF.Exp, scale=-1.0))
        ch.step("vector", lambda e, pr=pr: e.tensor_scalar_add(gtm[pr][:], gtm[pr][:], 1.0))
        ch.step("vector", lambda e, pr=pr: e.reciprocal(gtm[pr][:], gtm[pr][:]))
        ch.step("scalar", lambda e: e.activation(cols[:, 4:6], ps[1][:, 0:2], AF.Abs))
        ch.step("scalar", lambda e: e.activation(cols[:, 6:8], cols[:, 4:6], AF.Exp, scale=-1.0))
        ch.step("vector", lambda e: e.tensor_scalar_add(cols[:, 6:8], cols[:, 6:8], 1.0))
        ch.step("scalar", lambda e: e.activation(cols[:, 8:10], cols[:, 6:8], AF.Ln))
        ch.step("scalar", lambda e: e.activation(cols[:, 4:6], ps[1][:, 0:2], AF.Relu, scale=-1.0))
        ch.step("vector", lambda e: e.tensor_add(cols[:, 8:10], cols[:, 8:10], cols[:, 4:6]))
        ch.step("vector", lambda e, lar=lar: e.tensor_scalar(lar[:, 0:1], cols[:, 8:9],
                                                             -1.0, None, ALU.mult))
        ch.step("scalar", lambda e: e.activation(col(3), cols[:, 9:10], AF.Exp, scale=-1.0))
        ch.step("tensor", lambda e: e.transpose(ps["tr"][:], qtm[:], cst["ident"][:]))
        ch.step("vector", lambda e, pr=pr: e.tensor_copy(qf[pr][:], ps["tr"][:]))
        ch.step("tensor", lambda e, pr=pr: e.transpose(ps["tr"][:], kt[pr][:], cst["ident"][:]))
        ch.step("vector", lambda e, pr=pr: e.tensor_copy(kf[pr][:], ps["tr"][:]))
        ch.step("tensor", lambda e, lar=lar: e.matmul(ps[2][:, 0:2], cst["trilcum"][:],
                                                      lar[:, 0:2], start=True, stop=True))
        ch.step("tensor", lambda e, lar=lar: e.matmul(ps[2][0:2, 4:132], lar[:, 0:2],
                                                      cst["trilcum"][:], start=True, stop=True))
        ch.step("vector", lambda e, lcolr=lcolr: e.tensor_copy(lcolr[:], ps[2][:, 0:1]))
        ch.step("scalar", lambda e: e.activation(col(11), ps[2][:, 0:1], AF.Exp))
        ch.step("vector", lambda e, lrow=lrow: e.tensor_copy(lrow[:], ps[2][0:1, 4:132]))
        ch.step("tensor", lambda e, lrow=lrow: e.matmul(ps[3][:, 0:128],
                                                        cst["ones1x512f"][:1, 0:128],
                                                        lrow[:], start=True, stop=True))
        ch.step("vector", lambda e, lcolr=lcolr: e.tensor_scalar(m1[:], ps[3][:, 0:128],
                                                                 lcolr[:], 0.0,
                                                                 ALU.subtract, ALU.max))
        ch.step("scalar", lambda e: e.activation(ELl[:], m1[:], AF.Exp, scale=-1.0))
        ch.step("vector", lambda e, lcolr=lcolr: e.tensor_scalar(m1[:], ps[3][:, 0:128],
                                                                 lcolr[:], 0.0,
                                                                 ALU.subtract, ALU.min))
        ch.step("scalar", lambda e, pr=pr: e.activation(ELt[pr][:], m1[:], AF.Exp))
        ch.step("tensor", lambda e, pr=pr: e.matmul(ps[4][:, 0:128], kf[pr][:], kf[pr][:],
                                                    start=True, stop=True))
        ch.step("vector", lambda e: e.tensor_mul(m1[:], ps[4][:, 0:128], ELl[:]))
        ch.step("vector", lambda e: e.tensor_scalar(m1[:], m1[:], col(3), None, ALU.mult))
        ch.step("vector", lambda e: e.tensor_mul(m1[:], m1[:], cst["smask"][:]))
        ch.step("vector", lambda e: e.tensor_add(Ast[:], m1[:], cst["eyeI"][:]))
        ch.step("vector", lambda e, pr=pr: e.tensor_copy(Zt[pr][:], cst["eyeI"][:]))
        ch.step("vector", lambda e: e.tensor_copy(ZTt[:], cst["eyeI"][:]))
        for it in range(7):
            ch.step("tensor", lambda e, pr=pr: e.matmul(ps[4][:, 128:256], Ast[:], Zt[pr][:],
                                                        start=True, stop=True))
            ch.step("vector", lambda e: e.scalar_tensor_tensor(
                Vn[:], ps[4][:, 128:256], -1.0, cst["twoI"][:], ALU.mult, ALU.add))
            if it < 6:
                def nz(eng):
                    eng.matmul(ps[4][:, 256:384], ZTt[:], Vn[:], start=True, stop=True)
                    return eng.matmul(ps[4][:, 384:512], Vn[:], ZTt[:], start=True, stop=True)
                ch.step("tensor", nz)
                ch.step("vector", lambda e, pr=pr: e.tensor_copy(Zt[pr][:], ps[4][:, 256:384]))
                ch.step("vector", lambda e: e.tensor_copy(ZTt[:], ps[4][:, 384:512]))
            else:
                ch.step("tensor", lambda e: e.matmul(ps[4][:, 256:384], ZTt[:], Vn[:],
                                                     start=True, stop=True))
                ch.step("vector", lambda e, pr=pr: e.tensor_copy(Zt[pr][:], ps[4][:, 256:384]))
        sec1_end[ci] = ch.prev

    def emit_sec2(ci):
        pr = ci % 2
        cols = colsb[pr]
        col = lambda i: cols[:, i:i + 1]
        lrow, lcolr = lrow2[pr], lcol2[pr]
        ch = Chain(p)
        w0 = [t for t in (sec1_end[ci], sec2_end.get(ci - 1)) if t]
        ch.step("tensor", lambda e, pr=pr: e.matmul(ps[5][:, 0:128], kf[pr][:], Rst[:],
                                                    start=True, stop=True), extra_waits=w0)
        ch.step("vector", lambda e: e.tensor_scalar(col(13), col(3), col(11), -1.0,
                                                    ALU.mult, ALU.mult))
        ch.step("vector", lambda e, pr=pr: e.tensor_scalar(m2[:], vt[pr][:], col(3),
                                                           None, ALU.mult))
        ch.step("vector", lambda e: e.scalar_tensor_tensor(
            Bst[:], ps[5][:, 0:128], col(13), m2[:], ALU.mult, ALU.add))
        def wmm(eng, pr=pr):
            eng.matmul(ps[5][:, 128:256], Bst[:], Zt[pr][:], start=True, stop=True)
            return eng.matmul(ps[5][:, 256:384], Zt[pr][:], Bst[:], start=True, stop=True)
        ch.step("tensor", wmm)
        ch.step("vector", lambda e: e.tensor_copy(Wfm[:], ps[5][:, 128:256]))
        ch.step("vector", lambda e: e.tensor_copy(Wtm[:], ps[5][:, 256:384]))
        def qmm(eng, pr=pr):
            eng.matmul(ps[5][:, 0:128], qf[pr][:], Sst[:], start=True, stop=True)
            return eng.matmul(ps[5][:, 384:512], Wfm[:], qf[pr][:], start=True, stop=True)
        ch.step("tensor", qmm)
        ch.step("vector", lambda e, pr=pr: e.tensor_mul(m2[:], ps[5][:, 384:512], ELt[pr][:]))
        ch.step("vector", lambda e: e.tensor_mul(Pt[:], m2[:], cst["umask"][:]))
        ch.step("vector", lambda e: e.tensor_scalar(m2[:], ps[5][:, 0:128], col(11),
                                                    None, ALU.mult))
        ch.step("tensor", lambda e, pr=pr: e.matmul(ps[5][:, 128:256], Pt[:], kt[pr][:],
                                                    start=True, stop=True))
        ch.step("vector", lambda e: e.tensor_add(m2[:], m2[:], ps[5][:, 128:256]))
        ch.step("vector", lambda e, pr=pr: e.tensor_mul(Otm[:], m2[:], gtm[pr][:]))
        ch.step("tensor", lambda e, lrow=lrow: e.matmul(ps[5][:, 256:258],
                                                        cst["ones1x512f"][:1, 0:128],
                                                        lrow[:1, 126:128],
                                                        start=True, stop=True))
        ch.step("vector", lambda e, lcolr=lcolr: e.tensor_scalar(col(15), ps[5][:, 257:258],
                                                                 lcolr[:], None, ALU.subtract))
        ch.step("scalar", lambda e: e.activation(col(16), col(15), AF.Exp))
        ch.step("scalar", lambda e: e.activation(col(17), ps[5][:, 257:258], AF.Exp))
        ch.step("vector", lambda e: e.tensor_scalar(Wp[:], Wtm[:], col(16), None, ALU.mult))
        ch.step("vector", lambda e, pr=pr: e.tensor_scalar(ktw[:], kt[pr][:], col(16),
                                                           None, ALU.mult))
        def smm(eng, pr=pr):
            eng.matmul(ps[5][:, 0:128], Wp[:], kt[pr][:], start=True, stop=True)
            return eng.matmul(ps[5][:, 128:256], ktw[:], Wtm[:], start=True, stop=True)
        ch.step("tensor", smm)
        ch.step("vector", lambda e: e.scalar_tensor_tensor(
            Sst[:], Sst[:], col(17), ps[5][:, 0:128], ALU.mult, ALU.add))
        ch.step("vector", lambda e: e.scalar_tensor_tensor(
            Rst[:], Rst[:], col(17), ps[5][:, 128:256], ALU.mult, ALU.add))
        ch.step("tensor", lambda e: e.transpose(ps["trb"][:, 0:128], Otm[:],
                                                cst["ident_b"][:]))
        ch.step("vector", lambda e: e.tensor_copy(Ofm[:], ps["trb"][:, 0:128]))
        for h in range(2):
            def ymm(eng, h=h):
                eng.matmul(ps[5][:], Ofm[:], wo[:, h * 512:(h + 1) * 512],
                           start=True, stop=False)
                return eng.matmul(ps[5][:], cst["ones1b"][:1, 0:128],
                                  bo[:1, h * 512:(h + 1) * 512], start=False, stop=True)
            ch.step("tensor", ymm)
            ch.step("vector", lambda e, h=h, ci=ci: e.tensor_add(
                zres[:, h * 512:(h + 1) * 512], ps[5][:],
                XT[:, ci * 1024 + h * 512: ci * 1024 + (h + 1) * 512]))
        emit_ln(ch, env, lnt, zres[:], XT[:, ci * 1024:(ci + 1) * 1024])
        sec2_end[ci] = ch.prev

    emit_sec1(0)
    for ci in range(NT):
        if ci + 1 < NT:
            emit_sec1(ci + 1)
        emit_sec2(ci)

    chf = Chain(p)
    chf.prev = sec2_end[NT - 1]
    return chf


def emit_ln(ch, env, lnt, zres, xt_dst):
    AFc = AF
    ch.step("scalar", lambda e: e.activation(lnt["junk"][:], zres, AFc.Copy,
                                             accum_out=lnt["c18"][:]))
    ch.step("vector", lambda e: e.scalar_tensor_tensor(lnt["junk"][:], zres, 1.0, zres,
                                                       ALU.mult, ALU.mult,
                                                       accum_out=lnt["c19"][:]))
    ch.step("vector", lambda e: e.tensor_scalar(lnt["c20"][:], lnt["c18"][:], 1.0 / 1024.0,
                                                None, ALU.mult))
    ch.step("vector", lambda e: e.tensor_scalar(lnt["c21"][:], lnt["c19"][:], 1.0 / 1024.0,
                                                None, ALU.mult))
    ch.step("vector", lambda e: e.tensor_scalar(lnt["c22"][:], lnt["c20"][:], lnt["c20"][:],
                                                -1.0, ALU.mult, ALU.mult))
    ch.step("vector", lambda e: e.tensor_add(lnt["c23"][:], lnt["c22"][:], lnt["c21"][:]))
    ch.step("vector", lambda e: e.tensor_scalar_add(lnt["c23"][:], lnt["c23"][:], 1e-5))
    ch.step("scalar", lambda e: e.activation(lnt["c1"][:], lnt["c23"][:], AFc.Ln))
    ch.step("scalar", lambda e: e.activation(lnt["rstd"][:], lnt["c1"][:], AFc.Exp, scale=-0.5))
    ch.step("vector", lambda e: e.tensor_scalar(xt_dst, zres, lnt["c20"][:], lnt["rstd"][:],
                                                ALU.subtract, ALU.mult))


def alloc_lnt(ctx, nc, tag, junk=None):
    sb = lambda name, shape, dt: ctx.enter_context(nc.sbuf_tensor(name, shape, dt))
    cols = sb(tag + "_lncols", [128, 8], F32)
    return {"junk": junk if junk is not None else sb(tag + "_junk", [128, 1024], F32),
            "c18": cols[:, 0:1], "c19": cols[:, 1:2], "c20": cols[:, 2:3],
            "c21": cols[:, 3:4], "c22": cols[:, 4:5], "c23": cols[:, 5:6],
            "c1": cols[:, 6:7], "rstd": cols[:, 7:8]}


def emit_router(p, ch, env, ctx_tiles, n_e, rw, rb, lgt_cb):
    nc = p.nc
    ps = env["ps"]
    XT = env["XT"]
    cst = env["cst"]
    xtr = ctx_tiles["xtr"]
    for ci in range(NT):
        for dc in range(ND):
            ch.step("tensor", lambda e, ci=ci, dc=dc: e.transpose(
                ps["tr"][:], XT[:, ci * 1024 + dc * 128: ci * 1024 + (dc + 1) * 128],
                cst["ident"][:]))
            ch.step("vector", lambda e, dc=dc: e.tensor_copy(
                xtr[:, dc * 128:(dc + 1) * 128], ps["tr"][:]))
        def rmm(eng, n_e=n_e):
            for dc in range(ND):
                eng.matmul(ps[5][:, 0:n_e], xtr[:, dc * 128:(dc + 1) * 128],
                           rw[:, dc * n_e:(dc + 1) * n_e], start=(dc == 0), stop=False)
            return eng.matmul(ps[5][:, 0:n_e], cst["ones1x512f"][:1, 0:128], rb[:],
                              start=False, stop=True)
        ch.step("tensor", rmm)
        lgt_cb(ci)


def stage_b(p, ctx, env):
    nc = p.nc
    ps = env["ps"]
    XT, XFMB = env["XT"], env["XFMB"]
    cst = env["cst"]
    sb = lambda name, shape, dt: ctx.enter_context(nc.sbuf_tensor(name, shape, dt))

    rw = sb("b_rw", [128, ND * 32], F32)
    rb = sb("b_rb", [1, 32], F32)
    b1c = sb("b_b1c", [128, 64], F32)
    w1b = [sb("b_w1b%d" % i, [128, 8 * 256], BF16) for i in range(2)]
    w2b = [sb("b_w2b%d" % i, [128, 2 * 1024], BF16) for i in range(2)]
    b2r = [sb("b_b2r%d" % i, [1, 1024], BF16) for i in range(2)]
    hfm = sb("b_hfm", [128, 2 * 512], BF16)
    acc = sb("b_acc", [128, 4 * 1024], F32)
    xtr = sb("b_xtr", [128, 1024], F32)
    lgt = sb("b_lgt", [128, 32], F32)
    msk = sb("b_msk", [128, NT * 32], F32)
    mcol = sb("b_mcol", [128, 1], F32)
    tmp = sb("b_tmp", [128, 512], BF16)
    zres = sb("b_zres", [128, 1024], F32)
    lnt = alloc_lnt(ctx, nc, "b")

    ch = Chain(p)
    ch.prev = env.get("start_tok")

    p._sem("w")
    def loads(eng):
        eng.dma_start(rw[:], env["d_rwT"]).then_inc(p.sems["w"], 16)
        eng.dma_start(rb[:], env["d_rb"]).then_inc(p.sems["w"], 16)
        eng.dma_start(b1c[:], env["d_b1c"]).then_inc(p.sems["w"], 16)
        return None
    p.emit("sync", loads, waits=[env["xload_tok"]] if env.get("xload_tok") else [])
    p.cnt["w"] += 48
    wtok0 = ("w", p.val("w"))
    if ch.prev is None:
        ch.prev = wtok0

    def lgt_cb(ci):
        ch.step("vector", lambda e: e.tensor_copy(lgt[:], ps[5][:, 0:32]),
                extra_waits=[wtok0] if ci == 0 else [])
        ch.step("vector", lambda e: e.reduce_max(mcol[:], lgt[:], axis=AX.X))
        ch.step("vector", lambda e, ci=ci: e.tensor_scalar(
            msk[:, ci * 32:(ci + 1) * 32], lgt[:], mcol[:], None, ALU.is_equal))
        if env.get("dbgb") and ci == 0:
            def dlgt(eng):
                return eng.dma_start(env["dbgb"]["lgt0"], lgt[:])
            prev = ch.prev
            tk = p.emit("sync", dlgt, waits=[prev], incs=[("sy", 16)])
            ch.prev = ("sy", tk["sy"])
    emit_router(p, ch, env, {"xtr": xtr}, 32, rw, rb, lgt_cb)
    if env.get("dbgb"):
        def dmsk(eng):
            return eng.dma_start(env["dbgb"]["msk"], msk[:])
        prev = ch.prev
        tk = p.emit("sync", dmsk, waits=[prev], incs=[("sy", 16)])
        ch.prev = ("sy", tk["sy"])

    p._sem("pl"); p._sem("pe"); p._sem("ac"); p._sem("dv")
    wtok = {}
    tmp2 = [tmp, ctx.enter_context(nc.sbuf_tensor("b_tmp2", [128, 512], BF16))]

    def emit_wload(s, e_i, wait_tok):
        par = e_i % 2
        def wload(eng, e_i=e_i, par=par):
            eng.dma_start(w1b[par][:], env["d_w1"][e_i]).then_inc(p.sems["pl"], 16)
            eng.dma_start(w2b[par][:], env["d_w2"][e_i]).then_inc(p.sems["pl"], 16)
            eng.dma_start(b2r[par][:], env["d_b2"][e_i:e_i + 1, :]
                          ).then_inc(p.sems["pl"], 16)
            return None
        p.emit("gpsimd", wload, waits=[wait_tok] if wait_tok else [])
        p.cnt["pl"] += 48
        wtok[(s, e_i)] = ("pl", p.val("pl"))

    start_tok = ch.prev
    stage_start = env.get("start_tok")
    hbank_guard = [start_tok, start_tok]
    ybank_guard = [start_tok] * 4
    hfm_guard = None
    wbuf_guard = [stage_start, stage_start]
    acc_guard = {}
    tmp_guard = [None, None]

    for s in range(4):
        tkm = p.emit("vector", lambda e: e.memset(acc[:], 0.0),
                     waits=[start_tok], incs=[("dv", 1)])
        acc_init = ("dv", tkm["dv"])
        for sub in range(4):
            for half in range(2):
                acc_guard[(sub, half)] = acc_init
        emit_wload(s, 0, wbuf_guard[0])
        emit_wload(s, 1, wbuf_guard[1])
        for e_i in range(NE0):
            par = e_i % 2
            relu_toks = []
            for hc in range(2):
                def hmm2(eng, s=s, par=par, hc=hc):
                    last = None
                    for dc in range(ND):
                        last = eng.matmul(
                            ps[hc][:],
                            w1b[par][:, dc * 256 + hc * 128: dc * 256 + (hc + 1) * 128],
                            XFMB[:, dc * 2048 + s * 512: dc * 2048 + (s + 1) * 512],
                            start=(dc == 0), stop=(dc == ND - 1))
                    return last
                waits = [wtok[(s, e_i)], hbank_guard[hc]]
                if hc == 0 and hfm_guard is not None:
                    waits.append(hfm_guard)
                tk = p.emit("tensor", hmm2, waits=waits, incs=[("pe", 1)])
                mm_tok = ("pe", tk["pe"])
                tk = p.emit("scalar",
                            lambda e, e_i=e_i, hc=hc: e.activation(
                                hfm[:, hc * 512:(hc + 1) * 512], ps[hc][:], AF.Relu,
                                bias=b1c[:, e_i * 2 + hc: e_i * 2 + hc + 1]),
                            waits=[mm_tok], incs=[("ac", 1)])
                hbank_guard[hc] = ("ac", tk["ac"])
                relu_toks.append(hbank_guard[hc])
            last_ymm = None
            for sub in range(4):
                for half in range(2):
                    bk = (sub * 2 + half) % 4
                    def ymm(eng, par=par, sub=sub, half=half, bk=bk):
                        for hc in range(2):
                            eng.matmul(ps[2 + bk][:],
                                       hfm[:, hc * 512 + sub * 128: hc * 512 + (sub + 1) * 128],
                                       w2b[par][:, hc * 1024 + half * 512: hc * 1024 + (half + 1) * 512],
                                       start=(hc == 0), stop=False)
                        return eng.matmul(ps[2 + bk][:], cst["ones1b"][:1, 0:128],
                                          b2r[par][:1, half * 512:(half + 1) * 512],
                                          start=False, stop=True)
                    waits = [ybank_guard[bk]]
                    if sub == 0 and half == 0:
                        waits.append(relu_toks[1])
                    tk = p.emit("tensor", ymm, waits=waits, incs=[("pe", 1)])
                    last_ymm = ("pe", tk["pe"])
                    tpar = (sub * 2 + half) % 2
                    awaits = [last_ymm]
                    if tmp_guard[tpar] is not None:
                        awaits.append(tmp_guard[tpar])
                    tk = p.emit("scalar",
                                lambda e, s=s, sub=sub, e_i=e_i, tpar=tpar, bk=bk: e.activation(
                                    tmp2[tpar][:], ps[2 + bk][:], AF.Relu,
                                    scale=msk[:, (s * 4 + sub) * 32 + e_i:
                                              (s * 4 + sub) * 32 + e_i + 1]),
                                waits=awaits, incs=[("ac", 1)])
                    ac_tok = ("ac", tk["ac"])
                    ybank_guard[bk] = ac_tok
                    tk = p.emit("vector",
                                lambda e, sub=sub, half=half, tpar=tpar: e.tensor_add(
                                    acc[:, sub * 1024 + half * 512: sub * 1024 + (half + 1) * 512],
                                    acc[:, sub * 1024 + half * 512: sub * 1024 + (half + 1) * 512],
                                    tmp2[tpar][:]),
                                waits=[ac_tok, acc_guard[(sub, half)]], incs=[("dv", 1)])
                    acc_guard[(sub, half)] = ("dv", tk["dv"])
                    tmp_guard[tpar] = acc_guard[(sub, half)]
            hfm_guard = last_ymm
            wbuf_guard[par] = last_ymm
            if e_i + 2 < NE0:
                emit_wload(s, e_i + 2, wbuf_guard[par])
        ch.prev = ("dv", p.val("dv"))
        for sub in range(4):
            ci = s * 4 + sub
            ch.step("vector", lambda e, ci=ci, sub=sub: e.tensor_add(
                zres[:], acc[:, sub * 1024:(sub + 1) * 1024],
                XT[:, ci * 1024:(ci + 1) * 1024]))
            emit_ln(ch, env, lnt, zres[:], XT[:, ci * 1024:(ci + 1) * 1024])
        start_tok = ch.prev
    return ch


def stage_c(p, ctx, env):
    nc = p.nc
    ps = env["ps"]
    XT, XFMB = env["XT"], env["XFMB"]
    cst = env["cst"]
    sb = lambda name, shape, dt: ctx.enter_context(nc.sbuf_tensor(name, shape, dt))

    woTb = sb("c_wot", [128, 8 * 1024], BF16)
    wqp = [sb("c_wqp%d" % i, [128, 1024], BF16) for i in range(2)]
    wkp = [sb("c_wkp%d" % i, [128, 1024], BF16) for i in range(2)]
    wvp = [sb("c_wvp%d" % i, [128, 1024], BF16) for i in range(2)]
    qf = sb("c_qf", [128, 2048], BF16)
    kf = sb("c_kf", [128, 2048], BF16)
    vt = sb("c_vt", [128, 16 * 130], BF16)
    sq = sb("c_sq", [128, 2048], BF16)
    qrow = sb("c_qrow", [1, 2048], F32)
    krow = qrow
    scol = sb("c_scol", [1, 8], F32)
    negb = sb("c_negb", [128, 1], F32)
    pt2 = [sb("c_pt%d" % i, [128, 512], BF16) for i in range(2)]
    oall = sb("c_oall", [128, NT * 1024], BF16)
    ofm = sb("c_ofm", [128, 1024], BF16)
    cmask = sb("c_cmask", [128, 4 * 512], BF16)
    onescol = sb("c_onescol", [128, 1], BF16)
    rcol = sb("c_rcol", [128, 1], F32)
    zres = sb("c_zres", [128, 1024], F32)
    lnt = alloc_lnt(ctx, nc, "c")

    ch = Chain(p)
    ch.prev = env.get("start_tok")
    stage_start = env.get("start_tok")

    p._sem("w"); p._sem("pe"); p._sem("ac"); p._sem("dv")
    def loads(eng):
        for o in range(4):
            eng.dma_start(cmask[:, o * 512:(o + 1) * 512],
                          env["d_cmask"][o]).then_inc(p.sems["w"], 16)
        eng.dma_start(onescol[:], env["d_onescol"]).then_inc(p.sems["w"], 16)
        return None
    p.emit("sync", loads, waits=[env["xload_tok"]] if env.get("xload_tok") else [])
    def loads2(eng):
        for c in range(8):
            eng.dma_start(woTb[:].rearrange("p (c e) -> p c e", c=8)[:, c, :],
                          env["d_wo1T"][c * 128:(c + 1) * 128, :]).then_inc(p.sems["w"], 16)
        return None
    p.emit("gpsimd", loads2, waits=[env["xload_tok"]] if env.get("xload_tok") else [])
    p.cnt["w"] += 13 * 16
    wtok0 = ("w", p.val("w"))
    if ch.prev is None:
        ch.prev = wtok0

    p._sem("pl")
    wtok = {}
    wbuf_guard = [stage_start, stage_start]

    def emit_wload(pr, wait_tok):
        par = pr % 2
        def wload(eng, pr=pr, par=par):
            eng.dma_start(wqp[par][:], env["d_wq1"][pr]).then_inc(p.sems["pl"], 16)
            eng.dma_start(wkp[par][:], env["d_wk1"][pr]).then_inc(p.sems["pl"], 16)
            eng.dma_start(wvp[par][:], env["d_wv1"][pr]).then_inc(p.sems["pl"], 16)
            return None
        p.emit("gpsimd", wload, waits=[wait_tok] if wait_tok else [])
        p.cnt["pl"] += 48
        wtok[pr] = ("pl", p.val("pl"))

    emit_wload(0, wbuf_guard[0])
    emit_wload(1, wbuf_guard[1])

    qkv_bank_guard = [ch.prev, ch.prev]
    oall_guard = None
    for pr in range(8):
        par = pr % 2
        gi = 0
        last_wread = None
        qkv_evacs = []
        for tile, wp in ((qf, wqp[par]), (kf, wkp[par])):
            for st in range(4):
                bk = gi % 2
                def qmm(eng, wp=wp, st=st, bk=bk):
                    last = None
                    for dc in range(ND):
                        last = eng.matmul(ps[bk][:],
                                          wp[:, dc * 128:(dc + 1) * 128],
                                          XFMB[:, dc * 2048 + st * 512: dc * 2048 + (st + 1) * 512],
                                          start=(dc == 0), stop=(dc == ND - 1))
                    return last
                waits = [wtok[pr], qkv_bank_guard[bk]]
                tk = p.emit("tensor", qmm, waits=waits, incs=[("pe", 1)])
                last_wread = ("pe", tk["pe"])
                tk = p.emit("vector",
                            lambda e, tile=tile, st=st, bk=bk: e.tensor_copy(
                                tile[:, st * 512:(st + 1) * 512], ps[bk][:]),
                            waits=[last_wread], incs=[("dv", 1)])
                qkv_bank_guard[bk] = ("dv", tk["dv"])
                qkv_evacs.append(qkv_bank_guard[bk])
                gi += 1
        vw = [qkv_bank_guard[0]]
        if wbuf_guard[par]:
            vw.append(wbuf_guard[par])
        tk = p.emit("vector", lambda e: e.memset(vt[:], 1.0),
                    waits=vw, incs=[("dv", 1)])
        vt_ready = ("dv", tk["dv"])
        for mi in range(16):
            bk = gi % 2
            def vmm(eng, par=par, mi=mi, bk=bk):
                last = None
                for dc in range(ND):
                    last = eng.matmul(ps[bk][:, 0:128],
                                      XFMB[:, dc * 2048 + mi * 128: dc * 2048 + (mi + 1) * 128],
                                      wvp[par][:, dc * 128:(dc + 1) * 128],
                                      start=(dc == 0), stop=(dc == ND - 1))
                return last
            tk = p.emit("tensor", vmm, waits=[qkv_bank_guard[bk], vt_ready],
                        incs=[("pe", 1)])
            last_wread = ("pe", tk["pe"])
            tk = p.emit("vector",
                        lambda e, mi=mi, bk=bk: e.tensor_copy(
                            vt[:].rearrange("p (m hc) -> p m hc", m=16)[:, mi, :]
                                .rearrange("p (h c) -> p h c", h=2)[:, :, 0:64],
                            ps[bk][:, 0:128].rearrange("p (h c) -> p h c", h=2)),
                        waits=[last_wread], incs=[("dv", 1)])
            qkv_bank_guard[bk] = ("dv", tk["dv"])
            qkv_evacs.append(qkv_bank_guard[bk])
            gi += 1
        if pr + 2 < 8:
            emit_wload(pr + 2, last_wread)
        wbuf_guard[par] = last_wread

        ch.prev = qkv_evacs[-1]
        for qk, tile in ((0, qf), (2, kf)):
            ch.step("scalar", lambda e, tile=tile: e.activation(sq[:], tile[:], AF.Square))
            for hh in range(2):
                hs = hh * 64
                for st in range(4):
                    def nmm3(eng, hs=hs, st=st):
                        return eng.matmul(ps[5][0:1, 0:512],
                                          onescol[hs:hs + 64, 0:1],
                                          sq[hs:hs + 64, st * 512:(st + 1) * 512],
                                          start=True, stop=True)
                    ch.step("tensor", nmm3)
                    ch.step("vector", lambda e, st=st: e.tensor_copy(
                        qrow[:1, st * 512:(st + 1) * 512], ps[5][0:1, 0:512]))
                ch.step("vector", lambda e, qk=qk, hh=hh: e.reduce_max(
                    scol[:1, qk + hh: qk + hh + 1], qrow[:1, :], axis=AX.X))

        for hh in range(2):
            hs = hh * 64
            h_abs = pr * 2 + hh
            ch.step("vector", lambda e, hh=hh: e.tensor_mul(
                scol[:1, 4:5], scol[:1, hh:hh + 1], scol[:1, 2 + hh:3 + hh]))
            ch.step("scalar", lambda e: e.activation(scol[:1, 5:6], scol[:1, 4:5], AF.Ln))
            ch.step("scalar", lambda e: e.activation(scol[:1, 6:7], scol[:1, 5:6],
                                                     AF.Exp, scale=0.5))
            ch.step("vector", lambda e: e.tensor_scalar(scol[:1, 7:8], scol[:1, 6:7],
                                                        -0.125, None, ALU.mult))
            ch.step("tensor", lambda e: e.matmul(ps[5][:, 0:1], cst["ones1x512f"][:1, 0:128],
                                                 scol[:1, 7:8], start=True, stop=True))
            ch.step("vector", lambda e: e.tensor_copy(negb[:], ps[5][:, 0:1]))
            negb_tok = ch.prev

            sbank_guard = {0: ch.prev, 1: ch.prev}
            av_done = None
            pt_guard = [None, None]
            for st in range(4):
                nmi = st * 4 + 4
                o_evac_toks = []
                for mi in range(nmi):
                    sb_i = mi % 2
                    spb = ps[0] if sb_i == 0 else ps[5]
                    def smm(eng, hs=hs, mi=mi, st=st, spb=spb):
                        return eng.matmul(spb[:],
                                          kf[hs:hs + 64, mi * 128:(mi + 1) * 128],
                                          qf[hs:hs + 64, st * 512:(st + 1) * 512],
                                          start=True, stop=True)
                    tk = p.emit("tensor", smm, waits=[sbank_guard[sb_i]],
                                incs=[("pe", 1)])
                    smm_tok = ("pe", tk["pe"])
                    ewaits = [smm_tok]
                    if pt_guard[sb_i] is not None:
                        ewaits.append(pt_guard[sb_i])
                    tk = p.emit("scalar",
                                lambda e, sb_i=sb_i, spb=spb: e.activation(
                                    pt2[sb_i][:], spb[:], AF.Exp,
                                    scale=0.125, bias=negb[:]),
                                waits=ewaits, incs=[("ac", 1)])
                    exp_tok = ("ac", tk["ac"])
                    sbank_guard[sb_i] = exp_tok
                    o = mi - st * 4
                    ready_tok = exp_tok
                    if o >= 0:
                        tk = p.emit("vector",
                                    lambda e, o=o, sb_i=sb_i: e.tensor_mul(
                                        pt2[sb_i][:], pt2[sb_i][:],
                                        cmask[:, o * 512:(o + 1) * 512]),
                                    waits=[exp_tok], incs=[("dv", 1)])
                        ready_tok = ("dv", tk["dv"])
                    def avmm(eng, hh=hh, mi=mi, nmi=nmi, sb_i=sb_i):
                        last = None
                        for sub in range(4):
                            last = eng.matmul(
                                ps[1 + sub][:, 0:65],
                                pt2[sb_i][:, sub * 128:(sub + 1) * 128],
                                vt[:].rearrange("p (m c) -> p m c", m=16)[:, mi,
                                    hh * 65:(hh + 1) * 65],
                                start=(mi == 0), stop=(mi == nmi - 1))
                        return last
                    avw = [ready_tok]
                    if mi == 0 and av_done is not None:
                        avw.append(av_done)
                    elif mi == 0:
                        avw.append(negb_tok)
                    tk = p.emit("tensor", avmm, waits=avw, incs=[("pe", 1)])
                    pt_guard[sb_i] = ("pe", tk["pe"])
                    last_av = ("pe", tk["pe"])
                for sub in range(4):
                    ti = st * 4 + sub
                    tk = p.emit("vector",
                                lambda e, sub=sub: e.reciprocal(
                                    rcol[:], ps[1 + sub][:, 64:65]),
                                waits=[last_av] + ([o_evac_toks[-1]] if o_evac_toks else []),
                                incs=[("dv", 1)])
                    tk = p.emit("vector",
                                lambda e, sub=sub, ti=ti, h_abs=h_abs: e.tensor_scalar(
                                    oall[:, ti * 1024 + h_abs * 64: ti * 1024 + (h_abs + 1) * 64],
                                    ps[1 + sub][:, 0:64], rcol[:], None, ALU.mult),
                                waits=[("dv", tk["dv"])] + ([oall_guard] if oall_guard else []),
                                incs=[("dv", 1)])
                    o_evac_toks.append(("dv", tk["dv"]))
                av_done = o_evac_toks[-1]
            ch.prev = av_done
        qkv_bank_guard = [ch.prev, ch.prev]

    for ti in range(NT):
        def trall(eng, ti=ti):
            last = None
            for c in range(ND):
                last = eng.transpose(
                    ps["trb"][:, c * 128:(c + 1) * 128],
                    oall[:, ti * 1024 + c * 128: ti * 1024 + (c + 1) * 128],
                    cst["ident_b"][:])
            return last
        ch.step("tensor", trall)
        ch.step("vector", lambda e: e.tensor_copy(ofm[:], ps["trb"][:]))
        for half in range(2):
            def pmm(eng, half=half):
                last = None
                for c in range(ND):
                    last = eng.matmul(ps[half][:],
                                      ofm[:, c * 128:(c + 1) * 128],
                                      woTb[:, c * 1024 + half * 512: c * 1024 + (half + 1) * 512],
                                      start=(c == 0), stop=(c == ND - 1))
                return last
            ch.step("tensor", pmm)
            ch.step("vector", lambda e, ti=ti, half=half: e.tensor_add(
                zres[:, half * 512:(half + 1) * 512], ps[half][:],
                XT[:, ti * 1024 + half * 512: ti * 1024 + (half + 1) * 512]))
        emit_ln(ch, env, lnt, zres[:], XT[:, ti * 1024:(ti + 1) * 1024])
    return ch


def build_stage_c_program():
    nc = bass.Bass("TRN2", target_bir_lowering=False)
    env = {}
    d_x = nc.dram_tensor("x", [T, D], F32, kind="ExternalInput").ap()
    d_xT = nc.dram_tensor("xT", [D, T], F32, kind="ExternalInput").ap()
    env["d_wq1"] = nc.dram_tensor("wq1", [8, 128, 1024], F32, kind="ExternalInput").ap()
    env["d_wk1"] = nc.dram_tensor("wk1", [8, 128, 1024], F32, kind="ExternalInput").ap()
    env["d_wv1"] = nc.dram_tensor("wv1", [8, 128, 1024], F32, kind="ExternalInput").ap()
    env["d_wo1T"] = nc.dram_tensor("wo1T", [D, D], F32, kind="ExternalInput").ap()
    env["d_cmask"] = nc.dram_tensor("cmaskc", [4, 128, 512], BF16, kind="ExternalInput").ap()
    env["d_onescol"] = nc.dram_tensor("onescolc", [128, 1], BF16, kind="ExternalInput").ap()
    dcst = declare_consts(nc)
    out = nc.dram_tensor("xc", [T, D], F32, kind="ExternalOutput").ap()

    with ExitStack() as ctx:
        XT = ctx.enter_context(nc.sbuf_tensor("XT", [128, NT * 1024], F32))
        XFMB = ctx.enter_context(nc.sbuf_tensor("XFMB", [128, ND * 2048], BF16))
        cst = {nm: ctx.enter_context(nc.sbuf_tensor("s_" + nm, *CST_SHAPES[nm]))
               for nm in CST_ALL}
        ps = alloc_psum(ctx, nc)
        env.update(XT=XT, XFMB=XFMB, cst=cst, ps=ps)

        pl = Prog(nc)
        p = Prog(nc)
        p._sem("w2")
        pl.sems["w2"] = p.sems["w2"]
        pl.cnt["w2"] = 0

        def xloads(eng):
            for ci in range(NT):
                eng.dma_start(XT[:, ci * 1024:(ci + 1) * 1024],
                              d_x[ci * 128:(ci + 1) * 128, :]).then_inc(p.sems["w2"], 16)
            for nm in CST_ALL:
                eng.dma_start(cst[nm][:], dcst[nm]).then_inc(p.sems["w2"], 16)
            return None
        pl.emit("sync", xloads)

        def xloads2(eng):
            for dc in range(ND):
                eng.dma_start(XFMB[:, dc * 2048:(dc + 1) * 2048],
                              d_xT[dc * 128:(dc + 1) * 128, :]).then_inc(p.sems["w2"], 16)
            return None
        pl.emit("gpsimd", xloads2)
        nw2 = (NT + len(CST_ALL) + ND) * 16
        p.cnt["w2"] += nw2

        def w2wait(eng):
            eng.wait_ge(p.sems["w2"], nw2)
            return None
        pl.emit("gpsimd", w2wait)
        env["xload_tok"] = None

        ch = stage_c(p, ctx, env)

        def stores(eng):
            for ci in range(NT):
                eng.dma_start(out[ci * 128:(ci + 1) * 128, :],
                              XT[:, ci * 1024:(ci + 1) * 1024]).then_inc(p.sems["st"], 16)
            eng.wait_ge(p.sems["st"], NT * 16)
            return None
        p._sem("st")
        p.emit("sync", stores, waits=[ch.prev])
        p.cnt["st"] += NT * 16

        pl.play()
        p.play()
    return nc


def make_stage_c_inmaps(inputs, xs):
    com, _ = prep_host(inputs)
    maps = []
    for b in range(8):
        m = {"x": np.ascontiguousarray(xs[b]), "xT": np.ascontiguousarray(xs[b].T),
             "wq1": com["wq1"], "wk1": com["wk1"], "wv1": com["wv1"],
             "wo1T": com["wo1T"], "cmaskc": com["cmask"], "onescolc": com["onescol_b"]}
        for nm in CST_ALL:
            m["c_" + nm] = com[nm]
        maps.append(m)
    return maps


def stage_d(p, ctx, env, final_store=None):
    nc = p.nc
    ps = env["ps"]
    XT, XFMB = env["XT"], env["XFMB"]
    cst = env["cst"]
    sb = lambda name, shape, dt: ctx.enter_context(nc.sbuf_tensor(name, shape, dt))

    rw = sb("d_rw", [128, ND * 8], F32)
    rb = sb("d_rb", [1, 8], F32)
    wgb = [sb("d_wgb%d" % i, [128, 1024], BF16) for i in range(2)]
    wub = [sb("d_wub%d" % i, [128, 1024], BF16) for i in range(2)]
    wdh = [sb("d_wdh%d" % i, [128, NF1 * 256], BF16) for i in range(2)]
    hall = sb("d_hall", [128, NF1 * 512], BF16)
    sg = sb("d_sg", [128, 512], BF16)
    acc = sb("d_acc", [128, NT * 1024], BF16)
    xtr = sb("d_xtr", [128, 1024], F32)
    lgt = sb("d_lgt", [128, 8], F32)
    l2t = sb("d_l2t", [128, 8], F32)
    msk1 = sb("d_msk1", [128, 8], F32)
    msk2 = sb("d_msk2", [128, 8], F32)
    t8 = sb("d_t8", [128, 8], F32)
    ce = sb("d_ce", [128, NT * 8], F32)
    m1c = sb("d_m1c", [128, 1], F32)
    m2c = sb("d_m2c", [128, 1], F32)
    ec = sb("d_ec", [128, 1], F32)
    w1c = sb("d_w1c", [128, 1], F32)
    w2c = sb("d_w2c", [128, 1], F32)
    zres = sb("d_zres", [128, 1024], F32)
    lnt = alloc_lnt(ctx, nc, "d", junk=xtr[:])

    ch = Chain(p)
    ch.prev = env.get("start_tok")

    p._sem("w")
    def loads(eng):
        eng.dma_start(rw[:], env["d_rw1"]).then_inc(p.sems["w"], 16)
        eng.dma_start(rb[:], env["d_rb1"]).then_inc(p.sems["w"], 16)
        return None
    p.emit("sync", loads, waits=[env["xload_tok"]] if env.get("xload_tok") else [])
    p.cnt["w"] += 32
    wtok0 = ("w", p.val("w"))
    if ch.prev is None:
        ch.prev = wtok0

    def lgt_cb(ci):
        ch.step("vector", lambda e: e.tensor_copy(lgt[:], ps[5][:, 0:8]),
                extra_waits=[wtok0] if ci == 0 else [])
        ch.step("vector", lambda e: e.reduce_max(m1c[:], lgt[:], axis=AX.X))
        ch.step("vector", lambda e: e.tensor_scalar(msk1[:], lgt[:], m1c[:], None,
                                                    ALU.is_equal))
        ch.step("vector", lambda e: e.scalar_tensor_tensor(l2t[:], msk1[:], -1e30,
                                                           lgt[:], ALU.mult, ALU.add))
        ch.step("vector", lambda e: e.reduce_max(m2c[:], l2t[:], axis=AX.X))
        ch.step("vector", lambda e: e.tensor_scalar(msk2[:], l2t[:], m2c[:], None,
                                                    ALU.is_equal))
        ch.step("vector", lambda e: e.tensor_sub(ec[:], m2c[:], m1c[:]))
        ch.step("scalar", lambda e: e.activation(ec[:], ec[:], AF.Exp))
        ch.step("vector", lambda e: e.tensor_scalar_add(ec[:], ec[:], 1.0))
        ch.step("vector", lambda e: e.reciprocal(w1c[:], ec[:]))
        ch.step("vector", lambda e: e.tensor_scalar(w2c[:], w1c[:], -1.0, 1.0,
                                                    ALU.mult, ALU.add))
        ch.step("vector", lambda e: e.tensor_scalar(t8[:], msk1[:], w1c[:], None, ALU.mult))
        ch.step("vector", lambda e, ci=ci: e.scalar_tensor_tensor(
            ce[:, ci * 8:(ci + 1) * 8], msk2[:], w2c[:], t8[:], ALU.mult, ALU.add))
    emit_router(p, ch, env, {"xtr": xtr}, 8, rw, rb, lgt_cb)

    p._sem("pl"); p._sem("pe"); p._sem("ac"); p._sem("dv")
    wtok = {}
    sg2 = [sg, ctx.enter_context(nc.sbuf_tensor("d_sg2", [128, 512], BF16))]
    tmp2 = sg2

    def emit_gload(key, e_i, fc, wait_tok):
        par = fc % 2
        def wload(eng, e_i=e_i, fc=fc, par=par):
            eng.dma_start(wgb[par][:], env["d_wg1"][e_i, fc]).then_inc(p.sems["pl"], 16)
            eng.dma_start(wub[par][:], env["d_wu1"][e_i, fc]).then_inc(p.sems["pl"], 16)
            return None
        p.emit("gpsimd", wload, waits=[wait_tok] if wait_tok else [])
        p.cnt["pl"] += 32
        wtok[key] = ("pl", p.val("pl"))

    def emit_dload(key, e_i, qt, par, wait_tok):
        def wload(eng, e_i=e_i, qt=qt, par=par):
            for fc in range(NF1):
                eng.dma_start(wdh[par][:, fc * 256:(fc + 1) * 256],
                              env["d_wd1"][e_i, fc, :, qt * 256:(qt + 1) * 256]
                              ).then_inc(p.sems["pl"], 16)
            return None
        p.emit("gpsimd", wload, waits=[wait_tok] if wait_tok else [])
        p.cnt["pl"] += NF1 * 16
        wtok[key] = ("pl", p.val("pl"))

    start_tok = ch.prev
    stage_start = env.get("start_tok")
    tkm = p.emit("vector", lambda e: e.memset(acc[:], 0.0),
                 waits=[start_tok], incs=[("dv", 1)])
    acc_init = ("dv", tkm["dv"])
    acc_guard = {}
    for ti16 in range(16):
        for half in range(2):
            acc_guard[(ti16, half)] = acc_init

    bank_guard = [start_tok, start_tok]
    y_guard = [start_tok] * 4
    tmp_guard = [None, None]
    hall_guard = None
    dbuf_guard = [stage_start, stage_start]
    gbuf_guard = [stage_start, stage_start]

    for e_i in range(NE1):
        for ss in range(4):
            t0 = ss * 512
            emit_gload(("g", e_i, ss, 0), e_i, 0, gbuf_guard[0])
            emit_gload(("g", e_i, ss, 1), e_i, 1, gbuf_guard[1])
            emit_dload(("d", e_i, ss, 0), e_i, 0, 0, dbuf_guard[0])
            evac_tok = {}
            for fc in range(NF1):
                par = fc % 2
                psg, psu = (ps[0], ps[1]) if par == 0 else (ps[2], ps[3])
                def gumm(eng, par=par, psg=psg, psu=psu, t0=t0):
                    for dc in range(ND):
                        eng.matmul(psg[:], wgb[par][:, dc * 128:(dc + 1) * 128],
                                   XFMB[:, dc * 2048 + t0: dc * 2048 + t0 + 512],
                                   start=(dc == 0), stop=(dc == ND - 1))
                    last = None
                    for dc in range(ND):
                        last = eng.matmul(psu[:], wub[par][:, dc * 128:(dc + 1) * 128],
                                          XFMB[:, dc * 2048 + t0: dc * 2048 + t0 + 512],
                                          start=(dc == 0), stop=(dc == ND - 1))
                    return last
                waits = [wtok[("g", e_i, ss, fc)], bank_guard[par]]
                tk = p.emit("tensor", gumm, waits=waits, incs=[("pe", 1)])
                mm_tok = ("pe", tk["pe"])
                gbuf_guard[par] = mm_tok
                swaits = [mm_tok]
                if tmp_guard[par] is not None:
                    swaits.append(tmp_guard[par])
                tk = p.emit("scalar",
                            lambda e, psg=psg, par=par: e.activation(
                                sg2[par][:], psg[:], AF.Silu),
                            waits=swaits, incs=[("ac", 1)])
                silu_tok = ("ac", tk["ac"])
                mwaits = [silu_tok]
                if hall_guard is not None:
                    mwaits.append(hall_guard)
                tk = p.emit("vector",
                            lambda e, fc=fc, psu=psu, par=par: e.tensor_mul(
                                hall[:, fc * 512:(fc + 1) * 512], sg2[par][:], psu[:]),
                            waits=mwaits, incs=[("dv", 1)])
                evac_tok[fc] = ("dv", tk["dv"])
                bank_guard[par] = evac_tok[fc]
                if fc + 2 < NF1:
                    emit_gload(("g", e_i, ss, fc + 2), e_i, fc + 2, gbuf_guard[par])
            h_done = evac_tok[NF1 - 1]
            for qt in range(4):
                dpar = qt % 2
                if qt + 1 < 4:
                    emit_dload(("d", e_i, ss, qt + 1), e_i, qt + 1, (qt + 1) % 2,
                               dbuf_guard[(qt + 1) % 2])
                last_dmm = None
                for fc in range(NF1):
                    def dmm(eng, fc=fc, dpar=dpar):
                        last = None
                        for sub in range(4):
                            last = eng.matmul(
                                ps[2 + sub][:, 0:256],
                                hall[:, fc * 512 + sub * 128: fc * 512 + (sub + 1) * 128],
                                wdh[dpar][:, fc * 256:(fc + 1) * 256],
                                start=(fc == 0), stop=(fc == NF1 - 1))
                        return last
                    waits = []
                    if fc == 0:
                        waits = [wtok[("d", e_i, ss, qt)], h_done]
                        waits += [y_guard[j] for j in range(4)]
                    tk = p.emit("tensor", dmm, waits=waits, incs=[("pe", 1)])
                    last_dmm = ("pe", tk["pe"])
                dbuf_guard[dpar] = last_dmm
                hall_guard = last_dmm
                for sub in range(4):
                    ti = ss * 4 + sub
                    tpar = sub % 2
                    awaits = [last_dmm]
                    if tmp_guard[tpar] is not None:
                        awaits.append(tmp_guard[tpar])
                    tk = p.emit("scalar",
                                lambda e, ti=ti, e_i=e_i, sub=sub, tpar=tpar: e.activation(
                                    tmp2[tpar][:, 0:256], ps[2 + sub][:, 0:256], AF.Copy,
                                    scale=ce[:, ti * 8 + e_i: ti * 8 + e_i + 1]),
                                waits=awaits, incs=[("ac", 1)])
                    ac_tok = ("ac", tk["ac"])
                    y_guard[sub] = ac_tok
                    tk = p.emit("vector",
                                lambda e, ti=ti, qt=qt, tpar=tpar: e.tensor_add(
                                    acc[:, ti * 1024 + qt * 256: ti * 1024 + (qt + 1) * 256],
                                    acc[:, ti * 1024 + qt * 256: ti * 1024 + (qt + 1) * 256],
                                    tmp2[tpar][:, 0:256]),
                                waits=[ac_tok, acc_guard[(ti, qt % 2)]], incs=[("dv", 1)])
                    acc_guard[(ti, qt % 2)] = ("dv", tk["dv"])
                    tmp_guard[tpar] = acc_guard[(ti, qt % 2)]
            bank_guard[1] = ("ac", p.val("ac"))

    ch.prev = ("dv", p.val("dv"))
    for ti in range(NT):
        ch.step("vector", lambda e, ti=ti: e.tensor_add(
            zres[:], acc[:, ti * 1024:(ti + 1) * 1024],
            XT[:, ti * 1024:(ti + 1) * 1024]))
        emit_ln(ch, env, lnt, zres[:], XT[:, ti * 1024:(ti + 1) * 1024])
        if final_store is not None:
            final_store(ch, ti)
    return ch


def build_stage_d_program():
    nc = bass.Bass("TRN2", target_bir_lowering=False)
    env = {}
    d_x = nc.dram_tensor("x", [T, D], F32, kind="ExternalInput").ap()
    d_xT = nc.dram_tensor("xT", [D, T], F32, kind="ExternalInput").ap()
    env["d_rw1"] = nc.dram_tensor("rw1", [128, ND * 8], F32, kind="ExternalInput").ap()
    env["d_rb1"] = nc.dram_tensor("rb1", [1, 8], F32, kind="ExternalInput").ap()
    env["d_wg1"] = nc.dram_tensor("wg1", [NE1, NF1, 128, 1024], F32, kind="ExternalInput").ap()
    env["d_wu1"] = nc.dram_tensor("wu1", [NE1, NF1, 128, 1024], F32, kind="ExternalInput").ap()
    env["d_wd1"] = nc.dram_tensor("wd1", [NE1, NF1, 128, 1024], F32, kind="ExternalInput").ap()
    dcst = declare_consts(nc)
    out = nc.dram_tensor("xd", [T, D], F32, kind="ExternalOutput").ap()

    with ExitStack() as ctx:
        XT = ctx.enter_context(nc.sbuf_tensor("XT", [128, NT * 1024], F32))
        XFMB = ctx.enter_context(nc.sbuf_tensor("XFMB", [128, ND * 2048], BF16))
        cst = {nm: ctx.enter_context(nc.sbuf_tensor("s_" + nm, *CST_SHAPES[nm]))
               for nm in CST_ALL}
        ps = alloc_psum(ctx, nc)
        env.update(XT=XT, XFMB=XFMB, cst=cst, ps=ps)

        pl = Prog(nc)
        p = Prog(nc)
        p._sem("w2")
        pl.sems["w2"] = p.sems["w2"]
        pl.cnt["w2"] = 0

        def xloads(eng):
            for ci in range(NT):
                eng.dma_start(XT[:, ci * 1024:(ci + 1) * 1024],
                              d_x[ci * 128:(ci + 1) * 128, :]).then_inc(p.sems["w2"], 16)
            for nm in CST_ALL:
                eng.dma_start(cst[nm][:], dcst[nm]).then_inc(p.sems["w2"], 16)
            return None
        pl.emit("sync", xloads)

        def xloads2(eng):
            for dc in range(ND):
                eng.dma_start(XFMB[:, dc * 2048:(dc + 1) * 2048],
                              d_xT[dc * 128:(dc + 1) * 128, :]).then_inc(p.sems["w2"], 16)
            return None
        pl.emit("gpsimd", xloads2)
        nw2 = (NT + len(CST_ALL) + ND) * 16
        p.cnt["w2"] += nw2

        def w2wait(eng):
            eng.wait_ge(p.sems["w2"], nw2)
            return None
        pl.emit("gpsimd", w2wait)
        env["xload_tok"] = None

        p._sem("st")
        def mk_store(ch, ci):
            def st_fn(eng, ci=ci):
                return eng.dma_start(out[ci * 128:(ci + 1) * 128, :],
                                     XT[:, ci * 1024:(ci + 1) * 1024])
            tk = p.emit("sync", st_fn, waits=[ch.prev], incs=[("st", 16)])
        ch = stage_d(p, ctx, env, final_store=mk_store)

        def stwait(eng):
            eng.wait_ge(p.sems["st"], NT * 16)
            return None
        p.emit("sync", stwait, waits=[ch.prev])

        pl.play()
        p.play()
    return nc


def make_stage_d_inmaps(inputs, xs):
    com, _ = prep_host(inputs)
    maps = []
    for b in range(8):
        m = {"x": np.ascontiguousarray(xs[b]), "xT": np.ascontiguousarray(xs[b].T),
             "rw1": com["rw1"], "rb1": com["rb1"],
             "wg1": com["wg1"], "wu1": com["wu1"], "wd1": com["wd1"]}
        for nm in CST_ALL:
            m["c_" + nm] = com[nm]
        maps.append(m)
    return maps


DBG_TAPS = {
    "qtm": ([128, 128], "f"), "kt": ([128, 128], "f"), "vt": ([128, 128], "f"),
    "gtm": ([128, 128], "f"), "lar": ([128, 2], "f"), "beta": ([128, 1], "f"),
    "lcolr": ([128, 1], "f"), "ccol": ([128, 1], "f"),
    "ELl": ([128, 128], "f"), "ELt": ([128, 128], "f"), "Ast": ([128, 128], "f"),
    "Zt": ([128, 128], "f"), "Bst": ([128, 128], "f"), "Wtm": ([128, 128], "f"),
    "Pt": ([128, 128], "f"), "Otm": ([128, 128], "f"), "Sst": ([128, 128], "f"),
    "Rst": ([128, 128], "f"), "zres": ([128, 1024], "f"),
    "ps0q": ([128, 128], "f"), "xfm0": ([128, 128], "f"), "wq0": ([128, 128], "f"),
    "colsdump": ([128, 24], "f"),
}


def build_stage_a_program(debug=False, truncate=None):
    nc = bass.Bass("TRN2", target_bir_lowering=False)
    env = {}
    d = {}
    d["x"] = nc.dram_tensor("x", [T, D], F32, kind="ExternalInput").ap()
    d["xT"] = nc.dram_tensor("xT", [D, T], F32R, kind="ExternalInput").ap()
    env["d_wqkvg"] = nc.dram_tensor("wqkvg", [D, 512], F32R, kind="ExternalInput").ap()
    env["d_wab"] = nc.dram_tensor("wab", [D, 2], F32R, kind="ExternalInput").ap()
    env["d_wo0"] = nc.dram_tensor("wo0", [C0, D], F32, kind="ExternalInput").ap()
    env["d_bqkvg"] = nc.dram_tensor("bqkvg", [1, 512], F32R, kind="ExternalInput").ap()
    env["d_bab"] = nc.dram_tensor("bab", [1, 2], F32R, kind="ExternalInput").ap()
    env["d_bo0"] = nc.dram_tensor("bo0", [1, D], F32, kind="ExternalInput").ap()
    cst_names = [("ident", [128, 128], F32), ("ident_b", [128, 128], BF16),
                 ("twoI", [128, 128], F32), ("trilcum", [128, 128], F32),
                 ("smask", [128, 128], F32), ("umask", [128, 128], F32),
                 ("ones1x512", [1, 128], F32R), ("ones1x512f", [1, 128], F32),
                 ("ones1b", [1, 128], BF16), ("eyeI", [128, 128], F32)]
    dcst = {nm: nc.dram_tensor("c_" + nm, shape, dt, kind="ExternalInput").ap()
            for nm, shape, dt in cst_names}
    out = nc.dram_tensor("xa", [T, D], F32, kind="ExternalOutput").ap()

    with ExitStack() as ctx:
        XT = ctx.enter_context(nc.sbuf_tensor("XT", [128, NT * 1024], F32))
        XFMR = ctx.enter_context(nc.sbuf_tensor("XFMR", [128, ND * 2048], F32R))
        cst = {nm: ctx.enter_context(nc.sbuf_tensor("s_" + nm, shape, dt))
               for nm, shape, dt in cst_names}
        ps = alloc_psum(ctx, nc)
        env.update(XT=XT, XFMR=XFMR, cst=cst, ps=ps)

        pl = Prog(nc)
        p = Prog(nc)
        def xloads(eng):
            last = None
            for ci in range(NT):
                last = eng.dma_start(XT[:, ci * 1024:(ci + 1) * 1024],
                                     d["x"][ci * 128:(ci + 1) * 128, :])
                last.then_inc(p.sems["w2"], 16)
            for dc in range(ND):
                last = eng.dma_start(XFMR[:, dc * 2048:(dc + 1) * 2048],
                                     d["xT"][dc * 128:(dc + 1) * 128, :])
                last.then_inc(p.sems["w2"], 16)
            for nm, _s, _d in cst_names:
                last = eng.dma_start(cst[nm][:], dcst[nm])
                last.then_inc(p.sems["w2"], 16)
            return None
        p._sem("w2")
        pl.sems["w2"] = p.sems["w2"]
        pl.cnt["w2"] = 0
        pl.emit("gpsimd", xloads)
        nw2 = (NT + ND + len(cst_names)) * 16
        p.cnt["w2"] += nw2
        def w2wait(eng):
            eng.wait_ge(pl.sems["w2"], nw2)
            return None
        pl.emit("gpsimd", w2wait)
        env["xload_tok"] = ("w2", p.val("w2"))
        env["pload"] = pl

        env["truncate"] = truncate
        dbg = None
        if debug:
            dbg = {nm: nc.dram_tensor("dbg_" + nm, shape,
                                      F32R if k == "r" else F32,
                                      kind="ExternalOutput").ap()
                   for nm, (shape, k) in DBG_TAPS.items()}
        ch = stage_a(p, ctx, env, dbg=dbg)

        def stores(eng):
            last = None
            for ci in range(NT):
                last = eng.dma_start(out[ci * 128:(ci + 1) * 128, :],
                                     XT[:, ci * 1024:(ci + 1) * 1024])
                last.then_inc(p.sems["st"], 16)
            eng.wait_ge(p.sems["st"], NT * 16)
            return None
        p._sem("st")
        p.emit("sync", stores, waits=[ch.prev])
        p.cnt["st"] += NT * 16

        pl.play()
        p.play()
    return nc


def make_stage_a_inmaps(inputs):
    com, percore = prep_host(inputs)
    maps = []
    for b in range(8):
        m = {"x": percore[b]["x"], "xT": percore[b]["xT"],
             "wqkvg": com["wqkvg"], "wab": com["wab"], "wo0": com["wo0"],
             "bqkvg": com["bqkvg"], "bab": com["bab"], "bo0": com["bo0"]}
        for nm in ("ident", "ident_b", "twoI", "trilcum", "smask", "umask",
                   "ones1x512", "ones1x512f", "ones1b", "eyeI"):
            m["c_" + nm] = com[nm]
        maps.append(m)
    return maps


CST_ALL = ("ident", "ident_b", "twoI", "trilcum", "smask", "umask",
           "ones1x512", "ones1x512f", "ones1b", "eyeI")
CST_SHAPES = {"ident": ([128, 128], F32), "ident_b": ([128, 128], BF16),
              "twoI": ([128, 128], F32), "trilcum": ([128, 128], F32),
              "smask": ([128, 128], F32), "umask": ([128, 128], F32),
              "ones1x512": ([1, 128], F32R), "ones1x512f": ([1, 128], F32),
              "ones1b": ([1, 128], BF16), "eyeI": ([128, 128], F32)}


def declare_consts(nc):
    return {nm: nc.dram_tensor("c_" + nm, CST_SHAPES[nm][0], CST_SHAPES[nm][1],
                               kind="ExternalInput").ap() for nm in CST_ALL}


def build_stage_b_program(debug=False):
    nc = bass.Bass("TRN2", target_bir_lowering=False)
    env = {}
    d_x = nc.dram_tensor("x", [T, D], F32, kind="ExternalInput").ap()
    d_xT = nc.dram_tensor("xT", [D, T], F32, kind="ExternalInput").ap()
    env["d_rwT"] = nc.dram_tensor("rwT", [128, ND * 32], F32, kind="ExternalInput").ap()
    env["d_rb"] = nc.dram_tensor("rb", [1, 32], F32, kind="ExternalInput").ap()
    env["d_b1c"] = nc.dram_tensor("b1c", [128, 64], F32, kind="ExternalInput").ap()
    env["d_w1"] = nc.dram_tensor("w1", [NE0, 128, 8 * H0], F32, kind="ExternalInput").ap()
    env["d_w2"] = nc.dram_tensor("w2", [NE0, 128, 2 * D], F32, kind="ExternalInput").ap()
    env["d_b2"] = nc.dram_tensor("b2", [NE0, D], F32, kind="ExternalInput").ap()
    dcst = declare_consts(nc)
    out = nc.dram_tensor("xb", [T, D], F32, kind="ExternalOutput").ap()
    dbgb = None
    if debug:
        dbgb = {"msk": nc.dram_tensor("dbg_msk", [128, NT * 32], F32, kind="ExternalOutput").ap(),
                "lgt0": nc.dram_tensor("dbg_lgt0", [128, 32], F32, kind="ExternalOutput").ap(),
                "hfm0": nc.dram_tensor("dbg_hfm0", [128, 1024], BF16, kind="ExternalOutput").ap(),
                "xfmb0": nc.dram_tensor("dbg_xfmb0", [128, 512], BF16, kind="ExternalOutput").ap(),
                "w1b0": nc.dram_tensor("dbg_w1b0", [128, 2048], BF16, kind="ExternalOutput").ap(),
                "acc0": nc.dram_tensor("dbg_acc0", [128, 4096], F32, kind="ExternalOutput").ap()}

    with ExitStack() as ctx:
        XT = ctx.enter_context(nc.sbuf_tensor("XT", [128, NT * 1024], F32))
        XFMB = ctx.enter_context(nc.sbuf_tensor("XFMB", [128, ND * 2048], BF16))
        cst = {nm: ctx.enter_context(nc.sbuf_tensor("s_" + nm, *CST_SHAPES[nm]))
               for nm in CST_ALL}
        ps = alloc_psum(ctx, nc)
        env.update(XT=XT, XFMB=XFMB, cst=cst, ps=ps, dbgb=dbgb)

        pl = Prog(nc)
        p = Prog(nc)
        p._sem("w2")
        pl.sems["w2"] = p.sems["w2"]
        pl.cnt["w2"] = 0

        def xloads(eng):
            for ci in range(NT):
                eng.dma_start(XT[:, ci * 1024:(ci + 1) * 1024],
                              d_x[ci * 128:(ci + 1) * 128, :]).then_inc(p.sems["w2"], 16)
            for nm in CST_ALL:
                eng.dma_start(cst[nm][:], dcst[nm]).then_inc(p.sems["w2"], 16)
            return None
        pl.emit("sync", xloads)

        def xloads2(eng):
            for dc in range(ND):
                eng.dma_start(XFMB[:, dc * 2048:(dc + 1) * 2048],
                              d_xT[dc * 128:(dc + 1) * 128, :]).then_inc(p.sems["w2"], 16)
            return None
        pl.emit("gpsimd", xloads2)
        nw2 = (NT + len(CST_ALL) + ND) * 16
        p.cnt["w2"] += nw2

        def w2wait(eng):
            eng.wait_ge(p.sems["w2"], nw2)
            return None
        pl.emit("gpsimd", w2wait)
        env["xload_tok"] = None

        ch = stage_b(p, ctx, env)

        def stores(eng):
            for ci in range(NT):
                eng.dma_start(out[ci * 128:(ci + 1) * 128, :],
                              XT[:, ci * 1024:(ci + 1) * 1024]).then_inc(p.sems["st"], 16)
            eng.wait_ge(p.sems["st"], NT * 16)
            return None
        p._sem("st")
        p.emit("sync", stores, waits=[ch.prev])
        p.cnt["st"] += NT * 16

        pl.play()
        p.play()
    return nc


def make_stage_b_inmaps(inputs, xs):
    com, _ = prep_host(inputs)
    maps = []
    for b in range(8):
        m = {"x": np.ascontiguousarray(xs[b]), "xT": np.ascontiguousarray(xs[b].T),
             "rwT": com["rwT"], "rb": com["rb"], "b1c": com["b1c"],
             "w1": com["w1"], "w2": com["w2"], "b2": com["b2"]}
        for nm in CST_ALL:
            m["c_" + nm] = com[nm]
        maps.append(m)
    return maps


def emit_refresh_xfmb(p, env, xtmb, prev_tok):
    ps, XT, XFMB, cst = env["ps"], env["XT"], env["XFMB"], env["cst"]
    ch = Chain(p)
    ch.prev = prev_tok
    for ci in range(NT):
        ch.step("vector", lambda e, ci=ci: e.tensor_copy(
            xtmb[:], XT[:, ci * 1024:(ci + 1) * 1024]))
        for dc in range(ND):
            ch.step("tensor", lambda e, dc=dc: e.transpose(
                ps["trb"][:, 0:128], xtmb[:, dc * 128:(dc + 1) * 128],
                cst["ident_b"][:]))
            ch.step("vector", lambda e, ci=ci, dc=dc: e.tensor_copy(
                XFMB[:, dc * 2048 + ci * 128: dc * 2048 + (ci + 1) * 128],
                ps["trb"][:, 0:128]))
    return ch.prev


def build_fused_program():
    nc = bass.Bass("TRN2", target_bir_lowering=False)
    env = {}
    d_x = nc.dram_tensor("x", [T, D], F32, kind="ExternalInput").ap()
    d_xT = nc.dram_tensor("xT", [D, T], F32R, kind="ExternalInput").ap()
    env["d_wqkvg"] = nc.dram_tensor("wqkvg", [D, 512], F32R, kind="ExternalInput").ap()
    env["d_wab"] = nc.dram_tensor("wab", [D, 2], F32R, kind="ExternalInput").ap()
    env["d_wo0"] = nc.dram_tensor("wo0", [C0, D], F32, kind="ExternalInput").ap()
    env["d_bqkvg"] = nc.dram_tensor("bqkvg", [1, 512], F32R, kind="ExternalInput").ap()
    env["d_bab"] = nc.dram_tensor("bab", [1, 2], F32R, kind="ExternalInput").ap()
    env["d_bo0"] = nc.dram_tensor("bo0", [1, D], F32, kind="ExternalInput").ap()
    env["d_rwT"] = nc.dram_tensor("rwT", [128, ND * 32], F32, kind="ExternalInput").ap()
    env["d_rb"] = nc.dram_tensor("rb", [1, 32], F32, kind="ExternalInput").ap()
    env["d_b1c"] = nc.dram_tensor("b1c", [128, 64], F32, kind="ExternalInput").ap()
    env["d_w1"] = nc.dram_tensor("w1", [NE0, 128, 8 * H0], F32, kind="ExternalInput").ap()
    env["d_w2"] = nc.dram_tensor("w2", [NE0, 128, 2 * D], F32, kind="ExternalInput").ap()
    env["d_b2"] = nc.dram_tensor("b2", [NE0, D], F32, kind="ExternalInput").ap()
    env["d_wq1"] = nc.dram_tensor("wq1", [8, 128, 1024], F32, kind="ExternalInput").ap()
    env["d_wk1"] = nc.dram_tensor("wk1", [8, 128, 1024], F32, kind="ExternalInput").ap()
    env["d_wv1"] = nc.dram_tensor("wv1", [8, 128, 1024], F32, kind="ExternalInput").ap()
    env["d_wo1T"] = nc.dram_tensor("wo1T", [D, D], F32, kind="ExternalInput").ap()
    env["d_cmask"] = nc.dram_tensor("cmaskc", [4, 128, 512], BF16, kind="ExternalInput").ap()
    env["d_onescol"] = nc.dram_tensor("onescolc", [128, 1], BF16, kind="ExternalInput").ap()
    env["d_rw1"] = nc.dram_tensor("rw1", [128, ND * 8], F32, kind="ExternalInput").ap()
    env["d_rb1"] = nc.dram_tensor("rb1", [1, 8], F32, kind="ExternalInput").ap()
    env["d_wg1"] = nc.dram_tensor("wg1", [NE1, NF1, 128, 1024], F32, kind="ExternalInput").ap()
    env["d_wu1"] = nc.dram_tensor("wu1", [NE1, NF1, 128, 1024], F32, kind="ExternalInput").ap()
    env["d_wd1"] = nc.dram_tensor("wd1", [NE1, NF1, 128, 1024], F32, kind="ExternalInput").ap()
    dcst = declare_consts(nc)
    out = nc.dram_tensor("out", [T, D], F32, kind="ExternalOutput").ap()

    with ExitStack() as ctx:
        XT = ctx.enter_context(nc.sbuf_tensor("XT", [128, NT * 1024], F32))
        xtmb = ctx.enter_context(nc.sbuf_tensor("xtmb", [128, 1024], BF16))
        cst = {nm: ctx.enter_context(nc.sbuf_tensor("s_" + nm, *CST_SHAPES[nm]))
               for nm in CST_ALL}
        ps = alloc_psum(ctx, nc)
        env.update(XT=XT, cst=cst, ps=ps)

        pl = Prog(nc)
        p = Prog(nc)
        p._sem("w2")
        pl.sems["w2"] = p.sems["w2"]
        pl.cnt["w2"] = 0

        with ExitStack() as ctx_a:
            XFMR = ctx_a.enter_context(nc.sbuf_tensor("XFMR", [128, ND * 2048], F32R))
            env["XFMR"] = XFMR

            def xloads(eng):
                for ci in range(NT):
                    eng.dma_start(XT[:, ci * 1024:(ci + 1) * 1024],
                                  d_x[ci * 128:(ci + 1) * 128, :]).then_inc(p.sems["w2"], 16)
                for dc in range(ND):
                    eng.dma_start(XFMR[:, dc * 2048:(dc + 1) * 2048],
                                  d_xT[dc * 128:(dc + 1) * 128, :]).then_inc(p.sems["w2"], 16)
                for nm in CST_ALL:
                    eng.dma_start(cst[nm][:], dcst[nm]).then_inc(p.sems["w2"], 16)
                return None
            pl.emit("gpsimd", xloads)
            nw2 = (NT + ND + len(CST_ALL)) * 16
            p.cnt["w2"] += nw2

            def w2wait(eng):
                eng.wait_ge(p.sems["w2"], nw2)
                return None
            pl.emit("gpsimd", w2wait)
            env["xload_tok"] = ("w2", p.val("w2"))
            env["pload"] = pl

            ch_a = stage_a(p, ctx_a, env)
            tok = ch_a.prev
        env.pop("pload", None)

        XFMB = ctx.enter_context(nc.sbuf_tensor("XFMB", [128, ND * 2048], BF16))
        env["XFMB"] = XFMB
        tok = emit_refresh_xfmb(p, env, xtmb, tok)
        with ExitStack() as ctx_b:
            env["start_tok"] = tok
            env["xload_tok"] = tok
            ch_b = stage_b(p, ctx_b, env)
            tok = ch_b.prev

        tok = emit_refresh_xfmb(p, env, xtmb, tok)
        with ExitStack() as ctx_c:
            env["start_tok"] = tok
            env["xload_tok"] = tok
            ch_c = stage_c(p, ctx_c, env)
            tok = ch_c.prev

        tok = emit_refresh_xfmb(p, env, xtmb, tok)
        with ExitStack() as ctx_d:
            env["start_tok"] = tok
            env["xload_tok"] = tok
            p._sem("st")

            def mk_store(ch, ci):
                def st_fn(eng, ci=ci):
                    return eng.dma_start(out[ci * 128:(ci + 1) * 128, :],
                                         XT[:, ci * 1024:(ci + 1) * 1024])
                p.emit("sync", st_fn, waits=[ch.prev], incs=[("st", 16)])
            ch_d = stage_d(p, ctx_d, env, final_store=mk_store)

            def stwait(eng):
                eng.wait_ge(p.sems["st"], NT * 16)
                return None
            p.emit("sync", stwait, waits=[ch_d.prev])

        pl.play()
        p.play()
    return nc


def make_fused_inmaps(inputs):
    com, percore = prep_host(inputs)
    names_a = ("wqkvg", "wab", "wo0", "bqkvg", "bab", "bo0")
    maps = []
    for b in range(8):
        m = {"x": percore[b]["x"], "xT": percore[b]["xT"]}
        for nm in names_a:
            m[nm] = com[nm]
        m.update(rwT=com["rwT"], rb=com["rb"], b1c=com["b1c"], w1=com["w1"],
                 w2=com["w2"], b2=com["b2"],
                 wq1=com["wq1"], wk1=com["wk1"], wv1=com["wv1"], wo1T=com["wo1T"],
                 cmaskc=com["cmask"], onescolc=com["onescol_b"],
                 rw1=com["rw1"], rb1=com["rb1"], wg1=com["wg1"], wu1=com["wu1"],
                 wd1=com["wd1"])
        for nm in CST_ALL:
            m["c_" + nm] = com[nm]
        maps.append(m)
    return maps


_PROGS = {}


def kernel_unfused(**inputs):
    if "A" not in _PROGS:
        _PROGS["A"] = build_stage_a_program()
        _PROGS["B"] = build_stage_b_program()
        _PROGS["C"] = build_stage_c_program()
        _PROGS["D"] = build_stage_d_program()
    cores = list(range(8))
    res = run_bass_kernel_spmd(_PROGS["A"], make_stage_a_inmaps(inputs), core_ids=cores)
    xa = np.stack([r["xa"] for r in res.results], axis=0)
    res = run_bass_kernel_spmd(_PROGS["B"], make_stage_b_inmaps(inputs, xa), core_ids=cores)
    xb = np.stack([r["xb"] for r in res.results], axis=0)
    res = run_bass_kernel_spmd(_PROGS["C"], make_stage_c_inmaps(inputs, xb), core_ids=cores)
    xc = np.stack([r["xc"] for r in res.results], axis=0)
    res = run_bass_kernel_spmd(_PROGS["D"], make_stage_d_inmaps(inputs, xc), core_ids=cores)
    return np.stack([r["xd"] for r in res.results], axis=0)


def kernel(**inputs):
    if "F" not in _PROGS:
        _PROGS["F"] = build_fused_program()
    res = run_bass_kernel_spmd(_PROGS["F"], make_fused_inmaps(inputs),
                               core_ids=list(range(8)))
    return np.stack([r["out"] for r in res.results], axis=0)


# version 10
# speedup vs baseline: 1.0933x; 1.0933x over previous
import numpy as np
import ml_dtypes
from contextlib import ExitStack

import concourse.bass as bass
import concourse.mybir as mybir
from concourse.bass_utils import run_bass_kernel_spmd

F32 = mybir.dt.float32
F32R = mybir.dt.float32r
BF16 = mybir.dt.bfloat16
AF = mybir.ActivationFunctionType
ALU = mybir.AluOpType
AX = mybir.AxisListType

T = 2048
D = 1024
NT = T // 128
ND = D // 128
C0 = 128
NE0 = 32
H0 = 256
NH = 16
KQ = 64
NE1 = 8
F1 = 2752
NF1 = 22

ENGSEM = {"sync": "sy", "tensor": "pe", "vector": "dv", "scalar": "ac", "gpsimd": "pl"}


class Prog:

    def __init__(self, nc):
        self.nc = nc
        self.ops = {e: [] for e in ("sync", "tensor", "vector", "scalar", "gpsimd")}
        self.cnt = {}
        self.sems = {}

    def _sem(self, name):
        if name not in self.sems:
            self.sems[name] = self.nc.alloc_semaphore(name)
            self.cnt[name] = 0
        return self.sems[name]

    def emit(self, eng, fn, waits=(), incs=()):
        toks = {}
        incs2 = []
        for s, n in incs:
            self._sem(s)
            self.cnt[s] += n
            incs2.append((s, n, self.cnt[s]))
            toks[s] = self.cnt[s]
        waits2 = []
        for s, v in waits:
            self._sem(s)
            waits2.append((s, v))
        self.ops[eng].append((fn, waits2, incs2))
        return toks

    def val(self, sem):
        self._sem(sem)
        return self.cnt[sem]

    def play(self):
        with self.nc.Block() as blk:
            for eng_name, reg in (("sync", blk.sync), ("tensor", blk.tensor),
                                  ("vector", blk.vector), ("scalar", blk.scalar),
                                  ("gpsimd", blk.gpsimd)):
                def body(eng, eng_name=eng_name):
                    for fn, waits, incs in self.ops[eng_name]:
                        for s, v in waits:
                            eng.wait_ge(self.sems[s], v)
                        instr = fn(eng)
                        for s, n, _v in incs:
                            if instr is not None:
                                instr.then_inc(self.sems[s], n)
                            else:
                                eng.sem_inc(self.sems[s], n)
                reg(body)


class Chain:

    def __init__(self, p):
        self.p = p
        self.prev = None

    def step(self, eng, fn, extra_waits=()):
        sem = ENGSEM[eng]
        waits = list(extra_waits)
        if self.prev is not None:
            waits.append(self.prev)
        toks = self.p.emit(eng, fn, waits, [(sem, 1)])
        self.prev = (sem, toks[sem])
        return (sem, toks[sem])

    def wait_also(self, tok):
        if tok is not None:
            self._extra = tok



def prep_host(inputs):
    f = np.float32
    com = {}
    x = np.ascontiguousarray(np.asarray(inputs["L0_attn_x"], f))

    com["wqkvg"] = np.ascontiguousarray(np.concatenate(
        [inputs["L0_attn_Wq"], inputs["L0_attn_Wk"], inputs["L0_attn_Wv"],
         inputs["L0_attn_Wg"]], axis=1).astype(f))
    com["wab"] = np.ascontiguousarray(np.concatenate(
        [inputs["L0_attn_Wa"], inputs["L0_attn_Wb"]], axis=1).astype(f))
    com["bqkvg"] = np.ascontiguousarray(np.concatenate(
        [inputs["L0_attn_bq"], inputs["L0_attn_bk"], inputs["L0_attn_bv"],
         inputs["L0_attn_bg"]]).astype(f)[None, :])
    com["bab"] = np.ascontiguousarray(np.concatenate(
        [inputs["L0_attn_ba"], inputs["L0_attn_bb"]]).astype(f)[None, :])
    com["wo0"] = np.ascontiguousarray(np.asarray(inputs["L0_attn_Wo"], f))
    com["bo0"] = np.ascontiguousarray(np.asarray(inputs["L0_attn_bo"], f)[None, :])

    ident = np.eye(128, dtype=f)
    com["ident"] = ident
    com["ident_b"] = ident.astype(ml_dtypes.bfloat16)
    com["twoI"] = (2.0 * ident)
    r = np.arange(128)
    com["trilcum"] = (r[None, :] >= r[:, None]).astype(f)
    com["smask"] = (r[None, :] < r[:, None]).astype(f)
    com["umask"] = (r[None, :] >= r[:, None]).astype(f)
    com["ones1x512"] = np.ones((1, 128), f)
    com["ones1x512f"] = np.ones((1, 128), f)
    com["ones1b"] = np.ones((1, 128), ml_dtypes.bfloat16)
    com["eyeI"] = ident

    rwt = np.asarray(inputs["L0_ffn_router_w"], f).T
    com["rwT"] = np.ascontiguousarray(
        rwt.reshape(8, 128, 32).transpose(1, 0, 2).reshape(128, 8 * 32))
    com["rb"] = np.ascontiguousarray(np.asarray(inputs["L0_ffn_router_b"], f)[None, :])
    b1 = np.asarray(inputs["L0_ffn_bias1"], f)
    com["b1c"] = np.ascontiguousarray(b1.reshape(32, 2, 128).transpose(2, 0, 1).reshape(128, 64))
    w1 = np.asarray(inputs["L0_ffn_weights1"], f)
    com["w1"] = np.ascontiguousarray(
        w1.reshape(32, 8, 128, 256).transpose(0, 2, 1, 3).reshape(32, 128, 8 * 256))
    w2 = np.asarray(inputs["L0_ffn_weights2"], f)
    com["w2"] = np.ascontiguousarray(
        w2.reshape(32, 2, 128, 1024).transpose(0, 2, 1, 3).reshape(32, 128, 2 * 1024))
    com["b2"] = np.ascontiguousarray(np.asarray(inputs["L0_ffn_bias2"], f))

    def pairtile(w):
        w2 = np.ascontiguousarray(np.asarray(w, f).reshape(1024, 1024))
        return np.ascontiguousarray(
            w2.reshape(8, 128, 8, 128).transpose(2, 1, 0, 3).reshape(8, 128, 1024))
    com["wq1"] = pairtile(inputs["L1_attn_w_q"])
    com["wk1"] = pairtile(inputs["L1_attn_w_k"])
    com["wv1"] = pairtile(inputs["L1_attn_w_v"])
    com["wo1T"] = np.ascontiguousarray(
        np.asarray(inputs["L1_attn_w_o"], f).reshape(1024, 1024).T)
    r512 = np.arange(512)[None, :]
    m128 = np.arange(128)[:, None]
    com["cmask"] = np.ascontiguousarray(np.stack(
        [(r512 >= (m128 + o * 128)).astype(ml_dtypes.bfloat16) for o in range(4)]))
    com["onescol_b"] = np.ones((128, 1), ml_dtypes.bfloat16)

    com["rw1"] = np.ascontiguousarray(
        np.asarray(inputs["L1_ffn_Ws"], f).reshape(8, 128, 8).transpose(1, 0, 2).reshape(128, 64))
    com["rb1"] = np.ascontiguousarray(np.asarray(inputs["L1_ffn_bs"], f)[None, :])
    def ffntile(w):
        wp = np.zeros((8, 1024, NF1 * 128), f)
        wp[:, :, 0:2752] = np.asarray(w, f)
        return np.ascontiguousarray(
            wp.reshape(8, 8, 128, NF1, 128).transpose(0, 3, 2, 1, 4).reshape(8, NF1, 128, 1024))
    com["wg1"] = ffntile(inputs["L1_ffn_Wg"])
    com["wu1"] = ffntile(inputs["L1_ffn_Wu"])
    wd = np.zeros((8, NF1 * 128, 1024), f)
    wd[:, 0:2752, :] = np.asarray(inputs["L1_ffn_Wd"], f)
    com["wd1"] = np.ascontiguousarray(wd.reshape(8, NF1, 128, 1024))

    percore = []
    for b in range(8):
        d = {"x": np.ascontiguousarray(x[b]),
             "xT": np.ascontiguousarray(x[b].T)}
        percore.append(d)
    return com, percore



def alloc_psum(ctx, nc):
    ps = {}
    for i in range(6):
        ps[i] = ctx.enter_context(nc.psum_tensor("ps%d" % i, [128, 512], F32))
    ps["tr"] = ctx.enter_context(nc.psum_tensor("ps_tr", [128, 128], F32))
    ps["trb"] = ctx.enter_context(nc.psum_tensor("ps_trb", [128, 1024], BF16))
    return ps


def stage_a(p, ctx, env, dbg=None):
    nc = p.nc
    ps = env["ps"]
    XT, XFMR = env["XT"], env["XFMR"]
    cst = env["cst"]

    sb = lambda name, shape, dt: ctx.enter_context(nc.sbuf_tensor(name, shape, dt))
    wqkvg = sb("a_wqkvg", [128, 8 * 512], F32R)
    wab = sb("a_wab", [128, 8 * 2], F32R)
    wo = sb("a_wo", [128, 1024], BF16)
    bqkvg = sb("a_bqkvg", [1, 512], F32R)
    bab = sb("a_bab", [1, 2], F32R)
    bo = sb("a_bo", [1, 1024], BF16)
    def db(name, shape, dt):
        return [sb(name + "0", shape, dt), sb(name + "1", shape, dt)]
    qf = db("a_qf", [128, 128], F32)
    kf = db("a_kf", [128, 128], F32)
    kt = db("a_kt", [128, 128], F32)
    vt = db("a_vt", [128, 128], F32)
    gtm = db("a_gtm", [128, 128], F32)
    ELt = db("a_elt", [128, 128], F32)
    Zt = db("a_z", [128, 128], F32)
    colsb = db("a_cols", [128, 24], F32)
    lar2 = db("a_lar", [128, 2], F32)
    lrow2 = db("a_lrow", [1, 128], F32)
    lcol2 = db("a_lcolr", [128, 1], F32)
    qtm = sb("a_qtm", [128, 128], F32)
    scr = sb("a_scr", [128, 128], F32)
    ELl = sb("a_ell", [128, 128], F32)
    Ast = sb("a_ast", [128, 128], F32)
    ZTt = sb("a_zt", [128, 128], F32)
    Vn = sb("a_vn", [128, 128], F32)
    m1 = sb("a_m1", [128, 128], F32)
    Bst = sb("a_bst", [128, 128], F32)
    Wfm = sb("a_wfm", [128, 128], F32)
    Wtm = sb("a_wtm", [128, 128], F32)
    Wp = sb("a_wp", [128, 128], F32)
    ktw = sb("a_ktw", [128, 128], F32)
    Pt = sb("a_pt", [128, 128], F32)
    Otm = sb("a_otm", [128, 128], BF16)
    Ofm = sb("a_ofm", [128, 128], BF16)
    Sst = sb("a_s", [128, 128], F32)
    Rst = sb("a_r", [128, 128], F32)
    m2 = sb("a_m2", [128, 128], F32)
    zres = sb("a_zres", [128, 1024], F32)
    lnt = alloc_lnt(ctx, nc, "a")
    rstd = lnt["rstd"]

    p._sem("pe"); p._sem("ac"); p._sem("dv"); p._sem("w")

    def loads(eng):
        last = None
        for dc in range(ND):
            last = eng.dma_start(wqkvg[:, dc * 512:(dc + 1) * 512],
                                 env["d_wqkvg"][dc * 128:(dc + 1) * 128, :])
            last.then_inc(p.sems["w"], 16)
            last = eng.dma_start(wab[:, dc * 2:(dc + 1) * 2],
                                 env["d_wab"][dc * 128:(dc + 1) * 128, :])
            last.then_inc(p.sems["w"], 16)
        for nm, tile in (("d_bqkvg", bqkvg), ("d_bab", bab)):
            last = eng.dma_start(tile[:], env[nm])
            last.then_inc(p.sems["w"], 16)
        return None
    pload = env.get("pload", p)
    if pload is not p:
        pload.sems["w"] = p.sems["w"]
        pload.cnt["w"] = 0
    pload.emit("sync", loads, waits=(), incs=())
    def loads_cast(eng):
        eng.dma_start(wo[:], env["d_wo0"]).then_inc(p.sems["w"], 16)
        eng.dma_start(bo[:], env["d_bo0"]).then_inc(p.sems["w"], 16)
        return None
    pload.emit("gpsimd", loads_cast, waits=(), incs=())
    n_w = 20 * 16
    p.cnt["w"] += n_w
    if pload is not p:
        def wwait(eng):
            eng.wait_ge(p.sems["w"], n_w)
            return None
        pload.emit("sync", wwait)
    wtok = ("w", p.val("w"))
    ld_x = env["xload_tok"]

    tk = p.emit("vector", lambda e: e.tensor_scalar(Sst[:], cst["eyeI"][:], 0.0, None, ALU.mult),
                waits=[ld_x] if ld_x else [], incs=[("dv", 1)])
    tk = p.emit("vector", lambda e: e.tensor_scalar(Rst[:], cst["eyeI"][:], 0.0, None, ALU.mult),
                waits=(), incs=[("dv", 1)])
    tk = p.emit("vector", lambda e: e.tensor_copy(lar2[0][:, 1:2], cst["eyeI"][:, 0:1]),
                waits=(), incs=[("dv", 1)])
    tk = p.emit("vector", lambda e: e.tensor_copy(lar2[1][:, 1:2], cst["eyeI"][:, 0:1]),
                waits=(), incs=[("dv", 1)])
    init_tok = ("dv", tk["dv"])

    sec1_end = {}
    sec2_end = {-1: init_tok}

    def emit_sec1(ci):
        pr = ci % 2
        cols = colsb[pr]
        col = lambda i: cols[:, i:i + 1]
        lar, lrow, lcolr = lar2[pr], lrow2[pr], lcol2[pr]
        ch = Chain(p)
        w0 = [t for t in (sec2_end.get(ci - 2), sec1_end.get(ci - 1), wtok, ld_x) if t]
        tok0 = ci * 128
        xv = lambda dc, tok0=tok0: XFMR[:, dc * 2048 + tok0: dc * 2048 + tok0 + 128]

        def proj(eng, xv=xv):
            for dc in range(ND):
                eng.matmul(ps[0][:], xv(dc), wqkvg[:, dc * 512:(dc + 1) * 512],
                           start=(dc == 0), stop=False)
            return eng.matmul(ps[0][:], cst["ones1x512"][:1, 0:128], bqkvg[:],
                              start=False, stop=True)
        ch.step("tensor", proj, extra_waits=w0)
        def proj2(eng, xv=xv):
            for dc in range(ND):
                eng.matmul(ps[1][:, 0:2], xv(dc), wab[:, dc * 2:(dc + 1) * 2],
                           start=(dc == 0), stop=False)
            return eng.matmul(ps[1][:, 0:2], cst["ones1x512"][:1, 0:128], bab[:],
                              start=False, stop=True)
        ch.step("tensor", proj2)
        ch.step("vector", lambda e: e.tensor_copy(qtm[:], ps[0][:, 0:128]))
        ch.step("scalar", lambda e: e.activation(scr[:], ps[0][:, 128:256], AF.Square,
                                                 accum_out=col(0)))
        ch.step("scalar", lambda e: e.activation(col(1), col(0), AF.Ln))
        ch.step("scalar", lambda e: e.activation(col(2), col(1), AF.Exp, scale=-0.5))
        ch.step("vector", lambda e: e.tensor_scalar_min(col(2), col(2), 1e12))
        ch.step("vector", lambda e, pr=pr: e.tensor_scalar(kt[pr][:], ps[0][:, 128:256],
                                                           col(2), None, ALU.mult))
        ch.step("vector", lambda e, pr=pr: e.tensor_copy(vt[pr][:], ps[0][:, 256:384]))
        ch.step("scalar", lambda e: e.activation(gtm[pr][:], ps[0][:, 384:512], AF.Exp,
                                                 scale=-1.0) if False else
                e.activation(gtm[pr][:], ps[0][:, 384:512], AF.Exp, scale=-1.0))
        ch.step("vector", lambda e, pr=pr: e.tensor_scalar_add(gtm[pr][:], gtm[pr][:], 1.0))
        ch.step("vector", lambda e, pr=pr: e.reciprocal(gtm[pr][:], gtm[pr][:]))
        ch.step("scalar", lambda e: e.activation(cols[:, 4:6], ps[1][:, 0:2], AF.Abs))
        ch.step("scalar", lambda e: e.activation(cols[:, 6:8], cols[:, 4:6], AF.Exp, scale=-1.0))
        ch.step("vector", lambda e: e.tensor_scalar_add(cols[:, 6:8], cols[:, 6:8], 1.0))
        ch.step("scalar", lambda e: e.activation(cols[:, 8:10], cols[:, 6:8], AF.Ln))
        ch.step("scalar", lambda e: e.activation(cols[:, 4:6], ps[1][:, 0:2], AF.Relu, scale=-1.0))
        ch.step("vector", lambda e: e.tensor_add(cols[:, 8:10], cols[:, 8:10], cols[:, 4:6]))
        ch.step("vector", lambda e, lar=lar: e.tensor_scalar(lar[:, 0:1], cols[:, 8:9],
                                                             -1.0, None, ALU.mult))
        ch.step("scalar", lambda e: e.activation(col(3), cols[:, 9:10], AF.Exp, scale=-1.0))
        ch.step("tensor", lambda e: e.transpose(ps["tr"][:], qtm[:], cst["ident"][:]))
        ch.step("vector", lambda e, pr=pr: e.tensor_copy(qf[pr][:], ps["tr"][:]))
        ch.step("tensor", lambda e, pr=pr: e.transpose(ps["tr"][:], kt[pr][:], cst["ident"][:]))
        ch.step("vector", lambda e, pr=pr: e.tensor_copy(kf[pr][:], ps["tr"][:]))
        ch.step("tensor", lambda e, lar=lar: e.matmul(ps[2][:, 0:2], cst["trilcum"][:],
                                                      lar[:, 0:2], start=True, stop=True))
        ch.step("tensor", lambda e, lar=lar: e.matmul(ps[2][0:2, 4:132], lar[:, 0:2],
                                                      cst["trilcum"][:], start=True, stop=True))
        ch.step("vector", lambda e, lcolr=lcolr: e.tensor_copy(lcolr[:], ps[2][:, 0:1]))
        ch.step("scalar", lambda e: e.activation(col(11), ps[2][:, 0:1], AF.Exp))
        ch.step("vector", lambda e, lrow=lrow: e.tensor_copy(lrow[:], ps[2][0:1, 4:132]))
        ch.step("tensor", lambda e, lrow=lrow: e.matmul(ps[3][:, 0:128],
                                                        cst["ones1x512f"][:1, 0:128],
                                                        lrow[:], start=True, stop=True))
        ch.step("vector", lambda e, lcolr=lcolr: e.tensor_scalar(m1[:], ps[3][:, 0:128],
                                                                 lcolr[:], 0.0,
                                                                 ALU.subtract, ALU.max))
        ch.step("scalar", lambda e: e.activation(ELl[:], m1[:], AF.Exp, scale=-1.0))
        ch.step("vector", lambda e, lcolr=lcolr: e.tensor_scalar(m1[:], ps[3][:, 0:128],
                                                                 lcolr[:], 0.0,
                                                                 ALU.subtract, ALU.min))
        ch.step("scalar", lambda e, pr=pr: e.activation(ELt[pr][:], m1[:], AF.Exp))
        ch.step("tensor", lambda e, pr=pr: e.matmul(ps[4][:, 0:128], kf[pr][:], kf[pr][:],
                                                    start=True, stop=True))
        ch.step("vector", lambda e: e.tensor_mul(m1[:], ps[4][:, 0:128], ELl[:]))
        ch.step("vector", lambda e: e.tensor_scalar(m1[:], m1[:], col(3), None, ALU.mult))
        ch.step("vector", lambda e: e.tensor_mul(m1[:], m1[:], cst["smask"][:]))
        ch.step("vector", lambda e: e.tensor_add(Ast[:], m1[:], cst["eyeI"][:]))
        ch.step("vector", lambda e, pr=pr: e.tensor_copy(Zt[pr][:], cst["eyeI"][:]))
        ch.step("vector", lambda e: e.tensor_copy(ZTt[:], cst["eyeI"][:]))
        for it in range(7):
            ch.step("tensor", lambda e, pr=pr: e.matmul(ps[4][:, 128:256], Ast[:], Zt[pr][:],
                                                        start=True, stop=True))
            ch.step("vector", lambda e: e.scalar_tensor_tensor(
                Vn[:], ps[4][:, 128:256], -1.0, cst["twoI"][:], ALU.mult, ALU.add))
            if it < 6:
                def nz(eng):
                    eng.matmul(ps[4][:, 256:384], ZTt[:], Vn[:], start=True, stop=True)
                    return eng.matmul(ps[4][:, 384:512], Vn[:], ZTt[:], start=True, stop=True)
                ch.step("tensor", nz)
                ch.step("vector", lambda e, pr=pr: e.tensor_copy(Zt[pr][:], ps[4][:, 256:384]))
                ch.step("vector", lambda e: e.tensor_copy(ZTt[:], ps[4][:, 384:512]))
            else:
                ch.step("tensor", lambda e: e.matmul(ps[4][:, 256:384], ZTt[:], Vn[:],
                                                     start=True, stop=True))
                ch.step("vector", lambda e, pr=pr: e.tensor_copy(Zt[pr][:], ps[4][:, 256:384]))
        sec1_end[ci] = ch.prev

    def emit_sec2(ci):
        pr = ci % 2
        cols = colsb[pr]
        col = lambda i: cols[:, i:i + 1]
        lrow, lcolr = lrow2[pr], lcol2[pr]
        ch = Chain(p)
        w0 = [t for t in (sec1_end[ci], sec2_end.get(ci - 1)) if t]
        ch.step("tensor", lambda e, pr=pr: e.matmul(ps[5][:, 0:128], kf[pr][:], Rst[:],
                                                    start=True, stop=True), extra_waits=w0)
        ch.step("vector", lambda e: e.tensor_scalar(col(13), col(3), col(11), -1.0,
                                                    ALU.mult, ALU.mult))
        ch.step("vector", lambda e, pr=pr: e.tensor_scalar(m2[:], vt[pr][:], col(3),
                                                           None, ALU.mult))
        ch.step("vector", lambda e: e.scalar_tensor_tensor(
            Bst[:], ps[5][:, 0:128], col(13), m2[:], ALU.mult, ALU.add))
        def wmm(eng, pr=pr):
            eng.matmul(ps[5][:, 128:256], Bst[:], Zt[pr][:], start=True, stop=True)
            return eng.matmul(ps[5][:, 256:384], Zt[pr][:], Bst[:], start=True, stop=True)
        ch.step("tensor", wmm)
        ch.step("vector", lambda e: e.tensor_copy(Wfm[:], ps[5][:, 128:256]))
        ch.step("vector", lambda e: e.tensor_copy(Wtm[:], ps[5][:, 256:384]))
        def qmm(eng, pr=pr):
            eng.matmul(ps[5][:, 0:128], qf[pr][:], Sst[:], start=True, stop=True)
            return eng.matmul(ps[5][:, 384:512], Wfm[:], qf[pr][:], start=True, stop=True)
        ch.step("tensor", qmm)
        ch.step("vector", lambda e, pr=pr: e.tensor_mul(m2[:], ps[5][:, 384:512], ELt[pr][:]))
        ch.step("vector", lambda e: e.tensor_mul(Pt[:], m2[:], cst["umask"][:]))
        ch.step("vector", lambda e: e.tensor_scalar(m2[:], ps[5][:, 0:128], col(11),
                                                    None, ALU.mult))
        ch.step("tensor", lambda e, pr=pr: e.matmul(ps[5][:, 128:256], Pt[:], kt[pr][:],
                                                    start=True, stop=True))
        ch.step("vector", lambda e: e.tensor_add(m2[:], m2[:], ps[5][:, 128:256]))
        ch.step("vector", lambda e, pr=pr: e.tensor_mul(Otm[:], m2[:], gtm[pr][:]))
        ch.step("tensor", lambda e, lrow=lrow: e.matmul(ps[5][:, 256:258],
                                                        cst["ones1x512f"][:1, 0:128],
                                                        lrow[:1, 126:128],
                                                        start=True, stop=True))
        ch.step("vector", lambda e, lcolr=lcolr: e.tensor_scalar(col(15), ps[5][:, 257:258],
                                                                 lcolr[:], None, ALU.subtract))
        ch.step("scalar", lambda e: e.activation(col(16), col(15), AF.Exp))
        ch.step("scalar", lambda e: e.activation(col(17), ps[5][:, 257:258], AF.Exp))
        ch.step("vector", lambda e: e.tensor_scalar(Wp[:], Wtm[:], col(16), None, ALU.mult))
        ch.step("vector", lambda e, pr=pr: e.tensor_scalar(ktw[:], kt[pr][:], col(16),
                                                           None, ALU.mult))
        def smm(eng, pr=pr):
            eng.matmul(ps[5][:, 0:128], Wp[:], kt[pr][:], start=True, stop=True)
            return eng.matmul(ps[5][:, 128:256], ktw[:], Wtm[:], start=True, stop=True)
        ch.step("tensor", smm)
        ch.step("vector", lambda e: e.scalar_tensor_tensor(
            Sst[:], Sst[:], col(17), ps[5][:, 0:128], ALU.mult, ALU.add))
        ch.step("vector", lambda e: e.scalar_tensor_tensor(
            Rst[:], Rst[:], col(17), ps[5][:, 128:256], ALU.mult, ALU.add))
        ch.step("tensor", lambda e: e.transpose(ps["trb"][:, 0:128], Otm[:],
                                                cst["ident_b"][:]))
        ch.step("vector", lambda e: e.tensor_copy(Ofm[:], ps["trb"][:, 0:128]))
        for h in range(2):
            def ymm(eng, h=h):
                eng.matmul(ps[5][:], Ofm[:], wo[:, h * 512:(h + 1) * 512],
                           start=True, stop=False)
                return eng.matmul(ps[5][:], cst["ones1b"][:1, 0:128],
                                  bo[:1, h * 512:(h + 1) * 512], start=False, stop=True)
            ch.step("tensor", ymm)
            ch.step("vector", lambda e, h=h, ci=ci: e.tensor_add(
                zres[:, h * 512:(h + 1) * 512], ps[5][:],
                XT[:, ci * 1024 + h * 512: ci * 1024 + (h + 1) * 512]))
        emit_ln(ch, env, lnt, zres[:], XT[:, ci * 1024:(ci + 1) * 1024])
        sec2_end[ci] = ch.prev

    emit_sec1(0)
    for ci in range(NT):
        if ci + 1 < NT:
            emit_sec1(ci + 1)
        emit_sec2(ci)

    chf = Chain(p)
    chf.prev = sec2_end[NT - 1]
    return chf


def emit_ln(ch, env, lnt, zres, xt_dst):
    AFc = AF
    ch.step("scalar", lambda e: e.activation(lnt["junk"][:], zres, AFc.Copy,
                                             accum_out=lnt["c18"][:]))
    ch.step("vector", lambda e: e.scalar_tensor_tensor(lnt["junk"][:], zres, 1.0, zres,
                                                       ALU.mult, ALU.mult,
                                                       accum_out=lnt["c19"][:]))
    ch.step("vector", lambda e: e.tensor_scalar(lnt["c20"][:], lnt["c18"][:], 1.0 / 1024.0,
                                                None, ALU.mult))
    ch.step("vector", lambda e: e.tensor_scalar(lnt["c21"][:], lnt["c19"][:], 1.0 / 1024.0,
                                                None, ALU.mult))
    ch.step("vector", lambda e: e.tensor_scalar(lnt["c22"][:], lnt["c20"][:], lnt["c20"][:],
                                                -1.0, ALU.mult, ALU.mult))
    ch.step("vector", lambda e: e.tensor_add(lnt["c23"][:], lnt["c22"][:], lnt["c21"][:]))
    ch.step("vector", lambda e: e.tensor_scalar_add(lnt["c23"][:], lnt["c23"][:], 1e-5))
    ch.step("scalar", lambda e: e.activation(lnt["c1"][:], lnt["c23"][:], AFc.Ln))
    ch.step("scalar", lambda e: e.activation(lnt["rstd"][:], lnt["c1"][:], AFc.Exp, scale=-0.5))
    ch.step("vector", lambda e: e.tensor_scalar(xt_dst, zres, lnt["c20"][:], lnt["rstd"][:],
                                                ALU.subtract, ALU.mult))


def alloc_lnt(ctx, nc, tag, junk=None):
    sb = lambda name, shape, dt: ctx.enter_context(nc.sbuf_tensor(name, shape, dt))
    cols = sb(tag + "_lncols", [128, 8], F32)
    return {"junk": junk if junk is not None else sb(tag + "_junk", [128, 1024], F32),
            "c18": cols[:, 0:1], "c19": cols[:, 1:2], "c20": cols[:, 2:3],
            "c21": cols[:, 3:4], "c22": cols[:, 4:5], "c23": cols[:, 5:6],
            "c1": cols[:, 6:7], "rstd": cols[:, 7:8]}


def emit_router(p, ch, env, ctx_tiles, n_e, rw, rb, lgt_cb):
    nc = p.nc
    ps = env["ps"]
    XT = env["XT"]
    cst = env["cst"]
    xtr = ctx_tiles["xtr"]
    for ci in range(NT):
        for dc in range(ND):
            ch.step("tensor", lambda e, ci=ci, dc=dc: e.transpose(
                ps["tr"][:], XT[:, ci * 1024 + dc * 128: ci * 1024 + (dc + 1) * 128],
                cst["ident"][:]))
            ch.step("vector", lambda e, dc=dc: e.tensor_copy(
                xtr[:, dc * 128:(dc + 1) * 128], ps["tr"][:]))
        def rmm(eng, n_e=n_e):
            for dc in range(ND):
                eng.matmul(ps[5][:, 0:n_e], xtr[:, dc * 128:(dc + 1) * 128],
                           rw[:, dc * n_e:(dc + 1) * n_e], start=(dc == 0), stop=False)
            return eng.matmul(ps[5][:, 0:n_e], cst["ones1x512f"][:1, 0:128], rb[:],
                              start=False, stop=True)
        ch.step("tensor", rmm)
        lgt_cb(ci)


def stage_b(p, ctx, env):
    nc = p.nc
    ps = env["ps"]
    XT, XFMB = env["XT"], env["XFMB"]
    cst = env["cst"]
    sb = lambda name, shape, dt: ctx.enter_context(nc.sbuf_tensor(name, shape, dt))

    rw = sb("b_rw", [128, ND * 32], F32)
    rb = sb("b_rb", [1, 32], F32)
    b1c = sb("b_b1c", [128, 64], F32)
    w1b = [sb("b_w1b%d" % i, [128, 8 * 256], BF16) for i in range(2)]
    w2b = [sb("b_w2b%d" % i, [128, 2 * 1024], BF16) for i in range(2)]
    b2r = [sb("b_b2r%d" % i, [1, 1024], BF16) for i in range(2)]
    hfm = sb("b_hfm", [128, 2 * 512], BF16)
    acc = sb("b_acc", [128, 4 * 1024], F32)
    xtr = sb("b_xtr", [128, 1024], F32)
    lgt = sb("b_lgt", [128, 32], F32)
    msk = sb("b_msk", [128, NT * 32], F32)
    mcol = sb("b_mcol", [128, 1], F32)
    tmp = sb("b_tmp", [128, 512], BF16)
    zres = sb("b_zres", [128, 1024], F32)
    lnt = alloc_lnt(ctx, nc, "b")

    ch = Chain(p)
    ch.prev = env.get("start_tok")

    p._sem("w")
    def loads(eng):
        eng.dma_start(rw[:], env["d_rwT"]).then_inc(p.sems["w"], 16)
        eng.dma_start(rb[:], env["d_rb"]).then_inc(p.sems["w"], 16)
        eng.dma_start(b1c[:], env["d_b1c"]).then_inc(p.sems["w"], 16)
        return None
    p.emit("sync", loads, waits=[env["xload_tok"]] if env.get("xload_tok") else [])
    p.cnt["w"] += 48
    wtok0 = ("w", p.val("w"))
    if ch.prev is None:
        ch.prev = wtok0

    def lgt_cb(ci):
        ch.step("vector", lambda e: e.tensor_copy(lgt[:], ps[5][:, 0:32]),
                extra_waits=[wtok0] if ci == 0 else [])
        ch.step("vector", lambda e: e.reduce_max(mcol[:], lgt[:], axis=AX.X))
        ch.step("vector", lambda e, ci=ci: e.tensor_scalar(
            msk[:, ci * 32:(ci + 1) * 32], lgt[:], mcol[:], None, ALU.is_equal))
        if env.get("dbgb") and ci == 0:
            def dlgt(eng):
                return eng.dma_start(env["dbgb"]["lgt0"], lgt[:])
            prev = ch.prev
            tk = p.emit("sync", dlgt, waits=[prev], incs=[("sy", 16)])
            ch.prev = ("sy", tk["sy"])
    emit_router(p, ch, env, {"xtr": xtr}, 32, rw, rb, lgt_cb)
    if env.get("dbgb"):
        def dmsk(eng):
            return eng.dma_start(env["dbgb"]["msk"], msk[:])
        prev = ch.prev
        tk = p.emit("sync", dmsk, waits=[prev], incs=[("sy", 16)])
        ch.prev = ("sy", tk["sy"])

    p._sem("pl"); p._sem("pe"); p._sem("ac"); p._sem("dv")
    wtok = {}
    tmp2 = [tmp, ctx.enter_context(nc.sbuf_tensor("b_tmp2", [128, 512], BF16))]

    def emit_wload(s, e_i, wait_tok):
        par = e_i % 2
        def wload(eng, e_i=e_i, par=par):
            eng.dma_start(w1b[par][:], env["d_w1"][e_i]).then_inc(p.sems["pl"], 16)
            eng.dma_start(w2b[par][:], env["d_w2"][e_i]).then_inc(p.sems["pl"], 16)
            eng.dma_start(b2r[par][:], env["d_b2"][e_i:e_i + 1, :]
                          ).then_inc(p.sems["pl"], 16)
            return None
        p.emit("gpsimd", wload, waits=[wait_tok] if wait_tok else [])
        p.cnt["pl"] += 48
        wtok[(s, e_i)] = ("pl", p.val("pl"))

    start_tok = ch.prev
    stage_start = env.get("start_tok")
    hbank_guard = [start_tok, start_tok]
    ybank_guard = [start_tok] * 4
    hfm_guard = None
    wbuf_guard = [stage_start, stage_start]
    acc_guard = {}
    tmp_guard = [None, None]

    for s in range(4):
        tkm = p.emit("vector", lambda e: e.memset(acc[:], 0.0),
                     waits=[start_tok], incs=[("dv", 1)])
        acc_init = ("dv", tkm["dv"])
        for sub in range(4):
            for half in range(2):
                acc_guard[(sub, half)] = acc_init
        emit_wload(s, 0, wbuf_guard[0])
        emit_wload(s, 1, wbuf_guard[1])
        for e_i in range(NE0):
            par = e_i % 2
            relu_toks = []
            for hc in range(2):
                def hmm2(eng, s=s, par=par, hc=hc):
                    last = None
                    for dc in range(ND):
                        last = eng.matmul(
                            ps[hc][:],
                            w1b[par][:, dc * 256 + hc * 128: dc * 256 + (hc + 1) * 128],
                            XFMB[:, dc * 2048 + s * 512: dc * 2048 + (s + 1) * 512],
                            start=(dc == 0), stop=(dc == ND - 1))
                    return last
                waits = [wtok[(s, e_i)], hbank_guard[hc]]
                if hc == 0 and hfm_guard is not None:
                    waits.append(hfm_guard)
                tk = p.emit("tensor", hmm2, waits=waits, incs=[("pe", 1)])
                mm_tok = ("pe", tk["pe"])
                tk = p.emit("scalar",
                            lambda e, e_i=e_i, hc=hc: e.activation(
                                hfm[:, hc * 512:(hc + 1) * 512], ps[hc][:], AF.Relu,
                                bias=b1c[:, e_i * 2 + hc: e_i * 2 + hc + 1]),
                            waits=[mm_tok], incs=[("ac", 1)])
                hbank_guard[hc] = ("ac", tk["ac"])
                relu_toks.append(hbank_guard[hc])
            last_ymm = None
            for sub in range(4):
                for half in range(2):
                    bk = (sub * 2 + half) % 4
                    def ymm(eng, par=par, sub=sub, half=half, bk=bk):
                        for hc in range(2):
                            eng.matmul(ps[2 + bk][:],
                                       hfm[:, hc * 512 + sub * 128: hc * 512 + (sub + 1) * 128],
                                       w2b[par][:, hc * 1024 + half * 512: hc * 1024 + (half + 1) * 512],
                                       start=(hc == 0), stop=False)
                        return eng.matmul(ps[2 + bk][:], cst["ones1b"][:1, 0:128],
                                          b2r[par][:1, half * 512:(half + 1) * 512],
                                          start=False, stop=True)
                    waits = [ybank_guard[bk]]
                    if sub == 0 and half == 0:
                        waits.append(relu_toks[1])
                    tk = p.emit("tensor", ymm, waits=waits, incs=[("pe", 1)])
                    last_ymm = ("pe", tk["pe"])
                    tpar = (sub * 2 + half) % 2
                    awaits = [last_ymm]
                    if tmp_guard[tpar] is not None:
                        awaits.append(tmp_guard[tpar])
                    tk = p.emit("scalar",
                                lambda e, s=s, sub=sub, e_i=e_i, tpar=tpar, bk=bk: e.activation(
                                    tmp2[tpar][:], ps[2 + bk][:], AF.Relu,
                                    scale=msk[:, (s * 4 + sub) * 32 + e_i:
                                              (s * 4 + sub) * 32 + e_i + 1]),
                                waits=awaits, incs=[("ac", 1)])
                    ac_tok = ("ac", tk["ac"])
                    ybank_guard[bk] = ac_tok
                    tk = p.emit("vector",
                                lambda e, sub=sub, half=half, tpar=tpar: e.tensor_add(
                                    acc[:, sub * 1024 + half * 512: sub * 1024 + (half + 1) * 512],
                                    acc[:, sub * 1024 + half * 512: sub * 1024 + (half + 1) * 512],
                                    tmp2[tpar][:]),
                                waits=[ac_tok, acc_guard[(sub, half)]], incs=[("dv", 1)])
                    acc_guard[(sub, half)] = ("dv", tk["dv"])
                    tmp_guard[tpar] = acc_guard[(sub, half)]
            hfm_guard = last_ymm
            wbuf_guard[par] = last_ymm
            if e_i + 2 < NE0:
                emit_wload(s, e_i + 2, wbuf_guard[par])
        ch.prev = ("dv", p.val("dv"))
        for sub in range(4):
            ci = s * 4 + sub
            ch.step("vector", lambda e, ci=ci, sub=sub: e.tensor_add(
                zres[:], acc[:, sub * 1024:(sub + 1) * 1024],
                XT[:, ci * 1024:(ci + 1) * 1024]))
            emit_ln(ch, env, lnt, zres[:], XT[:, ci * 1024:(ci + 1) * 1024])
        start_tok = ch.prev
    return ch


def stage_c(p, ctx, env):
    nc = p.nc
    ps = env["ps"]
    XT, XFMB = env["XT"], env["XFMB"]
    cst = env["cst"]
    sb = lambda name, shape, dt: ctx.enter_context(nc.sbuf_tensor(name, shape, dt))

    woTb = sb("c_wot", [128, 8 * 1024], BF16)
    wqp = [sb("c_wqp%d" % i, [128, 1024], BF16) for i in range(2)]
    wkp = [sb("c_wkp%d" % i, [128, 1024], BF16) for i in range(2)]
    wvp = [sb("c_wvp%d" % i, [128, 1024], BF16) for i in range(2)]
    qf = sb("c_qf", [128, 2048], BF16)
    kf = sb("c_kf", [128, 2048], BF16)
    vt = sb("c_vt", [128, 16 * 130], BF16)
    sq = sb("c_sq", [128, 2048], BF16)
    qrow = sb("c_qrow", [1, 2048], F32)
    krow = qrow
    scol = sb("c_scol", [1, 8], F32)
    negb = sb("c_negb", [128, 1], F32)
    pt2 = [sb("c_pt%d" % i, [128, 512], BF16) for i in range(2)]
    oall = sb("c_oall", [128, NT * 1024], BF16)
    ofm = sb("c_ofm", [128, 1024], BF16)
    cmask = sb("c_cmask", [128, 4 * 512], BF16)
    onescol = sb("c_onescol", [128, 1], BF16)
    rcol = sb("c_rcol", [128, 1], F32)
    zres = sb("c_zres", [128, 1024], F32)
    lnt = alloc_lnt(ctx, nc, "c")

    ch = Chain(p)
    ch.prev = env.get("start_tok")
    stage_start = env.get("start_tok")

    p._sem("w"); p._sem("pe"); p._sem("ac"); p._sem("dv")
    def loads(eng):
        for o in range(4):
            eng.dma_start(cmask[:, o * 512:(o + 1) * 512],
                          env["d_cmask"][o]).then_inc(p.sems["w"], 16)
        eng.dma_start(onescol[:], env["d_onescol"]).then_inc(p.sems["w"], 16)
        return None
    p.emit("sync", loads, waits=[env["xload_tok"]] if env.get("xload_tok") else [])
    def loads2(eng):
        for c in range(8):
            eng.dma_start(woTb[:].rearrange("p (c e) -> p c e", c=8)[:, c, :],
                          env["d_wo1T"][c * 128:(c + 1) * 128, :]).then_inc(p.sems["w"], 16)
        return None
    p.emit("gpsimd", loads2, waits=[env["xload_tok"]] if env.get("xload_tok") else [])
    p.cnt["w"] += 13 * 16
    wtok0 = ("w", p.val("w"))
    if ch.prev is None:
        ch.prev = wtok0

    p._sem("pl")
    wtok = {}
    wbuf_guard = [stage_start, stage_start]

    def emit_wload(pr, wait_tok):
        par = pr % 2
        def wload(eng, pr=pr, par=par):
            eng.dma_start(wqp[par][:], env["d_wq1"][pr]).then_inc(p.sems["pl"], 16)
            eng.dma_start(wkp[par][:], env["d_wk1"][pr]).then_inc(p.sems["pl"], 16)
            eng.dma_start(wvp[par][:], env["d_wv1"][pr]).then_inc(p.sems["pl"], 16)
            return None
        p.emit("gpsimd", wload, waits=[wait_tok] if wait_tok else [])
        p.cnt["pl"] += 48
        wtok[pr] = ("pl", p.val("pl"))

    emit_wload(0, wbuf_guard[0])
    emit_wload(1, wbuf_guard[1])

    qkv_bank_guard = [ch.prev, ch.prev]
    oall_guard = None
    for pr in range(8):
        par = pr % 2
        gi = 0
        last_wread = None
        qkv_evacs = []
        for tile, wp in ((qf, wqp[par]), (kf, wkp[par])):
            for st in range(4):
                bk = gi % 2
                def qmm(eng, wp=wp, st=st, bk=bk):
                    last = None
                    for dc in range(ND):
                        last = eng.matmul(ps[bk][:],
                                          wp[:, dc * 128:(dc + 1) * 128],
                                          XFMB[:, dc * 2048 + st * 512: dc * 2048 + (st + 1) * 512],
                                          start=(dc == 0), stop=(dc == ND - 1))
                    return last
                waits = [wtok[pr], qkv_bank_guard[bk]]
                tk = p.emit("tensor", qmm, waits=waits, incs=[("pe", 1)])
                last_wread = ("pe", tk["pe"])
                tk = p.emit("vector",
                            lambda e, tile=tile, st=st, bk=bk: e.tensor_copy(
                                tile[:, st * 512:(st + 1) * 512], ps[bk][:]),
                            waits=[last_wread], incs=[("dv", 1)])
                qkv_bank_guard[bk] = ("dv", tk["dv"])
                qkv_evacs.append(qkv_bank_guard[bk])
                gi += 1
        vw = [qkv_bank_guard[0]]
        if wbuf_guard[par]:
            vw.append(wbuf_guard[par])
        tk = p.emit("vector", lambda e: e.memset(vt[:], 1.0),
                    waits=vw, incs=[("dv", 1)])
        vt_ready = ("dv", tk["dv"])
        for mi in range(16):
            bk = gi % 2
            def vmm(eng, par=par, mi=mi, bk=bk):
                last = None
                for dc in range(ND):
                    last = eng.matmul(ps[bk][:, 0:128],
                                      XFMB[:, dc * 2048 + mi * 128: dc * 2048 + (mi + 1) * 128],
                                      wvp[par][:, dc * 128:(dc + 1) * 128],
                                      start=(dc == 0), stop=(dc == ND - 1))
                return last
            tk = p.emit("tensor", vmm, waits=[qkv_bank_guard[bk], vt_ready],
                        incs=[("pe", 1)])
            last_wread = ("pe", tk["pe"])
            tk = p.emit("vector",
                        lambda e, mi=mi, bk=bk: e.tensor_copy(
                            vt[:].rearrange("p (m hc) -> p m hc", m=16)[:, mi, :]
                                .rearrange("p (h c) -> p h c", h=2)[:, :, 0:64],
                            ps[bk][:, 0:128].rearrange("p (h c) -> p h c", h=2)),
                        waits=[last_wread], incs=[("dv", 1)])
            qkv_bank_guard[bk] = ("dv", tk["dv"])
            qkv_evacs.append(qkv_bank_guard[bk])
            gi += 1
        if pr + 2 < 8:
            emit_wload(pr + 2, last_wread)
        wbuf_guard[par] = last_wread

        ch.prev = qkv_evacs[-1]
        for qk, tile in ((0, qf), (2, kf)):
            ch.step("scalar", lambda e, tile=tile: e.activation(sq[:], tile[:], AF.Square))
            for hh in range(2):
                hs = hh * 64
                for st in range(4):
                    def nmm3(eng, hs=hs, st=st):
                        return eng.matmul(ps[5][0:1, 0:512],
                                          onescol[hs:hs + 64, 0:1],
                                          sq[hs:hs + 64, st * 512:(st + 1) * 512],
                                          start=True, stop=True)
                    ch.step("tensor", nmm3)
                    ch.step("vector", lambda e, st=st: e.tensor_copy(
                        qrow[:1, st * 512:(st + 1) * 512], ps[5][0:1, 0:512]))
                ch.step("vector", lambda e, qk=qk, hh=hh: e.reduce_max(
                    scol[:1, qk + hh: qk + hh + 1], qrow[:1, :], axis=AX.X))

        for hh in range(2):
            hs = hh * 64
            h_abs = pr * 2 + hh
            ch.step("vector", lambda e, hh=hh: e.tensor_mul(
                scol[:1, 4:5], scol[:1, hh:hh + 1], scol[:1, 2 + hh:3 + hh]))
            ch.step("scalar", lambda e: e.activation(scol[:1, 5:6], scol[:1, 4:5], AF.Ln))
            ch.step("scalar", lambda e: e.activation(scol[:1, 6:7], scol[:1, 5:6],
                                                     AF.Exp, scale=0.5))
            ch.step("vector", lambda e: e.tensor_scalar(scol[:1, 7:8], scol[:1, 6:7],
                                                        -0.125, None, ALU.mult))
            ch.step("tensor", lambda e: e.matmul(ps[5][:, 0:1], cst["ones1x512f"][:1, 0:128],
                                                 scol[:1, 7:8], start=True, stop=True))
            ch.step("vector", lambda e: e.tensor_copy(negb[:], ps[5][:, 0:1]))
            negb_tok = ch.prev

            sbank_guard = {0: ch.prev, 1: ch.prev}
            av_done = None
            pt_guard = [None, None]
            for st in range(4):
                nmi = st * 4 + 4
                o_evac_toks = []
                for mi in range(nmi):
                    sb_i = mi % 2
                    spb = ps[0] if sb_i == 0 else ps[5]
                    def smm(eng, hs=hs, mi=mi, st=st, spb=spb):
                        return eng.matmul(spb[:],
                                          kf[hs:hs + 64, mi * 128:(mi + 1) * 128],
                                          qf[hs:hs + 64, st * 512:(st + 1) * 512],
                                          start=True, stop=True)
                    tk = p.emit("tensor", smm, waits=[sbank_guard[sb_i]],
                                incs=[("pe", 1)])
                    smm_tok = ("pe", tk["pe"])
                    ewaits = [smm_tok]
                    if pt_guard[sb_i] is not None:
                        ewaits.append(pt_guard[sb_i])
                    tk = p.emit("scalar",
                                lambda e, sb_i=sb_i, spb=spb: e.activation(
                                    pt2[sb_i][:], spb[:], AF.Exp,
                                    scale=0.125, bias=negb[:]),
                                waits=ewaits, incs=[("ac", 1)])
                    exp_tok = ("ac", tk["ac"])
                    sbank_guard[sb_i] = exp_tok
                    o = mi - st * 4
                    ready_tok = exp_tok
                    if o >= 0:
                        tk = p.emit("vector",
                                    lambda e, o=o, sb_i=sb_i: e.tensor_mul(
                                        pt2[sb_i][:], pt2[sb_i][:],
                                        cmask[:, o * 512:(o + 1) * 512]),
                                    waits=[exp_tok], incs=[("dv", 1)])
                        ready_tok = ("dv", tk["dv"])
                    def avmm(eng, hh=hh, mi=mi, nmi=nmi, sb_i=sb_i):
                        last = None
                        for sub in range(4):
                            last = eng.matmul(
                                ps[1 + sub][:, 0:65],
                                pt2[sb_i][:, sub * 128:(sub + 1) * 128],
                                vt[:].rearrange("p (m c) -> p m c", m=16)[:, mi,
                                    hh * 65:(hh + 1) * 65],
                                start=(mi == 0), stop=(mi == nmi - 1))
                        return last
                    avw = [ready_tok]
                    if mi == 0 and av_done is not None:
                        avw.append(av_done)
                    elif mi == 0:
                        avw.append(negb_tok)
                    tk = p.emit("tensor", avmm, waits=avw, incs=[("pe", 1)])
                    pt_guard[sb_i] = ("pe", tk["pe"])
                    last_av = ("pe", tk["pe"])
                for sub in range(4):
                    ti = st * 4 + sub
                    tk = p.emit("vector",
                                lambda e, sub=sub: e.reciprocal(
                                    rcol[:], ps[1 + sub][:, 64:65]),
                                waits=[last_av] + ([o_evac_toks[-1]] if o_evac_toks else []),
                                incs=[("dv", 1)])
                    tk = p.emit("vector",
                                lambda e, sub=sub, ti=ti, h_abs=h_abs: e.tensor_scalar(
                                    oall[:, ti * 1024 + h_abs * 64: ti * 1024 + (h_abs + 1) * 64],
                                    ps[1 + sub][:, 0:64], rcol[:], None, ALU.mult),
                                waits=[("dv", tk["dv"])] + ([oall_guard] if oall_guard else []),
                                incs=[("dv", 1)])
                    o_evac_toks.append(("dv", tk["dv"]))
                av_done = o_evac_toks[-1]
            ch.prev = av_done
        qkv_bank_guard = [ch.prev, ch.prev]

    for ti in range(NT):
        def trall(eng, ti=ti):
            last = None
            for c in range(ND):
                last = eng.transpose(
                    ps["trb"][:, c * 128:(c + 1) * 128],
                    oall[:, ti * 1024 + c * 128: ti * 1024 + (c + 1) * 128],
                    cst["ident_b"][:])
            return last
        ch.step("tensor", trall)
        ch.step("vector", lambda e: e.tensor_copy(ofm[:], ps["trb"][:]))
        for half in range(2):
            def pmm(eng, half=half):
                last = None
                for c in range(ND):
                    last = eng.matmul(ps[half][:],
                                      ofm[:, c * 128:(c + 1) * 128],
                                      woTb[:, c * 1024 + half * 512: c * 1024 + (half + 1) * 512],
                                      start=(c == 0), stop=(c == ND - 1))
                return last
            ch.step("tensor", pmm)
            ch.step("vector", lambda e, ti=ti, half=half: e.tensor_add(
                zres[:, half * 512:(half + 1) * 512], ps[half][:],
                XT[:, ti * 1024 + half * 512: ti * 1024 + (half + 1) * 512]))
        emit_ln(ch, env, lnt, zres[:], XT[:, ti * 1024:(ti + 1) * 1024])
    return ch


def build_stage_c_program():
    nc = bass.Bass("TRN2", target_bir_lowering=False)
    env = {}
    d_x = nc.dram_tensor("x", [T, D], F32, kind="ExternalInput").ap()
    d_xT = nc.dram_tensor("xT", [D, T], F32, kind="ExternalInput").ap()
    env["d_wq1"] = nc.dram_tensor("wq1", [8, 128, 1024], F32, kind="ExternalInput").ap()
    env["d_wk1"] = nc.dram_tensor("wk1", [8, 128, 1024], F32, kind="ExternalInput").ap()
    env["d_wv1"] = nc.dram_tensor("wv1", [8, 128, 1024], F32, kind="ExternalInput").ap()
    env["d_wo1T"] = nc.dram_tensor("wo1T", [D, D], F32, kind="ExternalInput").ap()
    env["d_cmask"] = nc.dram_tensor("cmaskc", [4, 128, 512], BF16, kind="ExternalInput").ap()
    env["d_onescol"] = nc.dram_tensor("onescolc", [128, 1], BF16, kind="ExternalInput").ap()
    dcst = declare_consts(nc)
    out = nc.dram_tensor("xc", [T, D], F32, kind="ExternalOutput").ap()

    with ExitStack() as ctx:
        XT = ctx.enter_context(nc.sbuf_tensor("XT", [128, NT * 1024], F32))
        XFMB = ctx.enter_context(nc.sbuf_tensor("XFMB", [128, ND * 2048], BF16))
        cst = {nm: ctx.enter_context(nc.sbuf_tensor("s_" + nm, *CST_SHAPES[nm]))
               for nm in CST_ALL}
        ps = alloc_psum(ctx, nc)
        env.update(XT=XT, XFMB=XFMB, cst=cst, ps=ps)

        pl = Prog(nc)
        p = Prog(nc)
        p._sem("w2")
        pl.sems["w2"] = p.sems["w2"]
        pl.cnt["w2"] = 0

        def xloads(eng):
            for ci in range(NT):
                eng.dma_start(XT[:, ci * 1024:(ci + 1) * 1024],
                              d_x[ci * 128:(ci + 1) * 128, :]).then_inc(p.sems["w2"], 16)
            for nm in CST_ALL:
                eng.dma_start(cst[nm][:], dcst[nm]).then_inc(p.sems["w2"], 16)
            return None
        pl.emit("sync", xloads)

        def xloads2(eng):
            for dc in range(ND):
                eng.dma_start(XFMB[:, dc * 2048:(dc + 1) * 2048],
                              d_xT[dc * 128:(dc + 1) * 128, :]).then_inc(p.sems["w2"], 16)
            return None
        pl.emit("gpsimd", xloads2)
        nw2 = (NT + len(CST_ALL) + ND) * 16
        p.cnt["w2"] += nw2

        def w2wait(eng):
            eng.wait_ge(p.sems["w2"], nw2)
            return None
        pl.emit("gpsimd", w2wait)
        env["xload_tok"] = None

        ch = stage_c(p, ctx, env)

        def stores(eng):
            for ci in range(NT):
                eng.dma_start(out[ci * 128:(ci + 1) * 128, :],
                              XT[:, ci * 1024:(ci + 1) * 1024]).then_inc(p.sems["st"], 16)
            eng.wait_ge(p.sems["st"], NT * 16)
            return None
        p._sem("st")
        p.emit("sync", stores, waits=[ch.prev])
        p.cnt["st"] += NT * 16

        pl.play()
        p.play()
    return nc


def make_stage_c_inmaps(inputs, xs):
    com, _ = prep_host(inputs)
    maps = []
    for b in range(8):
        m = {"x": np.ascontiguousarray(xs[b]), "xT": np.ascontiguousarray(xs[b].T),
             "wq1": com["wq1"], "wk1": com["wk1"], "wv1": com["wv1"],
             "wo1T": com["wo1T"], "cmaskc": com["cmask"], "onescolc": com["onescol_b"]}
        for nm in CST_ALL:
            m["c_" + nm] = com[nm]
        maps.append(m)
    return maps


def stage_d(p, ctx, env, final_store=None):
    nc = p.nc
    ps = env["ps"]
    XT, XFMB = env["XT"], env["XFMB"]
    cst = env["cst"]
    sb = lambda name, shape, dt: ctx.enter_context(nc.sbuf_tensor(name, shape, dt))

    rw = sb("d_rw", [128, ND * 8], F32)
    rb = sb("d_rb", [1, 8], F32)
    wgb = [sb("d_wgb%d" % i, [128, 1024], BF16) for i in range(4)]
    wub = [sb("d_wub%d" % i, [128, 1024], BF16) for i in range(4)]
    wdh = [sb("d_wdh%d" % i, [128, NF1 * 256], BF16) for i in range(2)]
    hall = sb("d_hall", [128, NF1 * 512], BF16)
    sg = sb("d_sg", [128, 512], BF16)
    acc = sb("d_acc", [128, NT * 1024], BF16)
    xtr = sb("d_xtr", [128, 1024], F32)
    lgt = sb("d_lgt", [128, 8], F32)
    l2t = sb("d_l2t", [128, 8], F32)
    msk1 = sb("d_msk1", [128, 8], F32)
    msk2 = sb("d_msk2", [128, 8], F32)
    t8 = sb("d_t8", [128, 8], F32)
    ce = sb("d_ce", [128, NT * 8], F32)
    m1c = sb("d_m1c", [128, 1], F32)
    m2c = sb("d_m2c", [128, 1], F32)
    ec = sb("d_ec", [128, 1], F32)
    w1c = sb("d_w1c", [128, 1], F32)
    w2c = sb("d_w2c", [128, 1], F32)
    zres = sb("d_zres", [128, 1024], F32)
    lnt = alloc_lnt(ctx, nc, "d", junk=xtr[:])

    ch = Chain(p)
    ch.prev = env.get("start_tok")

    p._sem("w")
    def loads(eng):
        eng.dma_start(rw[:], env["d_rw1"]).then_inc(p.sems["w"], 16)
        eng.dma_start(rb[:], env["d_rb1"]).then_inc(p.sems["w"], 16)
        return None
    p.emit("sync", loads, waits=[env["xload_tok"]] if env.get("xload_tok") else [])
    p.cnt["w"] += 32
    wtok0 = ("w", p.val("w"))
    if ch.prev is None:
        ch.prev = wtok0

    def lgt_cb(ci):
        ch.step("vector", lambda e: e.tensor_copy(lgt[:], ps[5][:, 0:8]),
                extra_waits=[wtok0] if ci == 0 else [])
        ch.step("vector", lambda e: e.reduce_max(m1c[:], lgt[:], axis=AX.X))
        ch.step("vector", lambda e: e.tensor_scalar(msk1[:], lgt[:], m1c[:], None,
                                                    ALU.is_equal))
        ch.step("vector", lambda e: e.scalar_tensor_tensor(l2t[:], msk1[:], -1e30,
                                                           lgt[:], ALU.mult, ALU.add))
        ch.step("vector", lambda e: e.reduce_max(m2c[:], l2t[:], axis=AX.X))
        ch.step("vector", lambda e: e.tensor_scalar(msk2[:], l2t[:], m2c[:], None,
                                                    ALU.is_equal))
        ch.step("vector", lambda e: e.tensor_sub(ec[:], m2c[:], m1c[:]))
        ch.step("scalar", lambda e: e.activation(ec[:], ec[:], AF.Exp))
        ch.step("vector", lambda e: e.tensor_scalar_add(ec[:], ec[:], 1.0))
        ch.step("vector", lambda e: e.reciprocal(w1c[:], ec[:]))
        ch.step("vector", lambda e: e.tensor_scalar(w2c[:], w1c[:], -1.0, 1.0,
                                                    ALU.mult, ALU.add))
        ch.step("vector", lambda e: e.tensor_scalar(t8[:], msk1[:], w1c[:], None, ALU.mult))
        ch.step("vector", lambda e, ci=ci: e.scalar_tensor_tensor(
            ce[:, ci * 8:(ci + 1) * 8], msk2[:], w2c[:], t8[:], ALU.mult, ALU.add))
    emit_router(p, ch, env, {"xtr": xtr}, 8, rw, rb, lgt_cb)

    p._sem("pl"); p._sem("pe"); p._sem("ac"); p._sem("dv")
    wtok = {}
    sg2 = [sg, ctx.enter_context(nc.sbuf_tensor("d_sg2", [128, 512], BF16))]
    tmp2 = sg2

    def emit_gload(key, e_i, fc, wait_tok):
        bi = fc % 4
        def wload(eng, e_i=e_i, fc=fc, bi=bi):
            eng.dma_start(wgb[bi][:], env["d_wg1"][e_i, fc]).then_inc(p.sems["pl"], 16)
            eng.dma_start(wub[bi][:], env["d_wu1"][e_i, fc]).then_inc(p.sems["pl"], 16)
            return None
        p.emit("gpsimd", wload, waits=[wait_tok] if wait_tok else [])
        p.cnt["pl"] += 32
        wtok[key] = ("pl", p.val("pl"))

    def emit_dload(key, e_i, qt, par, wait_tok):
        def wload(eng, e_i=e_i, qt=qt, par=par):
            for fc in range(NF1):
                eng.dma_start(wdh[par][:, fc * 256:(fc + 1) * 256],
                              env["d_wd1"][e_i, fc, :, qt * 256:(qt + 1) * 256]
                              ).then_inc(p.sems["pl"], 16)
            return None
        p.emit("gpsimd", wload, waits=[wait_tok] if wait_tok else [])
        p.cnt["pl"] += NF1 * 16
        wtok[key] = ("pl", p.val("pl"))

    start_tok = ch.prev
    stage_start = env.get("start_tok")
    tkm = p.emit("vector", lambda e: e.memset(acc[:], 0.0),
                 waits=[start_tok], incs=[("dv", 1)])
    acc_init = ("dv", tkm["dv"])
    acc_guard = {}
    for ti16 in range(16):
        for half in range(2):
            acc_guard[(ti16, half)] = acc_init

    bank_guard = [start_tok, start_tok]
    y_guard = [start_tok] * 4
    tmp_guard = [None, None]
    hall_guard = None
    dbuf_guard = [stage_start, stage_start]
    gbuf_guard = [stage_start, stage_start, stage_start, stage_start]

    for e_i in range(NE1):
        for ss in range(4):
            t0 = ss * 512
            for ff in range(4):
                emit_gload(("g", e_i, ss, ff), e_i, ff, gbuf_guard[ff])
            emit_dload(("d", e_i, ss, 0), e_i, 0, 0, dbuf_guard[0])
            evac_tok = {}
            for fc in range(NF1):
                par = fc % 2
                psg, psu = (ps[0], ps[1]) if par == 0 else (ps[2], ps[3])
                bi = fc % 4
                def gumm(eng, bi=bi, psg=psg, psu=psu, t0=t0):
                    for dc in range(ND):
                        eng.matmul(psg[:], wgb[bi][:, dc * 128:(dc + 1) * 128],
                                   XFMB[:, dc * 2048 + t0: dc * 2048 + t0 + 512],
                                   start=(dc == 0), stop=(dc == ND - 1))
                    last = None
                    for dc in range(ND):
                        last = eng.matmul(psu[:], wub[bi][:, dc * 128:(dc + 1) * 128],
                                          XFMB[:, dc * 2048 + t0: dc * 2048 + t0 + 512],
                                          start=(dc == 0), stop=(dc == ND - 1))
                    return last
                waits = [wtok[("g", e_i, ss, fc)], bank_guard[par]]
                tk = p.emit("tensor", gumm, waits=waits, incs=[("pe", 1)])
                mm_tok = ("pe", tk["pe"])
                gbuf_guard[bi] = mm_tok
                swaits = [mm_tok]
                if tmp_guard[par] is not None:
                    swaits.append(tmp_guard[par])
                tk = p.emit("scalar",
                            lambda e, psg=psg, par=par: e.activation(
                                sg2[par][:], psg[:], AF.Silu),
                            waits=swaits, incs=[("ac", 1)])
                silu_tok = ("ac", tk["ac"])
                mwaits = [silu_tok]
                if hall_guard is not None:
                    mwaits.append(hall_guard)
                tk = p.emit("vector",
                            lambda e, fc=fc, psu=psu, par=par: e.tensor_mul(
                                hall[:, fc * 512:(fc + 1) * 512], sg2[par][:], psu[:]),
                            waits=mwaits, incs=[("dv", 1)])
                evac_tok[fc] = ("dv", tk["dv"])
                bank_guard[par] = evac_tok[fc]
                if fc + 4 < NF1:
                    emit_gload(("g", e_i, ss, fc + 4), e_i, fc + 4, gbuf_guard[(fc + 4) % 4])
            h_done = evac_tok[NF1 - 1]
            for qt in range(4):
                dpar = qt % 2
                if qt + 1 < 4:
                    emit_dload(("d", e_i, ss, qt + 1), e_i, qt + 1, (qt + 1) % 2,
                               dbuf_guard[(qt + 1) % 2])
                last_dmm = None
                for fc in range(NF1):
                    def dmm(eng, fc=fc, dpar=dpar):
                        last = None
                        for sub in range(4):
                            last = eng.matmul(
                                ps[2 + sub][:, 0:256],
                                hall[:, fc * 512 + sub * 128: fc * 512 + (sub + 1) * 128],
                                wdh[dpar][:, fc * 256:(fc + 1) * 256],
                                start=(fc == 0), stop=(fc == NF1 - 1))
                        return last
                    waits = []
                    if fc == 0:
                        waits = [wtok[("d", e_i, ss, qt)], h_done]
                        waits += [y_guard[j] for j in range(4)]
                    tk = p.emit("tensor", dmm, waits=waits, incs=[("pe", 1)])
                    last_dmm = ("pe", tk["pe"])
                dbuf_guard[dpar] = last_dmm
                hall_guard = last_dmm
                for sub in range(4):
                    ti = ss * 4 + sub
                    tpar = sub % 2
                    awaits = [last_dmm]
                    if tmp_guard[tpar] is not None:
                        awaits.append(tmp_guard[tpar])
                    tk = p.emit("scalar",
                                lambda e, ti=ti, e_i=e_i, sub=sub, tpar=tpar: e.activation(
                                    tmp2[tpar][:, 0:256], ps[2 + sub][:, 0:256], AF.Copy,
                                    scale=ce[:, ti * 8 + e_i: ti * 8 + e_i + 1]),
                                waits=awaits, incs=[("ac", 1)])
                    ac_tok = ("ac", tk["ac"])
                    y_guard[sub] = ac_tok
                    tk = p.emit("vector",
                                lambda e, ti=ti, qt=qt, tpar=tpar: e.tensor_add(
                                    acc[:, ti * 1024 + qt * 256: ti * 1024 + (qt + 1) * 256],
                                    acc[:, ti * 1024 + qt * 256: ti * 1024 + (qt + 1) * 256],
                                    tmp2[tpar][:, 0:256]),
                                waits=[ac_tok, acc_guard[(ti, qt % 2)]], incs=[("dv", 1)])
                    acc_guard[(ti, qt % 2)] = ("dv", tk["dv"])
                    tmp_guard[tpar] = acc_guard[(ti, qt % 2)]
            bank_guard[1] = ("ac", p.val("ac"))

    ch.prev = ("dv", p.val("dv"))
    for ti in range(NT):
        ch.step("vector", lambda e, ti=ti: e.tensor_add(
            zres[:], acc[:, ti * 1024:(ti + 1) * 1024],
            XT[:, ti * 1024:(ti + 1) * 1024]))
        emit_ln(ch, env, lnt, zres[:], XT[:, ti * 1024:(ti + 1) * 1024])
        if final_store is not None:
            final_store(ch, ti)
    return ch


def build_stage_d_program():
    nc = bass.Bass("TRN2", target_bir_lowering=False)
    env = {}
    d_x = nc.dram_tensor("x", [T, D], F32, kind="ExternalInput").ap()
    d_xT = nc.dram_tensor("xT", [D, T], F32, kind="ExternalInput").ap()
    env["d_rw1"] = nc.dram_tensor("rw1", [128, ND * 8], F32, kind="ExternalInput").ap()
    env["d_rb1"] = nc.dram_tensor("rb1", [1, 8], F32, kind="ExternalInput").ap()
    env["d_wg1"] = nc.dram_tensor("wg1", [NE1, NF1, 128, 1024], F32, kind="ExternalInput").ap()
    env["d_wu1"] = nc.dram_tensor("wu1", [NE1, NF1, 128, 1024], F32, kind="ExternalInput").ap()
    env["d_wd1"] = nc.dram_tensor("wd1", [NE1, NF1, 128, 1024], F32, kind="ExternalInput").ap()
    dcst = declare_consts(nc)
    out = nc.dram_tensor("xd", [T, D], F32, kind="ExternalOutput").ap()

    with ExitStack() as ctx:
        XT = ctx.enter_context(nc.sbuf_tensor("XT", [128, NT * 1024], F32))
        XFMB = ctx.enter_context(nc.sbuf_tensor("XFMB", [128, ND * 2048], BF16))
        cst = {nm: ctx.enter_context(nc.sbuf_tensor("s_" + nm, *CST_SHAPES[nm]))
               for nm in CST_ALL}
        ps = alloc_psum(ctx, nc)
        env.update(XT=XT, XFMB=XFMB, cst=cst, ps=ps)

        pl = Prog(nc)
        p = Prog(nc)
        p._sem("w2")
        pl.sems["w2"] = p.sems["w2"]
        pl.cnt["w2"] = 0

        def xloads(eng):
            for ci in range(NT):
                eng.dma_start(XT[:, ci * 1024:(ci + 1) * 1024],
                              d_x[ci * 128:(ci + 1) * 128, :]).then_inc(p.sems["w2"], 16)
            for nm in CST_ALL:
                eng.dma_start(cst[nm][:], dcst[nm]).then_inc(p.sems["w2"], 16)
            return None
        pl.emit("sync", xloads)

        def xloads2(eng):
            for dc in range(ND):
                eng.dma_start(XFMB[:, dc * 2048:(dc + 1) * 2048],
                              d_xT[dc * 128:(dc + 1) * 128, :]).then_inc(p.sems["w2"], 16)
            return None
        pl.emit("gpsimd", xloads2)
        nw2 = (NT + len(CST_ALL) + ND) * 16
        p.cnt["w2"] += nw2

        def w2wait(eng):
            eng.wait_ge(p.sems["w2"], nw2)
            return None
        pl.emit("gpsimd", w2wait)
        env["xload_tok"] = None

        p._sem("st")
        def mk_store(ch, ci):
            def st_fn(eng, ci=ci):
                return eng.dma_start(out[ci * 128:(ci + 1) * 128, :],
                                     XT[:, ci * 1024:(ci + 1) * 1024])
            tk = p.emit("sync", st_fn, waits=[ch.prev], incs=[("st", 16)])
        ch = stage_d(p, ctx, env, final_store=mk_store)

        def stwait(eng):
            eng.wait_ge(p.sems["st"], NT * 16)
            return None
        p.emit("sync", stwait, waits=[ch.prev])

        pl.play()
        p.play()
    return nc


def make_stage_d_inmaps(inputs, xs):
    com, _ = prep_host(inputs)
    maps = []
    for b in range(8):
        m = {"x": np.ascontiguousarray(xs[b]), "xT": np.ascontiguousarray(xs[b].T),
             "rw1": com["rw1"], "rb1": com["rb1"],
             "wg1": com["wg1"], "wu1": com["wu1"], "wd1": com["wd1"]}
        for nm in CST_ALL:
            m["c_" + nm] = com[nm]
        maps.append(m)
    return maps


DBG_TAPS = {
    "qtm": ([128, 128], "f"), "kt": ([128, 128], "f"), "vt": ([128, 128], "f"),
    "gtm": ([128, 128], "f"), "lar": ([128, 2], "f"), "beta": ([128, 1], "f"),
    "lcolr": ([128, 1], "f"), "ccol": ([128, 1], "f"),
    "ELl": ([128, 128], "f"), "ELt": ([128, 128], "f"), "Ast": ([128, 128], "f"),
    "Zt": ([128, 128], "f"), "Bst": ([128, 128], "f"), "Wtm": ([128, 128], "f"),
    "Pt": ([128, 128], "f"), "Otm": ([128, 128], "f"), "Sst": ([128, 128], "f"),
    "Rst": ([128, 128], "f"), "zres": ([128, 1024], "f"),
    "ps0q": ([128, 128], "f"), "xfm0": ([128, 128], "f"), "wq0": ([128, 128], "f"),
    "colsdump": ([128, 24], "f"),
}


def build_stage_a_program(debug=False, truncate=None):
    nc = bass.Bass("TRN2", target_bir_lowering=False)
    env = {}
    d = {}
    d["x"] = nc.dram_tensor("x", [T, D], F32, kind="ExternalInput").ap()
    d["xT"] = nc.dram_tensor("xT", [D, T], F32R, kind="ExternalInput").ap()
    env["d_wqkvg"] = nc.dram_tensor("wqkvg", [D, 512], F32R, kind="ExternalInput").ap()
    env["d_wab"] = nc.dram_tensor("wab", [D, 2], F32R, kind="ExternalInput").ap()
    env["d_wo0"] = nc.dram_tensor("wo0", [C0, D], F32, kind="ExternalInput").ap()
    env["d_bqkvg"] = nc.dram_tensor("bqkvg", [1, 512], F32R, kind="ExternalInput").ap()
    env["d_bab"] = nc.dram_tensor("bab", [1, 2], F32R, kind="ExternalInput").ap()
    env["d_bo0"] = nc.dram_tensor("bo0", [1, D], F32, kind="ExternalInput").ap()
    cst_names = [("ident", [128, 128], F32), ("ident_b", [128, 128], BF16),
                 ("twoI", [128, 128], F32), ("trilcum", [128, 128], F32),
                 ("smask", [128, 128], F32), ("umask", [128, 128], F32),
                 ("ones1x512", [1, 128], F32R), ("ones1x512f", [1, 128], F32),
                 ("ones1b", [1, 128], BF16), ("eyeI", [128, 128], F32)]
    dcst = {nm: nc.dram_tensor("c_" + nm, shape, dt, kind="ExternalInput").ap()
            for nm, shape, dt in cst_names}
    out = nc.dram_tensor("xa", [T, D], F32, kind="ExternalOutput").ap()

    with ExitStack() as ctx:
        XT = ctx.enter_context(nc.sbuf_tensor("XT", [128, NT * 1024], F32))
        XFMR = ctx.enter_context(nc.sbuf_tensor("XFMR", [128, ND * 2048], F32R))
        cst = {nm: ctx.enter_context(nc.sbuf_tensor("s_" + nm, shape, dt))
               for nm, shape, dt in cst_names}
        ps = alloc_psum(ctx, nc)
        env.update(XT=XT, XFMR=XFMR, cst=cst, ps=ps)

        pl = Prog(nc)
        p = Prog(nc)
        def xloads(eng):
            last = None
            for ci in range(NT):
                last = eng.dma_start(XT[:, ci * 1024:(ci + 1) * 1024],
                                     d["x"][ci * 128:(ci + 1) * 128, :])
                last.then_inc(p.sems["w2"], 16)
            for dc in range(ND):
                last = eng.dma_start(XFMR[:, dc * 2048:(dc + 1) * 2048],
                                     d["xT"][dc * 128:(dc + 1) * 128, :])
                last.then_inc(p.sems["w2"], 16)
            for nm, _s, _d in cst_names:
                last = eng.dma_start(cst[nm][:], dcst[nm])
                last.then_inc(p.sems["w2"], 16)
            return None
        p._sem("w2")
        pl.sems["w2"] = p.sems["w2"]
        pl.cnt["w2"] = 0
        pl.emit("gpsimd", xloads)
        nw2 = (NT + ND + len(cst_names)) * 16
        p.cnt["w2"] += nw2
        def w2wait(eng):
            eng.wait_ge(pl.sems["w2"], nw2)
            return None
        pl.emit("gpsimd", w2wait)
        env["xload_tok"] = ("w2", p.val("w2"))
        env["pload"] = pl

        env["truncate"] = truncate
        dbg = None
        if debug:
            dbg = {nm: nc.dram_tensor("dbg_" + nm, shape,
                                      F32R if k == "r" else F32,
                                      kind="ExternalOutput").ap()
                   for nm, (shape, k) in DBG_TAPS.items()}
        ch = stage_a(p, ctx, env, dbg=dbg)

        def stores(eng):
            last = None
            for ci in range(NT):
                last = eng.dma_start(out[ci * 128:(ci + 1) * 128, :],
                                     XT[:, ci * 1024:(ci + 1) * 1024])
                last.then_inc(p.sems["st"], 16)
            eng.wait_ge(p.sems["st"], NT * 16)
            return None
        p._sem("st")
        p.emit("sync", stores, waits=[ch.prev])
        p.cnt["st"] += NT * 16

        pl.play()
        p.play()
    return nc


def make_stage_a_inmaps(inputs):
    com, percore = prep_host(inputs)
    maps = []
    for b in range(8):
        m = {"x": percore[b]["x"], "xT": percore[b]["xT"],
             "wqkvg": com["wqkvg"], "wab": com["wab"], "wo0": com["wo0"],
             "bqkvg": com["bqkvg"], "bab": com["bab"], "bo0": com["bo0"]}
        for nm in ("ident", "ident_b", "twoI", "trilcum", "smask", "umask",
                   "ones1x512", "ones1x512f", "ones1b", "eyeI"):
            m["c_" + nm] = com[nm]
        maps.append(m)
    return maps


CST_ALL = ("ident", "ident_b", "twoI", "trilcum", "smask", "umask",
           "ones1x512", "ones1x512f", "ones1b", "eyeI")
CST_SHAPES = {"ident": ([128, 128], F32), "ident_b": ([128, 128], BF16),
              "twoI": ([128, 128], F32), "trilcum": ([128, 128], F32),
              "smask": ([128, 128], F32), "umask": ([128, 128], F32),
              "ones1x512": ([1, 128], F32R), "ones1x512f": ([1, 128], F32),
              "ones1b": ([1, 128], BF16), "eyeI": ([128, 128], F32)}


def declare_consts(nc):
    return {nm: nc.dram_tensor("c_" + nm, CST_SHAPES[nm][0], CST_SHAPES[nm][1],
                               kind="ExternalInput").ap() for nm in CST_ALL}


def build_stage_b_program(debug=False):
    nc = bass.Bass("TRN2", target_bir_lowering=False)
    env = {}
    d_x = nc.dram_tensor("x", [T, D], F32, kind="ExternalInput").ap()
    d_xT = nc.dram_tensor("xT", [D, T], F32, kind="ExternalInput").ap()
    env["d_rwT"] = nc.dram_tensor("rwT", [128, ND * 32], F32, kind="ExternalInput").ap()
    env["d_rb"] = nc.dram_tensor("rb", [1, 32], F32, kind="ExternalInput").ap()
    env["d_b1c"] = nc.dram_tensor("b1c", [128, 64], F32, kind="ExternalInput").ap()
    env["d_w1"] = nc.dram_tensor("w1", [NE0, 128, 8 * H0], F32, kind="ExternalInput").ap()
    env["d_w2"] = nc.dram_tensor("w2", [NE0, 128, 2 * D], F32, kind="ExternalInput").ap()
    env["d_b2"] = nc.dram_tensor("b2", [NE0, D], F32, kind="ExternalInput").ap()
    dcst = declare_consts(nc)
    out = nc.dram_tensor("xb", [T, D], F32, kind="ExternalOutput").ap()
    dbgb = None
    if debug:
        dbgb = {"msk": nc.dram_tensor("dbg_msk", [128, NT * 32], F32, kind="ExternalOutput").ap(),
                "lgt0": nc.dram_tensor("dbg_lgt0", [128, 32], F32, kind="ExternalOutput").ap(),
                "hfm0": nc.dram_tensor("dbg_hfm0", [128, 1024], BF16, kind="ExternalOutput").ap(),
                "xfmb0": nc.dram_tensor("dbg_xfmb0", [128, 512], BF16, kind="ExternalOutput").ap(),
                "w1b0": nc.dram_tensor("dbg_w1b0", [128, 2048], BF16, kind="ExternalOutput").ap(),
                "acc0": nc.dram_tensor("dbg_acc0", [128, 4096], F32, kind="ExternalOutput").ap()}

    with ExitStack() as ctx:
        XT = ctx.enter_context(nc.sbuf_tensor("XT", [128, NT * 1024], F32))
        XFMB = ctx.enter_context(nc.sbuf_tensor("XFMB", [128, ND * 2048], BF16))
        cst = {nm: ctx.enter_context(nc.sbuf_tensor("s_" + nm, *CST_SHAPES[nm]))
               for nm in CST_ALL}
        ps = alloc_psum(ctx, nc)
        env.update(XT=XT, XFMB=XFMB, cst=cst, ps=ps, dbgb=dbgb)

        pl = Prog(nc)
        p = Prog(nc)
        p._sem("w2")
        pl.sems["w2"] = p.sems["w2"]
        pl.cnt["w2"] = 0

        def xloads(eng):
            for ci in range(NT):
                eng.dma_start(XT[:, ci * 1024:(ci + 1) * 1024],
                              d_x[ci * 128:(ci + 1) * 128, :]).then_inc(p.sems["w2"], 16)
            for nm in CST_ALL:
                eng.dma_start(cst[nm][:], dcst[nm]).then_inc(p.sems["w2"], 16)
            return None
        pl.emit("sync", xloads)

        def xloads2(eng):
            for dc in range(ND):
                eng.dma_start(XFMB[:, dc * 2048:(dc + 1) * 2048],
                              d_xT[dc * 128:(dc + 1) * 128, :]).then_inc(p.sems["w2"], 16)
            return None
        pl.emit("gpsimd", xloads2)
        nw2 = (NT + len(CST_ALL) + ND) * 16
        p.cnt["w2"] += nw2

        def w2wait(eng):
            eng.wait_ge(p.sems["w2"], nw2)
            return None
        pl.emit("gpsimd", w2wait)
        env["xload_tok"] = None

        ch = stage_b(p, ctx, env)

        def stores(eng):
            for ci in range(NT):
                eng.dma_start(out[ci * 128:(ci + 1) * 128, :],
                              XT[:, ci * 1024:(ci + 1) * 1024]).then_inc(p.sems["st"], 16)
            eng.wait_ge(p.sems["st"], NT * 16)
            return None
        p._sem("st")
        p.emit("sync", stores, waits=[ch.prev])
        p.cnt["st"] += NT * 16

        pl.play()
        p.play()
    return nc


def make_stage_b_inmaps(inputs, xs):
    com, _ = prep_host(inputs)
    maps = []
    for b in range(8):
        m = {"x": np.ascontiguousarray(xs[b]), "xT": np.ascontiguousarray(xs[b].T),
             "rwT": com["rwT"], "rb": com["rb"], "b1c": com["b1c"],
             "w1": com["w1"], "w2": com["w2"], "b2": com["b2"]}
        for nm in CST_ALL:
            m["c_" + nm] = com[nm]
        maps.append(m)
    return maps


def emit_refresh_xfmb(p, env, xtmb, prev_tok):
    ps, XT, XFMB, cst = env["ps"], env["XT"], env["XFMB"], env["cst"]
    ch = Chain(p)
    ch.prev = prev_tok
    for ci in range(NT):
        ch.step("vector", lambda e, ci=ci: e.tensor_copy(
            xtmb[:], XT[:, ci * 1024:(ci + 1) * 1024]))
        for dc in range(ND):
            ch.step("tensor", lambda e, dc=dc: e.transpose(
                ps["trb"][:, 0:128], xtmb[:, dc * 128:(dc + 1) * 128],
                cst["ident_b"][:]))
            ch.step("vector", lambda e, ci=ci, dc=dc: e.tensor_copy(
                XFMB[:, dc * 2048 + ci * 128: dc * 2048 + (ci + 1) * 128],
                ps["trb"][:, 0:128]))
    return ch.prev


def build_fused_program():
    nc = bass.Bass("TRN2", target_bir_lowering=False)
    env = {}
    d_x = nc.dram_tensor("x", [T, D], F32, kind="ExternalInput").ap()
    d_xT = nc.dram_tensor("xT", [D, T], F32R, kind="ExternalInput").ap()
    env["d_wqkvg"] = nc.dram_tensor("wqkvg", [D, 512], F32R, kind="ExternalInput").ap()
    env["d_wab"] = nc.dram_tensor("wab", [D, 2], F32R, kind="ExternalInput").ap()
    env["d_wo0"] = nc.dram_tensor("wo0", [C0, D], F32, kind="ExternalInput").ap()
    env["d_bqkvg"] = nc.dram_tensor("bqkvg", [1, 512], F32R, kind="ExternalInput").ap()
    env["d_bab"] = nc.dram_tensor("bab", [1, 2], F32R, kind="ExternalInput").ap()
    env["d_bo0"] = nc.dram_tensor("bo0", [1, D], F32, kind="ExternalInput").ap()
    env["d_rwT"] = nc.dram_tensor("rwT", [128, ND * 32], F32, kind="ExternalInput").ap()
    env["d_rb"] = nc.dram_tensor("rb", [1, 32], F32, kind="ExternalInput").ap()
    env["d_b1c"] = nc.dram_tensor("b1c", [128, 64], F32, kind="ExternalInput").ap()
    env["d_w1"] = nc.dram_tensor("w1", [NE0, 128, 8 * H0], F32, kind="ExternalInput").ap()
    env["d_w2"] = nc.dram_tensor("w2", [NE0, 128, 2 * D], F32, kind="ExternalInput").ap()
    env["d_b2"] = nc.dram_tensor("b2", [NE0, D], F32, kind="ExternalInput").ap()
    env["d_wq1"] = nc.dram_tensor("wq1", [8, 128, 1024], F32, kind="ExternalInput").ap()
    env["d_wk1"] = nc.dram_tensor("wk1", [8, 128, 1024], F32, kind="ExternalInput").ap()
    env["d_wv1"] = nc.dram_tensor("wv1", [8, 128, 1024], F32, kind="ExternalInput").ap()
    env["d_wo1T"] = nc.dram_tensor("wo1T", [D, D], F32, kind="ExternalInput").ap()
    env["d_cmask"] = nc.dram_tensor("cmaskc", [4, 128, 512], BF16, kind="ExternalInput").ap()
    env["d_onescol"] = nc.dram_tensor("onescolc", [128, 1], BF16, kind="ExternalInput").ap()
    env["d_rw1"] = nc.dram_tensor("rw1", [128, ND * 8], F32, kind="ExternalInput").ap()
    env["d_rb1"] = nc.dram_tensor("rb1", [1, 8], F32, kind="ExternalInput").ap()
    env["d_wg1"] = nc.dram_tensor("wg1", [NE1, NF1, 128, 1024], F32, kind="ExternalInput").ap()
    env["d_wu1"] = nc.dram_tensor("wu1", [NE1, NF1, 128, 1024], F32, kind="ExternalInput").ap()
    env["d_wd1"] = nc.dram_tensor("wd1", [NE1, NF1, 128, 1024], F32, kind="ExternalInput").ap()
    dcst = declare_consts(nc)
    out = nc.dram_tensor("out", [T, D], F32, kind="ExternalOutput").ap()

    with ExitStack() as ctx:
        XT = ctx.enter_context(nc.sbuf_tensor("XT", [128, NT * 1024], F32))
        xtmb = ctx.enter_context(nc.sbuf_tensor("xtmb", [128, 1024], BF16))
        cst = {nm: ctx.enter_context(nc.sbuf_tensor("s_" + nm, *CST_SHAPES[nm]))
               for nm in CST_ALL}
        ps = alloc_psum(ctx, nc)
        env.update(XT=XT, cst=cst, ps=ps)

        pl = Prog(nc)
        p = Prog(nc)
        p._sem("w2")
        pl.sems["w2"] = p.sems["w2"]
        pl.cnt["w2"] = 0

        with ExitStack() as ctx_a:
            XFMR = ctx_a.enter_context(nc.sbuf_tensor("XFMR", [128, ND * 2048], F32R))
            env["XFMR"] = XFMR

            def xloads(eng):
                for ci in range(NT):
                    eng.dma_start(XT[:, ci * 1024:(ci + 1) * 1024],
                                  d_x[ci * 128:(ci + 1) * 128, :]).then_inc(p.sems["w2"], 16)
                for dc in range(ND):
                    eng.dma_start(XFMR[:, dc * 2048:(dc + 1) * 2048],
                                  d_xT[dc * 128:(dc + 1) * 128, :]).then_inc(p.sems["w2"], 16)
                for nm in CST_ALL:
                    eng.dma_start(cst[nm][:], dcst[nm]).then_inc(p.sems["w2"], 16)
                return None
            pl.emit("gpsimd", xloads)
            nw2 = (NT + ND + len(CST_ALL)) * 16
            p.cnt["w2"] += nw2

            def w2wait(eng):
                eng.wait_ge(p.sems["w2"], nw2)
                return None
            pl.emit("gpsimd", w2wait)
            env["xload_tok"] = ("w2", p.val("w2"))
            env["pload"] = pl

            ch_a = stage_a(p, ctx_a, env)
            tok = ch_a.prev
        env.pop("pload", None)

        XFMB = ctx.enter_context(nc.sbuf_tensor("XFMB", [128, ND * 2048], BF16))
        env["XFMB"] = XFMB
        tok = emit_refresh_xfmb(p, env, xtmb, tok)
        with ExitStack() as ctx_b:
            env["start_tok"] = tok
            env["xload_tok"] = tok
            ch_b = stage_b(p, ctx_b, env)
            tok = ch_b.prev

        tok = emit_refresh_xfmb(p, env, xtmb, tok)
        with ExitStack() as ctx_c:
            env["start_tok"] = tok
            env["xload_tok"] = tok
            ch_c = stage_c(p, ctx_c, env)
            tok = ch_c.prev

        tok = emit_refresh_xfmb(p, env, xtmb, tok)
        with ExitStack() as ctx_d:
            env["start_tok"] = tok
            env["xload_tok"] = tok
            p._sem("st")

            def mk_store(ch, ci):
                def st_fn(eng, ci=ci):
                    return eng.dma_start(out[ci * 128:(ci + 1) * 128, :],
                                         XT[:, ci * 1024:(ci + 1) * 1024])
                p.emit("sync", st_fn, waits=[ch.prev], incs=[("st", 16)])
            ch_d = stage_d(p, ctx_d, env, final_store=mk_store)

            def stwait(eng):
                eng.wait_ge(p.sems["st"], NT * 16)
                return None
            p.emit("sync", stwait, waits=[ch_d.prev])

        pl.play()
        p.play()
    return nc


def make_fused_inmaps(inputs):
    com, percore = prep_host(inputs)
    names_a = ("wqkvg", "wab", "wo0", "bqkvg", "bab", "bo0")
    maps = []
    for b in range(8):
        m = {"x": percore[b]["x"], "xT": percore[b]["xT"]}
        for nm in names_a:
            m[nm] = com[nm]
        m.update(rwT=com["rwT"], rb=com["rb"], b1c=com["b1c"], w1=com["w1"],
                 w2=com["w2"], b2=com["b2"],
                 wq1=com["wq1"], wk1=com["wk1"], wv1=com["wv1"], wo1T=com["wo1T"],
                 cmaskc=com["cmask"], onescolc=com["onescol_b"],
                 rw1=com["rw1"], rb1=com["rb1"], wg1=com["wg1"], wu1=com["wu1"],
                 wd1=com["wd1"])
        for nm in CST_ALL:
            m["c_" + nm] = com[nm]
        maps.append(m)
    return maps


_PROGS = {}


def kernel_unfused(**inputs):
    if "A" not in _PROGS:
        _PROGS["A"] = build_stage_a_program()
        _PROGS["B"] = build_stage_b_program()
        _PROGS["C"] = build_stage_c_program()
        _PROGS["D"] = build_stage_d_program()
    cores = list(range(8))
    res = run_bass_kernel_spmd(_PROGS["A"], make_stage_a_inmaps(inputs), core_ids=cores)
    xa = np.stack([r["xa"] for r in res.results], axis=0)
    res = run_bass_kernel_spmd(_PROGS["B"], make_stage_b_inmaps(inputs, xa), core_ids=cores)
    xb = np.stack([r["xb"] for r in res.results], axis=0)
    res = run_bass_kernel_spmd(_PROGS["C"], make_stage_c_inmaps(inputs, xb), core_ids=cores)
    xc = np.stack([r["xc"] for r in res.results], axis=0)
    res = run_bass_kernel_spmd(_PROGS["D"], make_stage_d_inmaps(inputs, xc), core_ids=cores)
    return np.stack([r["xd"] for r in res.results], axis=0)


def kernel(**inputs):
    if "F" not in _PROGS:
        _PROGS["F"] = build_fused_program()
    res = run_bass_kernel_spmd(_PROGS["F"], make_fused_inmaps(inputs),
                               core_ids=list(range(8)))
    return np.stack([r["out"] for r in res.results], axis=0)


# version 11
# speedup vs baseline: 1.1636x; 1.0643x over previous
import numpy as np
import ml_dtypes
from contextlib import ExitStack

import concourse.bass as bass
import concourse.mybir as mybir
from concourse.bass_utils import run_bass_kernel_spmd

F32 = mybir.dt.float32
F32R = mybir.dt.float32r
BF16 = mybir.dt.bfloat16
AF = mybir.ActivationFunctionType
ALU = mybir.AluOpType
AX = mybir.AxisListType

T = 2048
D = 1024
NT = T // 128
ND = D // 128
C0 = 128
NE0 = 32
H0 = 256
NH = 16
KQ = 64
NE1 = 8
F1 = 2752
NF1 = 22

ENGSEM = {"sync": "sy", "tensor": "pe", "vector": "dv", "scalar": "ac", "gpsimd": "pl"}


class Prog:

    def __init__(self, nc):
        self.nc = nc
        self.ops = {e: [] for e in ("sync", "tensor", "vector", "scalar", "gpsimd")}
        self.cnt = {}
        self.sems = {}

    def _sem(self, name):
        if name not in self.sems:
            self.sems[name] = self.nc.alloc_semaphore(name)
            self.cnt[name] = 0
        return self.sems[name]

    def emit(self, eng, fn, waits=(), incs=()):
        toks = {}
        incs2 = []
        for s, n in incs:
            self._sem(s)
            self.cnt[s] += n
            incs2.append((s, n, self.cnt[s]))
            toks[s] = self.cnt[s]
        waits2 = []
        for s, v in waits:
            self._sem(s)
            waits2.append((s, v))
        self.ops[eng].append((fn, waits2, incs2))
        return toks

    def val(self, sem):
        self._sem(sem)
        return self.cnt[sem]

    def play(self):
        with self.nc.Block() as blk:
            for eng_name, reg in (("sync", blk.sync), ("tensor", blk.tensor),
                                  ("vector", blk.vector), ("scalar", blk.scalar),
                                  ("gpsimd", blk.gpsimd)):
                def body(eng, eng_name=eng_name):
                    for fn, waits, incs in self.ops[eng_name]:
                        for s, v in waits:
                            eng.wait_ge(self.sems[s], v)
                        instr = fn(eng)
                        for s, n, _v in incs:
                            if instr is not None:
                                instr.then_inc(self.sems[s], n)
                            else:
                                eng.sem_inc(self.sems[s], n)
                reg(body)


class Chain:

    def __init__(self, p):
        self.p = p
        self.prev = None

    def step(self, eng, fn, extra_waits=()):
        sem = ENGSEM[eng]
        waits = list(extra_waits)
        if self.prev is not None:
            waits.append(self.prev)
        toks = self.p.emit(eng, fn, waits, [(sem, 1)])
        self.prev = (sem, toks[sem])
        return (sem, toks[sem])

    def wait_also(self, tok):
        if tok is not None:
            self._extra = tok



def prep_host(inputs):
    f = np.float32
    com = {}
    x = np.ascontiguousarray(np.asarray(inputs["L0_attn_x"], f))

    com["wqkvg"] = np.ascontiguousarray(np.concatenate(
        [inputs["L0_attn_Wq"], inputs["L0_attn_Wk"], inputs["L0_attn_Wv"],
         inputs["L0_attn_Wg"]], axis=1).astype(f))
    com["wab"] = np.ascontiguousarray(np.concatenate(
        [inputs["L0_attn_Wa"], inputs["L0_attn_Wb"]], axis=1).astype(f))
    com["bqkvg"] = np.ascontiguousarray(np.concatenate(
        [inputs["L0_attn_bq"], inputs["L0_attn_bk"], inputs["L0_attn_bv"],
         inputs["L0_attn_bg"]]).astype(f)[None, :])
    com["bab"] = np.ascontiguousarray(np.concatenate(
        [inputs["L0_attn_ba"], inputs["L0_attn_bb"]]).astype(f)[None, :])
    com["wo0"] = np.ascontiguousarray(np.asarray(inputs["L0_attn_Wo"], f))
    com["bo0"] = np.ascontiguousarray(np.asarray(inputs["L0_attn_bo"], f)[None, :])

    ident = np.eye(128, dtype=f)
    com["ident"] = ident
    com["ident_b"] = ident.astype(ml_dtypes.bfloat16)
    com["twoI"] = (2.0 * ident)
    r = np.arange(128)
    com["trilcum"] = (r[None, :] >= r[:, None]).astype(f)
    com["smask"] = (r[None, :] < r[:, None]).astype(f)
    com["umask"] = (r[None, :] >= r[:, None]).astype(f)
    com["ones1x512"] = np.ones((1, 128), f)
    com["ones1x512f"] = np.ones((1, 128), f)
    com["ones1b"] = np.ones((1, 128), ml_dtypes.bfloat16)
    com["eyeI"] = ident
    com["epscol"] = np.full((128, 1), 1e-5, f)
    oc2 = np.zeros((128, 2), np.float32)
    oc2[0:64, 0] = 1.0
    oc2[64:128, 1] = 1.0
    com["onescol2"] = oc2.astype(ml_dtypes.bfloat16)
    s01 = np.zeros((2, 128), f)
    s01[1, :] = 1.0
    com["sel01"] = s01


    rwt = np.asarray(inputs["L0_ffn_router_w"], f).T
    com["rwT"] = np.ascontiguousarray(
        rwt.reshape(8, 128, 32).transpose(1, 0, 2).reshape(128, 8 * 32))
    com["rb"] = np.ascontiguousarray(np.asarray(inputs["L0_ffn_router_b"], f)[None, :])
    b1 = np.asarray(inputs["L0_ffn_bias1"], f)
    com["b1c"] = np.ascontiguousarray(b1.reshape(32, 2, 128).transpose(2, 0, 1).reshape(128, 64))
    w1 = np.asarray(inputs["L0_ffn_weights1"], f)
    com["w1"] = np.ascontiguousarray(
        w1.reshape(32, 8, 128, 256).transpose(0, 2, 1, 3).reshape(32, 128, 8 * 256))
    w2 = np.asarray(inputs["L0_ffn_weights2"], f)
    com["w2"] = np.ascontiguousarray(
        w2.reshape(32, 2, 128, 1024).transpose(0, 2, 1, 3).reshape(32, 128, 2 * 1024))
    com["b2"] = np.ascontiguousarray(np.asarray(inputs["L0_ffn_bias2"], f))

    def pairtile(w):
        w2 = np.ascontiguousarray(np.asarray(w, f).reshape(1024, 1024))
        return np.ascontiguousarray(
            w2.reshape(8, 128, 8, 128).transpose(2, 1, 0, 3).reshape(8, 128, 1024))
    com["wq1"] = pairtile(inputs["L1_attn_w_q"])
    com["wk1"] = pairtile(inputs["L1_attn_w_k"])
    com["wv1"] = pairtile(inputs["L1_attn_w_v"])
    com["wo1T"] = np.ascontiguousarray(
        np.asarray(inputs["L1_attn_w_o"], f).reshape(1024, 1024).T)
    r512 = np.arange(512)[None, :]
    m128 = np.arange(128)[:, None]
    com["cmask"] = np.ascontiguousarray(np.stack(
        [(r512 >= (m128 + o * 128)).astype(ml_dtypes.bfloat16) for o in range(4)]))
    com["onescol_b"] = np.ones((128, 1), ml_dtypes.bfloat16)

    com["rw1"] = np.ascontiguousarray(
        np.asarray(inputs["L1_ffn_Ws"], f).reshape(8, 128, 8).transpose(1, 0, 2).reshape(128, 64))
    com["rb1"] = np.ascontiguousarray(np.asarray(inputs["L1_ffn_bs"], f)[None, :])
    def ffntile(w):
        wp = np.zeros((8, 1024, NF1 * 128), f)
        wp[:, :, 0:2752] = np.asarray(w, f)
        return np.ascontiguousarray(
            wp.reshape(8, 8, 128, NF1, 128).transpose(0, 3, 2, 1, 4).reshape(8, NF1, 128, 1024))
    com["wg1"] = ffntile(inputs["L1_ffn_Wg"])
    com["wu1"] = ffntile(inputs["L1_ffn_Wu"])
    wd = np.zeros((8, NF1 * 128, 1024), f)
    wd[:, 0:2752, :] = np.asarray(inputs["L1_ffn_Wd"], f)
    com["wd1"] = np.ascontiguousarray(wd.reshape(8, NF1, 128, 1024))

    percore = []
    for b in range(8):
        d = {"x": np.ascontiguousarray(x[b]),
             "xT": np.ascontiguousarray(x[b].T)}
        percore.append(d)
    return com, percore



def alloc_psum(ctx, nc):
    ps = {}
    for i in range(6):
        ps[i] = ctx.enter_context(nc.psum_tensor("ps%d" % i, [128, 512], F32))
    ps["tr"] = ctx.enter_context(nc.psum_tensor("ps_tr", [128, 128], F32))
    ps["trb"] = ctx.enter_context(nc.psum_tensor("ps_trb", [128, 1024], BF16))
    return ps


def stage_a(p, ctx, env, dbg=None):
    nc = p.nc
    ps = env["ps"]
    XT, XFMR = env["XT"], env["XFMR"]
    cst = env["cst"]

    sb = lambda name, shape, dt: ctx.enter_context(nc.sbuf_tensor(name, shape, dt))
    wqkvg = sb("a_wqkvg", [128, 8 * 512], F32R)
    wab = sb("a_wab", [128, 8 * 2], F32R)
    wo = sb("a_wo", [128, 1024], BF16)
    bqkvg = sb("a_bqkvg", [1, 512], F32R)
    bab = sb("a_bab", [1, 2], F32R)
    bo = sb("a_bo", [1, 1024], BF16)
    def db(name, shape, dt):
        return [sb(name + "0", shape, dt), sb(name + "1", shape, dt)]
    qf = db("a_qf", [128, 128], F32)
    kf = db("a_kf", [128, 128], F32)
    kt = db("a_kt", [128, 128], F32)
    vt = db("a_vt", [128, 128], F32)
    gtm = db("a_gtm", [128, 128], F32)
    ELt = db("a_elt", [128, 128], F32)
    Zt = db("a_z", [128, 128], F32)
    colsb = db("a_cols", [128, 24], F32)
    lar2 = db("a_lar", [128, 2], F32)
    lrow2 = db("a_lrow", [1, 128], F32)
    lcol2 = db("a_lcolr", [128, 1], F32)
    qtm = sb("a_qtm", [128, 128], F32)
    scr = sb("a_scr", [128, 128], F32)
    ELl = sb("a_ell", [128, 128], F32)
    Ast = sb("a_ast", [128, 128], F32)
    ZTt = sb("a_zt", [128, 128], F32)
    Vn = sb("a_vn", [128, 128], F32)
    m1 = sb("a_m1", [128, 128], F32)
    Bst = sb("a_bst", [128, 128], F32)
    Wfm = sb("a_wfm", [128, 128], F32)
    Wtm = sb("a_wtm", [128, 128], F32)
    Wp = sb("a_wp", [128, 128], F32)
    ktw = sb("a_ktw", [128, 128], F32)
    Pt = sb("a_pt", [128, 128], F32)
    Otm = sb("a_otm", [128, 128], BF16)
    Ofm = sb("a_ofm", [128, 128], BF16)
    Sst = sb("a_s", [128, 128], F32)
    Rst = sb("a_r", [128, 128], F32)
    m2 = sb("a_m2", [128, 128], F32)
    zres = sb("a_zres", [128, 1024], F32)
    lnt = alloc_lnt(ctx, nc, "a")
    rstd = lnt["rstd"]

    p._sem("pe"); p._sem("ac"); p._sem("dv"); p._sem("w")

    def loads(eng):
        last = None
        for dc in range(ND):
            last = eng.dma_start(wqkvg[:, dc * 512:(dc + 1) * 512],
                                 env["d_wqkvg"][dc * 128:(dc + 1) * 128, :])
            last.then_inc(p.sems["w"], 16)
            last = eng.dma_start(wab[:, dc * 2:(dc + 1) * 2],
                                 env["d_wab"][dc * 128:(dc + 1) * 128, :])
            last.then_inc(p.sems["w"], 16)
        for nm, tile in (("d_bqkvg", bqkvg), ("d_bab", bab)):
            last = eng.dma_start(tile[:], env[nm])
            last.then_inc(p.sems["w"], 16)
        return None
    pload = env.get("pload", p)
    if pload is not p:
        pload.sems["w"] = p.sems["w"]
        pload.cnt["w"] = 0
    pload.emit("sync", loads, waits=(), incs=())
    def loads_cast(eng):
        eng.dma_start(wo[:], env["d_wo0"]).then_inc(p.sems["w"], 16)
        eng.dma_start(bo[:], env["d_bo0"]).then_inc(p.sems["w"], 16)
        return None
    pload.emit("gpsimd", loads_cast, waits=(), incs=())
    n_w = 20 * 16
    p.cnt["w"] += n_w
    if pload is not p:
        def wwait(eng):
            eng.wait_ge(p.sems["w"], n_w)
            return None
        pload.emit("sync", wwait)
    wtok = ("w", p.val("w"))
    ld_x = env["xload_tok"]

    tk = p.emit("vector", lambda e: e.tensor_scalar(Sst[:], cst["eyeI"][:], 0.0, None, ALU.mult),
                waits=[ld_x] if ld_x else [], incs=[("dv", 1)])
    tk = p.emit("vector", lambda e: e.tensor_scalar(Rst[:], cst["eyeI"][:], 0.0, None, ALU.mult),
                waits=(), incs=[("dv", 1)])
    tk = p.emit("vector", lambda e: e.tensor_copy(lar2[0][:, 1:2], cst["eyeI"][:, 0:1]),
                waits=(), incs=[("dv", 1)])
    tk = p.emit("vector", lambda e: e.tensor_copy(lar2[1][:, 1:2], cst["eyeI"][:, 0:1]),
                waits=(), incs=[("dv", 1)])
    init_tok = ("dv", tk["dv"])

    sec1_end = {}
    sec2_end = {-1: init_tok}

    def emit_sec1(ci):
        pr = ci % 2
        cols = colsb[pr]
        col = lambda i: cols[:, i:i + 1]
        lar, lrow, lcolr = lar2[pr], lrow2[pr], lcol2[pr]
        ch = Chain(p)
        w0 = [t for t in (sec2_end.get(ci - 2), sec1_end.get(ci - 1), wtok, ld_x) if t]
        tok0 = ci * 128
        xv = lambda dc, tok0=tok0: XFMR[:, dc * 2048 + tok0: dc * 2048 + tok0 + 128]

        def proj(eng, xv=xv):
            for dc in range(ND):
                eng.matmul(ps[0][:], xv(dc), wqkvg[:, dc * 512:(dc + 1) * 512],
                           start=(dc == 0), stop=False)
            return eng.matmul(ps[0][:], cst["ones1x512"][:1, 0:128], bqkvg[:],
                              start=False, stop=True)
        ch.step("tensor", proj, extra_waits=w0)
        def proj2(eng, xv=xv):
            for dc in range(ND):
                eng.matmul(ps[1][:, 0:2], xv(dc), wab[:, dc * 2:(dc + 1) * 2],
                           start=(dc == 0), stop=False)
            return eng.matmul(ps[1][:, 0:2], cst["ones1x512"][:1, 0:128], bab[:],
                              start=False, stop=True)
        ch.step("tensor", proj2)
        ch.step("vector", lambda e: e.tensor_copy(qtm[:], ps[0][:, 0:128]))
        ch.step("scalar", lambda e: e.activation(scr[:], ps[0][:, 128:256], AF.Square,
                                                 accum_out=col(0)))
        ch.step("scalar", lambda e: e.activation(col(1), col(0), AF.Ln))
        ch.step("scalar", lambda e: e.activation(col(2), col(1), AF.Exp, scale=-0.5))
        ch.step("vector", lambda e: e.tensor_scalar_min(col(2), col(2), 1e12))
        ch.step("vector", lambda e, pr=pr: e.tensor_scalar(kt[pr][:], ps[0][:, 128:256],
                                                           col(2), None, ALU.mult))
        ch.step("vector", lambda e, pr=pr: e.tensor_copy(vt[pr][:], ps[0][:, 256:384]))
        ch.step("scalar", lambda e: e.activation(gtm[pr][:], ps[0][:, 384:512], AF.Exp,
                                                 scale=-1.0) if False else
                e.activation(gtm[pr][:], ps[0][:, 384:512], AF.Exp, scale=-1.0))
        ch.step("vector", lambda e, pr=pr: e.tensor_scalar_add(gtm[pr][:], gtm[pr][:], 1.0))
        ch.step("vector", lambda e, pr=pr: e.reciprocal(gtm[pr][:], gtm[pr][:]))
        ch.step("scalar", lambda e: e.activation(cols[:, 4:6], ps[1][:, 0:2], AF.Abs))
        ch.step("scalar", lambda e: e.activation(cols[:, 6:8], cols[:, 4:6], AF.Exp, scale=-1.0))
        ch.step("vector", lambda e: e.tensor_scalar_add(cols[:, 6:8], cols[:, 6:8], 1.0))
        ch.step("scalar", lambda e: e.activation(cols[:, 8:10], cols[:, 6:8], AF.Ln))
        ch.step("scalar", lambda e: e.activation(cols[:, 4:6], ps[1][:, 0:2], AF.Relu, scale=-1.0))
        ch.step("vector", lambda e: e.tensor_add(cols[:, 8:10], cols[:, 8:10], cols[:, 4:6]))
        ch.step("vector", lambda e, lar=lar: e.tensor_scalar(lar[:, 0:1], cols[:, 8:9],
                                                             -1.0, None, ALU.mult))
        ch.step("scalar", lambda e: e.activation(col(3), cols[:, 9:10], AF.Exp, scale=-1.0))
        ch.step("tensor", lambda e: e.transpose(ps["tr"][:], qtm[:], cst["ident"][:]))
        ch.step("vector", lambda e, pr=pr: e.tensor_copy(qf[pr][:], ps["tr"][:]))
        ch.step("tensor", lambda e, pr=pr: e.transpose(ps["tr"][:], kt[pr][:], cst["ident"][:]))
        ch.step("vector", lambda e, pr=pr: e.tensor_copy(kf[pr][:], ps["tr"][:]))
        ch.step("tensor", lambda e, lar=lar: e.matmul(ps[2][:, 0:2], cst["trilcum"][:],
                                                      lar[:, 0:2], start=True, stop=True))
        ch.step("tensor", lambda e, lar=lar: e.matmul(ps[2][0:2, 4:132], lar[:, 0:2],
                                                      cst["trilcum"][:], start=True, stop=True))
        ch.step("vector", lambda e, lcolr=lcolr: e.tensor_copy(lcolr[:], ps[2][:, 0:1]))
        ch.step("scalar", lambda e: e.activation(col(11), ps[2][:, 0:1], AF.Exp))
        ch.step("vector", lambda e, lrow=lrow: e.tensor_copy(lrow[:], ps[2][0:1, 4:132]))
        ch.step("tensor", lambda e, lrow=lrow: e.matmul(ps[3][:, 0:128],
                                                        cst["ones1x512f"][:1, 0:128],
                                                        lrow[:], start=True, stop=True))
        ch.step("vector", lambda e, lcolr=lcolr: e.tensor_scalar(m1[:], ps[3][:, 0:128],
                                                                 lcolr[:], 0.0,
                                                                 ALU.subtract, ALU.max))
        ch.step("scalar", lambda e: e.activation(ELl[:], m1[:], AF.Exp, scale=-1.0))
        ch.step("vector", lambda e, lcolr=lcolr: e.tensor_scalar(m1[:], ps[3][:, 0:128],
                                                                 lcolr[:], 0.0,
                                                                 ALU.subtract, ALU.min))
        ch.step("scalar", lambda e, pr=pr: e.activation(ELt[pr][:], m1[:], AF.Exp))
        ch.step("tensor", lambda e, pr=pr: e.matmul(ps[4][:, 0:128], kf[pr][:], kf[pr][:],
                                                    start=True, stop=True))
        ch.step("vector", lambda e: e.tensor_mul(m1[:], ps[4][:, 0:128], ELl[:]))
        ch.step("vector", lambda e: e.tensor_scalar(m1[:], m1[:], col(3), None, ALU.mult))
        ch.step("vector", lambda e: e.tensor_mul(m1[:], m1[:], cst["smask"][:]))
        ch.step("vector", lambda e: e.tensor_add(Ast[:], m1[:], cst["eyeI"][:]))
        ch.step("vector", lambda e, pr=pr: e.tensor_copy(Zt[pr][:], cst["eyeI"][:]))
        ch.step("vector", lambda e: e.tensor_copy(ZTt[:], cst["eyeI"][:]))
        for it in range(7):
            ch.step("tensor", lambda e, pr=pr: e.matmul(ps[4][:, 128:256], Ast[:], Zt[pr][:],
                                                        start=True, stop=True))
            ch.step("vector", lambda e: e.scalar_tensor_tensor(
                Vn[:], ps[4][:, 128:256], -1.0, cst["twoI"][:], ALU.mult, ALU.add))
            if it < 6:
                def nz(eng):
                    eng.matmul(ps[4][:, 256:384], ZTt[:], Vn[:], start=True, stop=True)
                    return eng.matmul(ps[4][:, 384:512], Vn[:], ZTt[:], start=True, stop=True)
                ch.step("tensor", nz)
                ch.step("vector", lambda e, pr=pr: e.tensor_copy(Zt[pr][:], ps[4][:, 256:384]))
                ch.step("vector", lambda e: e.tensor_copy(ZTt[:], ps[4][:, 384:512]))
            else:
                ch.step("tensor", lambda e: e.matmul(ps[4][:, 256:384], ZTt[:], Vn[:],
                                                     start=True, stop=True))
                ch.step("vector", lambda e, pr=pr: e.tensor_copy(Zt[pr][:], ps[4][:, 256:384]))
        sec1_end[ci] = ch.prev

    def emit_sec2(ci):
        pr = ci % 2
        cols = colsb[pr]
        col = lambda i: cols[:, i:i + 1]
        lrow, lcolr = lrow2[pr], lcol2[pr]
        ch = Chain(p)
        w0 = [t for t in (sec1_end[ci], sec2_end.get(ci - 1)) if t]
        ch.step("tensor", lambda e, pr=pr: e.matmul(ps[5][:, 0:128], kf[pr][:], Rst[:],
                                                    start=True, stop=True), extra_waits=w0)
        ch.step("vector", lambda e: e.tensor_scalar(col(13), col(3), col(11), -1.0,
                                                    ALU.mult, ALU.mult))
        ch.step("vector", lambda e, pr=pr: e.tensor_scalar(m2[:], vt[pr][:], col(3),
                                                           None, ALU.mult))
        ch.step("vector", lambda e: e.scalar_tensor_tensor(
            Bst[:], ps[5][:, 0:128], col(13), m2[:], ALU.mult, ALU.add))
        def wmm(eng, pr=pr):
            eng.matmul(ps[5][:, 128:256], Bst[:], Zt[pr][:], start=True, stop=True)
            return eng.matmul(ps[5][:, 256:384], Zt[pr][:], Bst[:], start=True, stop=True)
        ch.step("tensor", wmm)
        ch.step("vector", lambda e: e.tensor_copy(Wfm[:], ps[5][:, 128:256]))
        ch.step("vector", lambda e: e.tensor_copy(Wtm[:], ps[5][:, 256:384]))
        def qmm(eng, pr=pr):
            eng.matmul(ps[5][:, 0:128], qf[pr][:], Sst[:], start=True, stop=True)
            return eng.matmul(ps[5][:, 384:512], Wfm[:], qf[pr][:], start=True, stop=True)
        ch.step("tensor", qmm)
        ch.step("vector", lambda e, pr=pr: e.tensor_mul(m2[:], ps[5][:, 384:512], ELt[pr][:]))
        ch.step("vector", lambda e: e.tensor_mul(Pt[:], m2[:], cst["umask"][:]))
        ch.step("vector", lambda e: e.tensor_scalar(m2[:], ps[5][:, 0:128], col(11),
                                                    None, ALU.mult))
        ch.step("tensor", lambda e, pr=pr: e.matmul(ps[5][:, 128:256], Pt[:], kt[pr][:],
                                                    start=True, stop=True))
        ch.step("vector", lambda e: e.tensor_add(m2[:], m2[:], ps[5][:, 128:256]))
        ch.step("vector", lambda e, pr=pr: e.tensor_mul(Otm[:], m2[:], gtm[pr][:]))
        ch.step("tensor", lambda e, lrow=lrow: e.matmul(ps[5][:, 256:258],
                                                        cst["ones1x512f"][:1, 0:128],
                                                        lrow[:1, 126:128],
                                                        start=True, stop=True))
        ch.step("vector", lambda e, lcolr=lcolr: e.tensor_scalar(col(15), ps[5][:, 257:258],
                                                                 lcolr[:], None, ALU.subtract))
        ch.step("scalar", lambda e: e.activation(col(16), col(15), AF.Exp))
        ch.step("scalar", lambda e: e.activation(col(17), ps[5][:, 257:258], AF.Exp))
        ch.step("vector", lambda e: e.tensor_scalar(Wp[:], Wtm[:], col(16), None, ALU.mult))
        ch.step("vector", lambda e, pr=pr: e.tensor_scalar(ktw[:], kt[pr][:], col(16),
                                                           None, ALU.mult))
        def smm(eng, pr=pr):
            eng.matmul(ps[5][:, 0:128], Wp[:], kt[pr][:], start=True, stop=True)
            return eng.matmul(ps[5][:, 128:256], ktw[:], Wtm[:], start=True, stop=True)
        ch.step("tensor", smm)
        ch.step("vector", lambda e: e.scalar_tensor_tensor(
            Sst[:], Sst[:], col(17), ps[5][:, 0:128], ALU.mult, ALU.add))
        ch.step("vector", lambda e: e.scalar_tensor_tensor(
            Rst[:], Rst[:], col(17), ps[5][:, 128:256], ALU.mult, ALU.add))
        ch.step("tensor", lambda e: e.transpose(ps["trb"][:, 0:128], Otm[:],
                                                cst["ident_b"][:]))
        ch.step("vector", lambda e: e.tensor_copy(Ofm[:], ps["trb"][:, 0:128]))
        for h in range(2):
            def ymm(eng, h=h):
                eng.matmul(ps[5][:], Ofm[:], wo[:, h * 512:(h + 1) * 512],
                           start=True, stop=False)
                return eng.matmul(ps[5][:], cst["ones1b"][:1, 0:128],
                                  bo[:1, h * 512:(h + 1) * 512], start=False, stop=True)
            ch.step("tensor", ymm)
            ch.step("vector", lambda e, h=h, ci=ci: e.tensor_add(
                zres[:, h * 512:(h + 1) * 512], ps[5][:],
                XT[:, ci * 1024 + h * 512: ci * 1024 + (h + 1) * 512]))
        emit_ln(ch, env, lnt, zres[:], XT[:, ci * 1024:(ci + 1) * 1024])
        sec2_end[ci] = ch.prev

    emit_sec1(0)
    for ci in range(NT):
        if ci + 1 < NT:
            emit_sec1(ci + 1)
        emit_sec2(ci)

    chf = Chain(p)
    chf.prev = sec2_end[NT - 1]
    return chf


def emit_ln(ch, env, lnt, zres, xt_dst):
    cst = env["cst"]
    ch.step("scalar", lambda e: e.activation(lnt["junk"][:], zres, AF.Copy,
                                             accum_out=lnt["c18"][:]))
    ch.step("vector", lambda e: e.scalar_tensor_tensor(lnt["junk"][:], zres, 1.0, zres,
                                                       ALU.mult, ALU.mult,
                                                       accum_out=lnt["c19"][:]))
    ch.step("vector", lambda e: e.tensor_scalar(lnt["c22"][:], lnt["c18"][:],
                                                lnt["c18"][:], 1.0 / 1024.0,
                                                ALU.mult, ALU.mult))
    ch.step("vector", lambda e: e.tensor_sub(lnt["c23"][:], lnt["c19"][:], lnt["c22"][:]))
    ch.step("scalar", lambda e: e.activation(lnt["c1"][:], lnt["c23"][:], AF.Ln,
                                             scale=1.0 / 1024.0, bias=cst["epscol"][:, 0:1]))
    ch.step("scalar", lambda e: e.activation(lnt["rstd"][:], lnt["c1"][:], AF.Exp, scale=-0.5))
    ch.step("vector", lambda e: e.tensor_scalar(lnt["c20"][:], lnt["c18"][:], 1.0 / 1024.0,
                                                None, ALU.mult))
    ch.step("vector", lambda e: e.tensor_scalar(xt_dst, zres, lnt["c20"][:], lnt["rstd"][:],
                                                ALU.subtract, ALU.mult))


def alloc_lnt(ctx, nc, tag, junk=None):
    sb = lambda name, shape, dt: ctx.enter_context(nc.sbuf_tensor(name, shape, dt))
    cols = sb(tag + "_lncols", [128, 8], F32)
    return {"junk": junk if junk is not None else sb(tag + "_junk", [128, 1024], F32),
            "c18": cols[:, 0:1], "c19": cols[:, 1:2], "c20": cols[:, 2:3],
            "c21": cols[:, 3:4], "c22": cols[:, 4:5], "c23": cols[:, 5:6],
            "c1": cols[:, 6:7], "rstd": cols[:, 7:8]}


def emit_router(p, ch, env, ctx_tiles, n_e, rw, rb, lgt_cb):
    nc = p.nc
    ps = env["ps"]
    XT = env["XT"]
    cst = env["cst"]
    xtr = ctx_tiles["xtr"]
    for ci in range(NT):
        for dc in range(ND):
            ch.step("tensor", lambda e, ci=ci, dc=dc: e.transpose(
                ps["tr"][:], XT[:, ci * 1024 + dc * 128: ci * 1024 + (dc + 1) * 128],
                cst["ident"][:]))
            ch.step("vector", lambda e, dc=dc: e.tensor_copy(
                xtr[:, dc * 128:(dc + 1) * 128], ps["tr"][:]))
        def rmm(eng, n_e=n_e):
            for dc in range(ND):
                eng.matmul(ps[5][:, 0:n_e], xtr[:, dc * 128:(dc + 1) * 128],
                           rw[:, dc * n_e:(dc + 1) * n_e], start=(dc == 0), stop=False)
            return eng.matmul(ps[5][:, 0:n_e], cst["ones1x512f"][:1, 0:128], rb[:],
                              start=False, stop=True)
        ch.step("tensor", rmm)
        lgt_cb(ci)


def stage_b(p, ctx, env):
    nc = p.nc
    ps = env["ps"]
    XT, XFMB = env["XT"], env["XFMB"]
    cst = env["cst"]
    sb = lambda name, shape, dt: ctx.enter_context(nc.sbuf_tensor(name, shape, dt))

    rw = sb("b_rw", [128, ND * 32], F32)
    rb = sb("b_rb", [1, 32], F32)
    b1c = sb("b_b1c", [128, 64], F32)
    w1b = [sb("b_w1b%d" % i, [128, 8 * 256], BF16) for i in range(2)]
    w2b = [sb("b_w2b%d" % i, [128, 2 * 1024], BF16) for i in range(2)]
    b2r = [sb("b_b2r%d" % i, [1, 1024], BF16) for i in range(2)]
    hfm = sb("b_hfm", [128, 2 * 512], BF16)
    acc = sb("b_acc", [128, 4 * 1024], F32)
    xtr = sb("b_xtr", [128, 1024], F32)
    lgt = sb("b_lgt", [128, 32], F32)
    msk = sb("b_msk", [128, NT * 32], F32)
    mcol = sb("b_mcol", [128, 1], F32)
    tmp = sb("b_tmp", [128, 512], BF16)
    zres = sb("b_zres", [128, 1024], F32)
    lnt = alloc_lnt(ctx, nc, "b")

    ch = Chain(p)
    ch.prev = env.get("start_tok")

    p._sem("w")
    def loads(eng):
        eng.dma_start(rw[:], env["d_rwT"]).then_inc(p.sems["w"], 16)
        eng.dma_start(rb[:], env["d_rb"]).then_inc(p.sems["w"], 16)
        eng.dma_start(b1c[:], env["d_b1c"]).then_inc(p.sems["w"], 16)
        return None
    p.emit("sync", loads, waits=[env["xload_tok"]] if env.get("xload_tok") else [])
    p.cnt["w"] += 48
    wtok0 = ("w", p.val("w"))
    if ch.prev is None:
        ch.prev = wtok0

    def lgt_cb(ci):
        ch.step("vector", lambda e: e.tensor_copy(lgt[:], ps[5][:, 0:32]),
                extra_waits=[wtok0] if ci == 0 else [])
        ch.step("vector", lambda e: e.reduce_max(mcol[:], lgt[:], axis=AX.X))
        ch.step("vector", lambda e, ci=ci: e.tensor_scalar(
            msk[:, ci * 32:(ci + 1) * 32], lgt[:], mcol[:], None, ALU.is_equal))
        if env.get("dbgb") and ci == 0:
            def dlgt(eng):
                return eng.dma_start(env["dbgb"]["lgt0"], lgt[:])
            prev = ch.prev
            tk = p.emit("sync", dlgt, waits=[prev], incs=[("sy", 16)])
            ch.prev = ("sy", tk["sy"])
    emit_router(p, ch, env, {"xtr": xtr}, 32, rw, rb, lgt_cb)
    if env.get("dbgb"):
        def dmsk(eng):
            return eng.dma_start(env["dbgb"]["msk"], msk[:])
        prev = ch.prev
        tk = p.emit("sync", dmsk, waits=[prev], incs=[("sy", 16)])
        ch.prev = ("sy", tk["sy"])

    p._sem("pl"); p._sem("pe"); p._sem("ac"); p._sem("dv")
    wtok = {}
    tmp2 = [tmp, ctx.enter_context(nc.sbuf_tensor("b_tmp2", [128, 512], BF16))]

    def emit_wload(s, e_i, wait_tok):
        par = e_i % 2
        def wload(eng, e_i=e_i, par=par):
            eng.dma_start(w1b[par][:], env["d_w1"][e_i]).then_inc(p.sems["pl"], 16)
            eng.dma_start(w2b[par][:], env["d_w2"][e_i]).then_inc(p.sems["pl"], 16)
            eng.dma_start(b2r[par][:], env["d_b2"][e_i:e_i + 1, :]
                          ).then_inc(p.sems["pl"], 16)
            return None
        p.emit("gpsimd", wload, waits=[wait_tok] if wait_tok else [])
        p.cnt["pl"] += 48
        wtok[(s, e_i)] = ("pl", p.val("pl"))

    start_tok = ch.prev
    stage_start = env.get("start_tok")
    hbank_guard = [start_tok, start_tok]
    ybank_guard = [start_tok] * 4
    hfm_guard = None
    wbuf_guard = [stage_start, stage_start]
    acc_guard = {}
    tmp_guard = [None, None]

    for s in range(4):
        tkm = p.emit("vector", lambda e: e.memset(acc[:], 0.0),
                     waits=[start_tok], incs=[("dv", 1)])
        acc_init = ("dv", tkm["dv"])
        for sub in range(4):
            for half in range(2):
                acc_guard[(sub, half)] = acc_init
        emit_wload(s, 0, wbuf_guard[0])
        emit_wload(s, 1, wbuf_guard[1])
        for e_i in range(NE0):
            par = e_i % 2
            relu_toks = []
            for hc in range(2):
                def hmm2(eng, s=s, par=par, hc=hc):
                    last = None
                    for dc in range(ND):
                        last = eng.matmul(
                            ps[hc][:],
                            w1b[par][:, dc * 256 + hc * 128: dc * 256 + (hc + 1) * 128],
                            XFMB[:, dc * 2048 + s * 512: dc * 2048 + (s + 1) * 512],
                            start=(dc == 0), stop=(dc == ND - 1))
                    return last
                waits = [wtok[(s, e_i)], hbank_guard[hc]]
                if hc == 0 and hfm_guard is not None:
                    waits.append(hfm_guard)
                tk = p.emit("tensor", hmm2, waits=waits, incs=[("pe", 1)])
                mm_tok = ("pe", tk["pe"])
                tk = p.emit("scalar",
                            lambda e, e_i=e_i, hc=hc: e.activation(
                                hfm[:, hc * 512:(hc + 1) * 512], ps[hc][:], AF.Relu,
                                bias=b1c[:, e_i * 2 + hc: e_i * 2 + hc + 1]),
                            waits=[mm_tok], incs=[("ac", 1)])
                hbank_guard[hc] = ("ac", tk["ac"])
                relu_toks.append(hbank_guard[hc])
            last_ymm = None
            for sub in range(4):
                for half in range(2):
                    bk = (sub * 2 + half) % 4
                    def ymm(eng, par=par, sub=sub, half=half, bk=bk):
                        for hc in range(2):
                            eng.matmul(ps[2 + bk][:],
                                       hfm[:, hc * 512 + sub * 128: hc * 512 + (sub + 1) * 128],
                                       w2b[par][:, hc * 1024 + half * 512: hc * 1024 + (half + 1) * 512],
                                       start=(hc == 0), stop=False)
                        return eng.matmul(ps[2 + bk][:], cst["ones1b"][:1, 0:128],
                                          b2r[par][:1, half * 512:(half + 1) * 512],
                                          start=False, stop=True)
                    waits = [ybank_guard[bk]]
                    if sub == 0 and half == 0:
                        waits.append(relu_toks[1])
                    tk = p.emit("tensor", ymm, waits=waits, incs=[("pe", 1)])
                    last_ymm = ("pe", tk["pe"])
                    tpar = (sub * 2 + half) % 2
                    awaits = [last_ymm]
                    if tmp_guard[tpar] is not None:
                        awaits.append(tmp_guard[tpar])
                    tk = p.emit("scalar",
                                lambda e, s=s, sub=sub, e_i=e_i, tpar=tpar, bk=bk: e.activation(
                                    tmp2[tpar][:], ps[2 + bk][:], AF.Relu,
                                    scale=msk[:, (s * 4 + sub) * 32 + e_i:
                                              (s * 4 + sub) * 32 + e_i + 1]),
                                waits=awaits, incs=[("ac", 1)])
                    ac_tok = ("ac", tk["ac"])
                    ybank_guard[bk] = ac_tok
                    tk = p.emit("vector",
                                lambda e, sub=sub, half=half, tpar=tpar: e.tensor_add(
                                    acc[:, sub * 1024 + half * 512: sub * 1024 + (half + 1) * 512],
                                    acc[:, sub * 1024 + half * 512: sub * 1024 + (half + 1) * 512],
                                    tmp2[tpar][:]),
                                waits=[ac_tok, acc_guard[(sub, half)]], incs=[("dv", 1)])
                    acc_guard[(sub, half)] = ("dv", tk["dv"])
                    tmp_guard[tpar] = acc_guard[(sub, half)]
            hfm_guard = last_ymm
            wbuf_guard[par] = last_ymm
            if e_i + 2 < NE0:
                emit_wload(s, e_i + 2, wbuf_guard[par])
        ch.prev = ("dv", p.val("dv"))
        for sub in range(4):
            ci = s * 4 + sub
            ch.step("vector", lambda e, ci=ci, sub=sub: e.tensor_add(
                zres[:], acc[:, sub * 1024:(sub + 1) * 1024],
                XT[:, ci * 1024:(ci + 1) * 1024]))
            emit_ln(ch, env, lnt, zres[:], XT[:, ci * 1024:(ci + 1) * 1024])
        start_tok = ch.prev
    return ch


def stage_c(p, ctx, env):
    nc = p.nc
    ps = env["ps"]
    XT, XFMB = env["XT"], env["XFMB"]
    cst = env["cst"]
    sb = lambda name, shape, dt: ctx.enter_context(nc.sbuf_tensor(name, shape, dt))

    woTb = sb("c_wot", [128, 8 * 1024], BF16)
    wqp = [sb("c_wqp%d" % i, [128, 1024], BF16) for i in range(2)]
    wkp = [sb("c_wkp%d" % i, [128, 1024], BF16) for i in range(2)]
    wvp = [sb("c_wvp%d" % i, [128, 1024], BF16) for i in range(2)]
    qf = sb("c_qf", [128, 2048], BF16)
    kf = sb("c_kf", [128, 2048], BF16)
    vt = sb("c_vt", [128, 16 * 130], BF16)
    sq = sb("c_sq", [128, 2048], BF16)
    qrow = sb("c_qrow", [2, 2048], F32)
    nrm = sb("c_nrm", [2, 8], F32)
    negb2 = sb("c_negb", [128, 2], F32)
    pt2 = [sb("c_pt%d" % i, [128, 512], BF16) for i in range(2)]
    oall = sb("c_oall", [128, NT * 1024], BF16)
    ofm = sb("c_ofm", [128, 1024], BF16)
    cmask = sb("c_cmask", [128, 4 * 512], BF16)
    onescol = sb("c_onescol", [128, 1], BF16)
    rcol = sb("c_rcol", [128, 1], F32)
    zres = sb("c_zres", [128, 1024], F32)
    lnt = alloc_lnt(ctx, nc, "c")

    ch = Chain(p)
    ch.prev = env.get("start_tok")
    stage_start = env.get("start_tok")

    p._sem("w"); p._sem("pe"); p._sem("ac"); p._sem("dv")
    def loads(eng):
        for o in range(4):
            eng.dma_start(cmask[:, o * 512:(o + 1) * 512],
                          env["d_cmask"][o]).then_inc(p.sems["w"], 16)
        eng.dma_start(onescol[:], env["d_onescol"]).then_inc(p.sems["w"], 16)
        return None
    p.emit("sync", loads, waits=[env["xload_tok"]] if env.get("xload_tok") else [])
    def loads2(eng):
        for c in range(8):
            eng.dma_start(woTb[:].rearrange("p (c e) -> p c e", c=8)[:, c, :],
                          env["d_wo1T"][c * 128:(c + 1) * 128, :]).then_inc(p.sems["w"], 16)
        return None
    p.emit("gpsimd", loads2, waits=[env["xload_tok"]] if env.get("xload_tok") else [])
    p.cnt["w"] += 13 * 16
    wtok0 = ("w", p.val("w"))
    if ch.prev is None:
        ch.prev = wtok0

    p._sem("pl")
    wtok = {}
    wbuf_guard = [stage_start, stage_start]

    def emit_wload(pr, wait_tok):
        par = pr % 2
        def wload(eng, pr=pr, par=par):
            eng.dma_start(wqp[par][:], env["d_wq1"][pr]).then_inc(p.sems["pl"], 16)
            eng.dma_start(wkp[par][:], env["d_wk1"][pr]).then_inc(p.sems["pl"], 16)
            eng.dma_start(wvp[par][:], env["d_wv1"][pr]).then_inc(p.sems["pl"], 16)
            return None
        p.emit("gpsimd", wload, waits=[wait_tok] if wait_tok else [])
        p.cnt["pl"] += 48
        wtok[pr] = ("pl", p.val("pl"))

    emit_wload(0, wbuf_guard[0])
    emit_wload(1, wbuf_guard[1])

    qkv_bank_guard = [ch.prev, ch.prev]
    oall_guard = None
    for pr in range(8):
        par = pr % 2
        gi = 0
        last_wread = None
        qkv_evacs = []
        for tile, wp in ((qf, wqp[par]), (kf, wkp[par])):
            for st in range(4):
                bk = gi % 2
                def qmm(eng, wp=wp, st=st, bk=bk):
                    last = None
                    for dc in range(ND):
                        last = eng.matmul(ps[bk][:],
                                          wp[:, dc * 128:(dc + 1) * 128],
                                          XFMB[:, dc * 2048 + st * 512: dc * 2048 + (st + 1) * 512],
                                          start=(dc == 0), stop=(dc == ND - 1))
                    return last
                waits = [wtok[pr], qkv_bank_guard[bk]]
                tk = p.emit("tensor", qmm, waits=waits, incs=[("pe", 1)])
                last_wread = ("pe", tk["pe"])
                tk = p.emit("vector",
                            lambda e, tile=tile, st=st, bk=bk: e.tensor_copy(
                                tile[:, st * 512:(st + 1) * 512], ps[bk][:]),
                            waits=[last_wread], incs=[("dv", 1)])
                qkv_bank_guard[bk] = ("dv", tk["dv"])
                qkv_evacs.append(qkv_bank_guard[bk])
                gi += 1
        vw = [qkv_bank_guard[0]]
        if wbuf_guard[par]:
            vw.append(wbuf_guard[par])
        tk = p.emit("vector", lambda e: e.memset(vt[:], 1.0),
                    waits=vw, incs=[("dv", 1)])
        vt_ready = ("dv", tk["dv"])
        for mi in range(16):
            bk = gi % 2
            def vmm(eng, par=par, mi=mi, bk=bk):
                last = None
                for dc in range(ND):
                    last = eng.matmul(ps[bk][:, 0:128],
                                      XFMB[:, dc * 2048 + mi * 128: dc * 2048 + (mi + 1) * 128],
                                      wvp[par][:, dc * 128:(dc + 1) * 128],
                                      start=(dc == 0), stop=(dc == ND - 1))
                return last
            tk = p.emit("tensor", vmm, waits=[qkv_bank_guard[bk], vt_ready],
                        incs=[("pe", 1)])
            last_wread = ("pe", tk["pe"])
            tk = p.emit("vector",
                        lambda e, mi=mi, bk=bk: e.tensor_copy(
                            vt[:].rearrange("p (m hc) -> p m hc", m=16)[:, mi, :]
                                .rearrange("p (h c) -> p h c", h=2)[:, :, 0:64],
                            ps[bk][:, 0:128].rearrange("p (h c) -> p h c", h=2)),
                        waits=[last_wread], incs=[("dv", 1)])
            qkv_bank_guard[bk] = ("dv", tk["dv"])
            qkv_evacs.append(qkv_bank_guard[bk])
            gi += 1
        if pr + 2 < 8:
            emit_wload(pr + 2, last_wread)
        wbuf_guard[par] = last_wread

        ch.prev = qkv_evacs[-1]
        for qk, tile in ((0, qf), (2, kf)):
            ch.step("scalar", lambda e, tile=tile: e.activation(sq[:], tile[:], AF.Square))
            for st in range(4):
                ch.step("tensor", lambda e, st=st: e.matmul(
                    ps[5][0:2, 0:512], cst["onescol2"][:, 0:2],
                    sq[:, st * 512:(st + 1) * 512], start=True, stop=True))
                ch.step("vector", lambda e, st=st: e.tensor_copy(
                    qrow[0:2, st * 512:(st + 1) * 512], ps[5][0:2, 0:512]))
            ch.step("vector", lambda e, qk=qk: e.reduce_max(
                nrm[0:2, qk:qk + 1], qrow[0:2, :], axis=AX.X))
        ch.step("vector", lambda e: e.tensor_mul(nrm[0:2, 4:5], nrm[0:2, 0:1],
                                                 nrm[0:2, 2:3]))
        ch.step("scalar", lambda e: e.activation(nrm[0:2, 5:6], nrm[0:2, 4:5], AF.Ln))
        ch.step("scalar", lambda e: e.activation(nrm[0:2, 6:7], nrm[0:2, 5:6],
                                                 AF.Exp, scale=0.5))
        ch.step("vector", lambda e: e.tensor_scalar(nrm[0:2, 7:8], nrm[0:2, 6:7],
                                                    -0.125, None, ALU.mult))
        def bmm(eng):
            eng.matmul(ps[5][:, 0:2], cst["ones1x512f"][:1, 0:128], nrm[0:1, 6:8],
                       start=True, stop=True)
            return eng.matmul(ps[5][:, 2:4], cst["sel01"][:], nrm[0:2, 6:8],
                              start=True, stop=True)
        ch.step("tensor", bmm)
        ch.step("vector", lambda e: e.tensor_copy(negb2[:, 0:1], ps[5][:, 1:2]))
        ch.step("vector", lambda e: e.tensor_copy(negb2[:, 1:2], ps[5][:, 3:4]))

        for hh in range(2):
            hs = hh * 64
            h_abs = pr * 2 + hh
            negb_tok = ch.prev

            sbank_guard = {0: ch.prev, 1: ch.prev}
            av_done = None
            pt_guard = [None, None]
            for st in range(4):
                nmi = st * 4 + 4
                o_evac_toks = []
                for mi in range(nmi):
                    sb_i = mi % 2
                    spb = ps[0] if sb_i == 0 else ps[5]
                    def smm(eng, hs=hs, mi=mi, st=st, spb=spb):
                        return eng.matmul(spb[:],
                                          kf[hs:hs + 64, mi * 128:(mi + 1) * 128],
                                          qf[hs:hs + 64, st * 512:(st + 1) * 512],
                                          start=True, stop=True)
                    tk = p.emit("tensor", smm, waits=[sbank_guard[sb_i]],
                                incs=[("pe", 1)])
                    smm_tok = ("pe", tk["pe"])
                    ewaits = [smm_tok]
                    if pt_guard[sb_i] is not None:
                        ewaits.append(pt_guard[sb_i])
                    tk = p.emit("scalar",
                                lambda e, sb_i=sb_i, spb=spb, hh=hh: e.activation(
                                    pt2[sb_i][:], spb[:], AF.Exp,
                                    scale=0.125, bias=negb2[:, hh:hh + 1]),
                                waits=ewaits, incs=[("ac", 1)])
                    exp_tok = ("ac", tk["ac"])
                    sbank_guard[sb_i] = exp_tok
                    o = mi - st * 4
                    ready_tok = exp_tok
                    if o >= 0:
                        tk = p.emit("vector",
                                    lambda e, o=o, sb_i=sb_i: e.tensor_mul(
                                        pt2[sb_i][:], pt2[sb_i][:],
                                        cmask[:, o * 512:(o + 1) * 512]),
                                    waits=[exp_tok], incs=[("dv", 1)])
                        ready_tok = ("dv", tk["dv"])
                    def avmm(eng, hh=hh, mi=mi, nmi=nmi, sb_i=sb_i):
                        last = None
                        for sub in range(4):
                            last = eng.matmul(
                                ps[1 + sub][:, 0:65],
                                pt2[sb_i][:, sub * 128:(sub + 1) * 128],
                                vt[:].rearrange("p (m c) -> p m c", m=16)[:, mi,
                                    hh * 65:(hh + 1) * 65],
                                start=(mi == 0), stop=(mi == nmi - 1))
                        return last
                    avw = [ready_tok]
                    if mi == 0 and av_done is not None:
                        avw.append(av_done)
                    elif mi == 0:
                        avw.append(negb_tok)
                    tk = p.emit("tensor", avmm, waits=avw, incs=[("pe", 1)])
                    pt_guard[sb_i] = ("pe", tk["pe"])
                    last_av = ("pe", tk["pe"])
                for sub in range(4):
                    ti = st * 4 + sub
                    tk = p.emit("vector",
                                lambda e, sub=sub: e.reciprocal(
                                    rcol[:], ps[1 + sub][:, 64:65]),
                                waits=[last_av] + ([o_evac_toks[-1]] if o_evac_toks else []),
                                incs=[("dv", 1)])
                    tk = p.emit("vector",
                                lambda e, sub=sub, ti=ti, h_abs=h_abs: e.tensor_scalar(
                                    oall[:, ti * 1024 + h_abs * 64: ti * 1024 + (h_abs + 1) * 64],
                                    ps[1 + sub][:, 0:64], rcol[:], None, ALU.mult),
                                waits=[("dv", tk["dv"])] + ([oall_guard] if oall_guard else []),
                                incs=[("dv", 1)])
                    o_evac_toks.append(("dv", tk["dv"]))
                av_done = o_evac_toks[-1]
            ch.prev = av_done
        qkv_bank_guard = [ch.prev, ch.prev]

    for ti in range(NT):
        def trall(eng, ti=ti):
            last = None
            for c in range(ND):
                last = eng.transpose(
                    ps["trb"][:, c * 128:(c + 1) * 128],
                    oall[:, ti * 1024 + c * 128: ti * 1024 + (c + 1) * 128],
                    cst["ident_b"][:])
            return last
        ch.step("tensor", trall)
        ch.step("vector", lambda e: e.tensor_copy(ofm[:], ps["trb"][:]))
        for half in range(2):
            def pmm(eng, half=half):
                last = None
                for c in range(ND):
                    last = eng.matmul(ps[half][:],
                                      ofm[:, c * 128:(c + 1) * 128],
                                      woTb[:, c * 1024 + half * 512: c * 1024 + (half + 1) * 512],
                                      start=(c == 0), stop=(c == ND - 1))
                return last
            ch.step("tensor", pmm)
            ch.step("vector", lambda e, ti=ti, half=half: e.tensor_add(
                zres[:, half * 512:(half + 1) * 512], ps[half][:],
                XT[:, ti * 1024 + half * 512: ti * 1024 + (half + 1) * 512]))
        emit_ln(ch, env, lnt, zres[:], XT[:, ti * 1024:(ti + 1) * 1024])
    return ch


def build_stage_c_program():
    nc = bass.Bass("TRN2", target_bir_lowering=False)
    env = {}
    d_x = nc.dram_tensor("x", [T, D], F32, kind="ExternalInput").ap()
    d_xT = nc.dram_tensor("xT", [D, T], F32, kind="ExternalInput").ap()
    env["d_wq1"] = nc.dram_tensor("wq1", [8, 128, 1024], F32, kind="ExternalInput").ap()
    env["d_wk1"] = nc.dram_tensor("wk1", [8, 128, 1024], F32, kind="ExternalInput").ap()
    env["d_wv1"] = nc.dram_tensor("wv1", [8, 128, 1024], F32, kind="ExternalInput").ap()
    env["d_wo1T"] = nc.dram_tensor("wo1T", [D, D], F32, kind="ExternalInput").ap()
    env["d_cmask"] = nc.dram_tensor("cmaskc", [4, 128, 512], BF16, kind="ExternalInput").ap()
    env["d_onescol"] = nc.dram_tensor("onescolc", [128, 1], BF16, kind="ExternalInput").ap()
    dcst = declare_consts(nc)
    out = nc.dram_tensor("xc", [T, D], F32, kind="ExternalOutput").ap()

    with ExitStack() as ctx:
        XT = ctx.enter_context(nc.sbuf_tensor("XT", [128, NT * 1024], F32))
        XFMB = ctx.enter_context(nc.sbuf_tensor("XFMB", [128, ND * 2048], BF16))
        cst = {nm: ctx.enter_context(nc.sbuf_tensor("s_" + nm, *CST_SHAPES[nm]))
               for nm in CST_ALL}
        ps = alloc_psum(ctx, nc)
        env.update(XT=XT, XFMB=XFMB, cst=cst, ps=ps)

        pl = Prog(nc)
        p = Prog(nc)
        p._sem("w2")
        pl.sems["w2"] = p.sems["w2"]
        pl.cnt["w2"] = 0

        def xloads(eng):
            for ci in range(NT):
                eng.dma_start(XT[:, ci * 1024:(ci + 1) * 1024],
                              d_x[ci * 128:(ci + 1) * 128, :]).then_inc(p.sems["w2"], 16)
            for nm in CST_ALL:
                eng.dma_start(cst[nm][:], dcst[nm]).then_inc(p.sems["w2"], 16)
            return None
        pl.emit("sync", xloads)

        def xloads2(eng):
            for dc in range(ND):
                eng.dma_start(XFMB[:, dc * 2048:(dc + 1) * 2048],
                              d_xT[dc * 128:(dc + 1) * 128, :]).then_inc(p.sems["w2"], 16)
            return None
        pl.emit("gpsimd", xloads2)
        nw2 = (NT + len(CST_ALL) + ND) * 16
        p.cnt["w2"] += nw2

        def w2wait(eng):
            eng.wait_ge(p.sems["w2"], nw2)
            return None
        pl.emit("gpsimd", w2wait)
        env["xload_tok"] = None

        ch = stage_c(p, ctx, env)

        def stores(eng):
            for ci in range(NT):
                eng.dma_start(out[ci * 128:(ci + 1) * 128, :],
                              XT[:, ci * 1024:(ci + 1) * 1024]).then_inc(p.sems["st"], 16)
            eng.wait_ge(p.sems["st"], NT * 16)
            return None
        p._sem("st")
        p.emit("sync", stores, waits=[ch.prev])
        p.cnt["st"] += NT * 16

        pl.play()
        p.play()
    return nc


def make_stage_c_inmaps(inputs, xs):
    com, _ = prep_host(inputs)
    maps = []
    for b in range(8):
        m = {"x": np.ascontiguousarray(xs[b]), "xT": np.ascontiguousarray(xs[b].T),
             "wq1": com["wq1"], "wk1": com["wk1"], "wv1": com["wv1"],
             "wo1T": com["wo1T"], "cmaskc": com["cmask"], "onescolc": com["onescol_b"]}
        for nm in CST_ALL:
            m["c_" + nm] = com[nm]
        maps.append(m)
    return maps


def stage_d(p, ctx, env, final_store=None):
    nc = p.nc
    ps = env["ps"]
    XT, XFMB = env["XT"], env["XFMB"]
    cst = env["cst"]
    sb = lambda name, shape, dt: ctx.enter_context(nc.sbuf_tensor(name, shape, dt))

    rw = sb("d_rw", [128, ND * 8], F32)
    rb = sb("d_rb", [1, 8], F32)
    wgb = [sb("d_wgb%d" % i, [128, 1024], BF16) for i in range(4)]
    wub = [sb("d_wub%d" % i, [128, 1024], BF16) for i in range(4)]
    wdh = [sb("d_wdh%d" % i, [128, NF1 * 256], BF16) for i in range(2)]
    hall = sb("d_hall", [128, NF1 * 512], BF16)
    sg = sb("d_sg", [128, 512], BF16)
    acc = sb("d_acc", [128, NT * 1024], BF16)
    xtr = sb("d_xtr", [128, 1024], F32)
    lgt = sb("d_lgt", [128, 8], F32)
    l2t = sb("d_l2t", [128, 8], F32)
    msk1 = sb("d_msk1", [128, 8], F32)
    msk2 = sb("d_msk2", [128, 8], F32)
    t8 = sb("d_t8", [128, 8], F32)
    ce = sb("d_ce", [128, NT * 8], F32)
    m1c = sb("d_m1c", [128, 1], F32)
    m2c = sb("d_m2c", [128, 1], F32)
    ec = sb("d_ec", [128, 1], F32)
    w1c = sb("d_w1c", [128, 1], F32)
    w2c = sb("d_w2c", [128, 1], F32)
    zres = sb("d_zres", [128, 1024], F32)
    lnt = alloc_lnt(ctx, nc, "d", junk=xtr[:])

    ch = Chain(p)
    ch.prev = env.get("start_tok")

    p._sem("w")
    def loads(eng):
        eng.dma_start(rw[:], env["d_rw1"]).then_inc(p.sems["w"], 16)
        eng.dma_start(rb[:], env["d_rb1"]).then_inc(p.sems["w"], 16)
        return None
    p.emit("sync", loads, waits=[env["xload_tok"]] if env.get("xload_tok") else [])
    p.cnt["w"] += 32
    wtok0 = ("w", p.val("w"))
    if ch.prev is None:
        ch.prev = wtok0

    def lgt_cb(ci):
        ch.step("vector", lambda e: e.tensor_copy(lgt[:], ps[5][:, 0:8]),
                extra_waits=[wtok0] if ci == 0 else [])
        ch.step("vector", lambda e: e.reduce_max(m1c[:], lgt[:], axis=AX.X))
        ch.step("vector", lambda e: e.tensor_scalar(msk1[:], lgt[:], m1c[:], None,
                                                    ALU.is_equal))
        ch.step("vector", lambda e: e.scalar_tensor_tensor(l2t[:], msk1[:], -1e30,
                                                           lgt[:], ALU.mult, ALU.add))
        ch.step("vector", lambda e: e.reduce_max(m2c[:], l2t[:], axis=AX.X))
        ch.step("vector", lambda e: e.tensor_scalar(msk2[:], l2t[:], m2c[:], None,
                                                    ALU.is_equal))
        ch.step("vector", lambda e: e.tensor_sub(ec[:], m2c[:], m1c[:]))
        ch.step("scalar", lambda e: e.activation(ec[:], ec[:], AF.Exp))
        ch.step("vector", lambda e: e.tensor_scalar_add(ec[:], ec[:], 1.0))
        ch.step("vector", lambda e: e.reciprocal(w1c[:], ec[:]))
        ch.step("vector", lambda e: e.tensor_scalar(w2c[:], w1c[:], -1.0, 1.0,
                                                    ALU.mult, ALU.add))
        ch.step("vector", lambda e: e.tensor_scalar(t8[:], msk1[:], w1c[:], None, ALU.mult))
        ch.step("vector", lambda e, ci=ci: e.scalar_tensor_tensor(
            ce[:, ci * 8:(ci + 1) * 8], msk2[:], w2c[:], t8[:], ALU.mult, ALU.add))
    emit_router(p, ch, env, {"xtr": xtr}, 8, rw, rb, lgt_cb)

    p._sem("pl"); p._sem("pe"); p._sem("ac"); p._sem("dv")
    wtok = {}
    sg2 = [sg, ctx.enter_context(nc.sbuf_tensor("d_sg2", [128, 512], BF16))]
    tmp2 = sg2

    def emit_gload(key, e_i, fc, wait_tok):
        bi = fc % 4
        def wload(eng, e_i=e_i, fc=fc, bi=bi):
            eng.dma_start(wgb[bi][:], env["d_wg1"][e_i, fc]).then_inc(p.sems["pl"], 16)
            eng.dma_start(wub[bi][:], env["d_wu1"][e_i, fc]).then_inc(p.sems["pl"], 16)
            return None
        p.emit("gpsimd", wload, waits=[wait_tok] if wait_tok else [])
        p.cnt["pl"] += 32
        wtok[key] = ("pl", p.val("pl"))

    def emit_dload(key, e_i, qt, par, wait_tok):
        def wload(eng, e_i=e_i, qt=qt, par=par):
            for fc in range(NF1):
                eng.dma_start(wdh[par][:, fc * 256:(fc + 1) * 256],
                              env["d_wd1"][e_i, fc, :, qt * 256:(qt + 1) * 256]
                              ).then_inc(p.sems["pl"], 16)
            return None
        p.emit("gpsimd", wload, waits=[wait_tok] if wait_tok else [])
        p.cnt["pl"] += NF1 * 16
        wtok[key] = ("pl", p.val("pl"))

    start_tok = ch.prev
    stage_start = env.get("start_tok")
    tkm = p.emit("vector", lambda e: e.memset(acc[:], 0.0),
                 waits=[start_tok], incs=[("dv", 1)])
    acc_init = ("dv", tkm["dv"])
    acc_guard = {}
    for ti16 in range(16):
        for half in range(2):
            acc_guard[(ti16, half)] = acc_init

    bank_guard = [start_tok, start_tok]
    y_guard = [start_tok] * 4
    tmp_guard = [None, None]
    hall_guard = None
    dbuf_guard = [stage_start, stage_start]
    gbuf_guard = [stage_start, stage_start, stage_start, stage_start]

    for e_i in range(NE1):
        for ss in range(4):
            t0 = ss * 512
            for ff in range(4):
                emit_gload(("g", e_i, ss, ff), e_i, ff, gbuf_guard[ff])
            emit_dload(("d", e_i, ss, 0), e_i, 0, 0, dbuf_guard[0])
            evac_tok = {}
            for fc in range(NF1):
                par = fc % 2
                psg, psu = (ps[0], ps[1]) if par == 0 else (ps[2], ps[3])
                bi = fc % 4
                def gumm(eng, bi=bi, psg=psg, psu=psu, t0=t0):
                    for dc in range(ND):
                        eng.matmul(psg[:], wgb[bi][:, dc * 128:(dc + 1) * 128],
                                   XFMB[:, dc * 2048 + t0: dc * 2048 + t0 + 512],
                                   start=(dc == 0), stop=(dc == ND - 1))
                    last = None
                    for dc in range(ND):
                        last = eng.matmul(psu[:], wub[bi][:, dc * 128:(dc + 1) * 128],
                                          XFMB[:, dc * 2048 + t0: dc * 2048 + t0 + 512],
                                          start=(dc == 0), stop=(dc == ND - 1))
                    return last
                waits = [wtok[("g", e_i, ss, fc)], bank_guard[par]]
                tk = p.emit("tensor", gumm, waits=waits, incs=[("pe", 1)])
                mm_tok = ("pe", tk["pe"])
                gbuf_guard[bi] = mm_tok
                swaits = [mm_tok]
                if tmp_guard[par] is not None:
                    swaits.append(tmp_guard[par])
                tk = p.emit("scalar",
                            lambda e, psg=psg, par=par: e.activation(
                                sg2[par][:], psg[:], AF.Silu),
                            waits=swaits, incs=[("ac", 1)])
                silu_tok = ("ac", tk["ac"])
                mwaits = [silu_tok]
                if hall_guard is not None:
                    mwaits.append(hall_guard)
                tk = p.emit("vector",
                            lambda e, fc=fc, psu=psu, par=par: e.tensor_mul(
                                hall[:, fc * 512:(fc + 1) * 512], sg2[par][:], psu[:]),
                            waits=mwaits, incs=[("dv", 1)])
                evac_tok[fc] = ("dv", tk["dv"])
                bank_guard[par] = evac_tok[fc]
                if fc + 4 < NF1:
                    emit_gload(("g", e_i, ss, fc + 4), e_i, fc + 4, gbuf_guard[(fc + 4) % 4])
            h_done = evac_tok[NF1 - 1]
            for qt in range(4):
                dpar = qt % 2
                if qt + 1 < 4:
                    emit_dload(("d", e_i, ss, qt + 1), e_i, qt + 1, (qt + 1) % 2,
                               dbuf_guard[(qt + 1) % 2])
                last_dmm = None
                for fc in range(NF1):
                    def dmm(eng, fc=fc, dpar=dpar):
                        last = None
                        for sub in range(4):
                            last = eng.matmul(
                                ps[2 + sub][:, 0:256],
                                hall[:, fc * 512 + sub * 128: fc * 512 + (sub + 1) * 128],
                                wdh[dpar][:, fc * 256:(fc + 1) * 256],
                                start=(fc == 0), stop=(fc == NF1 - 1))
                        return last
                    waits = []
                    if fc == 0:
                        waits = [wtok[("d", e_i, ss, qt)], h_done]
                        waits += [y_guard[j] for j in range(4)]
                    tk = p.emit("tensor", dmm, waits=waits, incs=[("pe", 1)])
                    last_dmm = ("pe", tk["pe"])
                dbuf_guard[dpar] = last_dmm
                hall_guard = last_dmm
                for sub in range(4):
                    ti = ss * 4 + sub
                    tpar = sub % 2
                    awaits = [last_dmm]
                    if tmp_guard[tpar] is not None:
                        awaits.append(tmp_guard[tpar])
                    tk = p.emit("scalar",
                                lambda e, ti=ti, e_i=e_i, sub=sub, tpar=tpar: e.activation(
                                    tmp2[tpar][:, 0:256], ps[2 + sub][:, 0:256], AF.Copy,
                                    scale=ce[:, ti * 8 + e_i: ti * 8 + e_i + 1]),
                                waits=awaits, incs=[("ac", 1)])
                    ac_tok = ("ac", tk["ac"])
                    y_guard[sub] = ac_tok
                    tk = p.emit("vector",
                                lambda e, ti=ti, qt=qt, tpar=tpar: e.tensor_add(
                                    acc[:, ti * 1024 + qt * 256: ti * 1024 + (qt + 1) * 256],
                                    acc[:, ti * 1024 + qt * 256: ti * 1024 + (qt + 1) * 256],
                                    tmp2[tpar][:, 0:256]),
                                waits=[ac_tok, acc_guard[(ti, qt % 2)]], incs=[("dv", 1)])
                    acc_guard[(ti, qt % 2)] = ("dv", tk["dv"])
                    tmp_guard[tpar] = acc_guard[(ti, qt % 2)]
            bank_guard[1] = ("ac", p.val("ac"))

    ch.prev = ("dv", p.val("dv"))
    for ti in range(NT):
        ch.step("vector", lambda e, ti=ti: e.tensor_add(
            zres[:], acc[:, ti * 1024:(ti + 1) * 1024],
            XT[:, ti * 1024:(ti + 1) * 1024]))
        emit_ln(ch, env, lnt, zres[:], XT[:, ti * 1024:(ti + 1) * 1024])
        if final_store is not None:
            final_store(ch, ti)
    return ch


def build_stage_d_program():
    nc = bass.Bass("TRN2", target_bir_lowering=False)
    env = {}
    d_x = nc.dram_tensor("x", [T, D], F32, kind="ExternalInput").ap()
    d_xT = nc.dram_tensor("xT", [D, T], F32, kind="ExternalInput").ap()
    env["d_rw1"] = nc.dram_tensor("rw1", [128, ND * 8], F32, kind="ExternalInput").ap()
    env["d_rb1"] = nc.dram_tensor("rb1", [1, 8], F32, kind="ExternalInput").ap()
    env["d_wg1"] = nc.dram_tensor("wg1", [NE1, NF1, 128, 1024], F32, kind="ExternalInput").ap()
    env["d_wu1"] = nc.dram_tensor("wu1", [NE1, NF1, 128, 1024], F32, kind="ExternalInput").ap()
    env["d_wd1"] = nc.dram_tensor("wd1", [NE1, NF1, 128, 1024], F32, kind="ExternalInput").ap()
    dcst = declare_consts(nc)
    out = nc.dram_tensor("xd", [T, D], F32, kind="ExternalOutput").ap()

    with ExitStack() as ctx:
        XT = ctx.enter_context(nc.sbuf_tensor("XT", [128, NT * 1024], F32))
        XFMB = ctx.enter_context(nc.sbuf_tensor("XFMB", [128, ND * 2048], BF16))
        cst = {nm: ctx.enter_context(nc.sbuf_tensor("s_" + nm, *CST_SHAPES[nm]))
               for nm in CST_ALL}
        ps = alloc_psum(ctx, nc)
        env.update(XT=XT, XFMB=XFMB, cst=cst, ps=ps)

        pl = Prog(nc)
        p = Prog(nc)
        p._sem("w2")
        pl.sems["w2"] = p.sems["w2"]
        pl.cnt["w2"] = 0

        def xloads(eng):
            for ci in range(NT):
                eng.dma_start(XT[:, ci * 1024:(ci + 1) * 1024],
                              d_x[ci * 128:(ci + 1) * 128, :]).then_inc(p.sems["w2"], 16)
            for nm in CST_ALL:
                eng.dma_start(cst[nm][:], dcst[nm]).then_inc(p.sems["w2"], 16)
            return None
        pl.emit("sync", xloads)

        def xloads2(eng):
            for dc in range(ND):
                eng.dma_start(XFMB[:, dc * 2048:(dc + 1) * 2048],
                              d_xT[dc * 128:(dc + 1) * 128, :]).then_inc(p.sems["w2"], 16)
            return None
        pl.emit("gpsimd", xloads2)
        nw2 = (NT + len(CST_ALL) + ND) * 16
        p.cnt["w2"] += nw2

        def w2wait(eng):
            eng.wait_ge(p.sems["w2"], nw2)
            return None
        pl.emit("gpsimd", w2wait)
        env["xload_tok"] = None

        p._sem("st")
        def mk_store(ch, ci):
            def st_fn(eng, ci=ci):
                return eng.dma_start(out[ci * 128:(ci + 1) * 128, :],
                                     XT[:, ci * 1024:(ci + 1) * 1024])
            tk = p.emit("sync", st_fn, waits=[ch.prev], incs=[("st", 16)])
        ch = stage_d(p, ctx, env, final_store=mk_store)

        def stwait(eng):
            eng.wait_ge(p.sems["st"], NT * 16)
            return None
        p.emit("sync", stwait, waits=[ch.prev])

        pl.play()
        p.play()
    return nc


def make_stage_d_inmaps(inputs, xs):
    com, _ = prep_host(inputs)
    maps = []
    for b in range(8):
        m = {"x": np.ascontiguousarray(xs[b]), "xT": np.ascontiguousarray(xs[b].T),
             "rw1": com["rw1"], "rb1": com["rb1"],
             "wg1": com["wg1"], "wu1": com["wu1"], "wd1": com["wd1"]}
        for nm in CST_ALL:
            m["c_" + nm] = com[nm]
        maps.append(m)
    return maps


DBG_TAPS = {
    "qtm": ([128, 128], "f"), "kt": ([128, 128], "f"), "vt": ([128, 128], "f"),
    "gtm": ([128, 128], "f"), "lar": ([128, 2], "f"), "beta": ([128, 1], "f"),
    "lcolr": ([128, 1], "f"), "ccol": ([128, 1], "f"),
    "ELl": ([128, 128], "f"), "ELt": ([128, 128], "f"), "Ast": ([128, 128], "f"),
    "Zt": ([128, 128], "f"), "Bst": ([128, 128], "f"), "Wtm": ([128, 128], "f"),
    "Pt": ([128, 128], "f"), "Otm": ([128, 128], "f"), "Sst": ([128, 128], "f"),
    "Rst": ([128, 128], "f"), "zres": ([128, 1024], "f"),
    "ps0q": ([128, 128], "f"), "xfm0": ([128, 128], "f"), "wq0": ([128, 128], "f"),
    "colsdump": ([128, 24], "f"),
}


def build_stage_a_program(debug=False, truncate=None):
    nc = bass.Bass("TRN2", target_bir_lowering=False)
    env = {}
    d = {}
    d["x"] = nc.dram_tensor("x", [T, D], F32, kind="ExternalInput").ap()
    d["xT"] = nc.dram_tensor("xT", [D, T], F32R, kind="ExternalInput").ap()
    env["d_wqkvg"] = nc.dram_tensor("wqkvg", [D, 512], F32R, kind="ExternalInput").ap()
    env["d_wab"] = nc.dram_tensor("wab", [D, 2], F32R, kind="ExternalInput").ap()
    env["d_wo0"] = nc.dram_tensor("wo0", [C0, D], F32, kind="ExternalInput").ap()
    env["d_bqkvg"] = nc.dram_tensor("bqkvg", [1, 512], F32R, kind="ExternalInput").ap()
    env["d_bab"] = nc.dram_tensor("bab", [1, 2], F32R, kind="ExternalInput").ap()
    env["d_bo0"] = nc.dram_tensor("bo0", [1, D], F32, kind="ExternalInput").ap()
    cst_names = [(nm,) + CST_SHAPES[nm] for nm in CST_ALL]
    dcst = {nm: nc.dram_tensor("c_" + nm, shape, dt, kind="ExternalInput").ap()
            for nm, shape, dt in cst_names}
    out = nc.dram_tensor("xa", [T, D], F32, kind="ExternalOutput").ap()

    with ExitStack() as ctx:
        XT = ctx.enter_context(nc.sbuf_tensor("XT", [128, NT * 1024], F32))
        XFMR = ctx.enter_context(nc.sbuf_tensor("XFMR", [128, ND * 2048], F32R))
        cst = {nm: ctx.enter_context(nc.sbuf_tensor("s_" + nm, shape, dt))
               for nm, shape, dt in cst_names}
        ps = alloc_psum(ctx, nc)
        env.update(XT=XT, XFMR=XFMR, cst=cst, ps=ps)

        pl = Prog(nc)
        p = Prog(nc)
        def xloads(eng):
            last = None
            for ci in range(NT):
                last = eng.dma_start(XT[:, ci * 1024:(ci + 1) * 1024],
                                     d["x"][ci * 128:(ci + 1) * 128, :])
                last.then_inc(p.sems["w2"], 16)
            for dc in range(ND):
                last = eng.dma_start(XFMR[:, dc * 2048:(dc + 1) * 2048],
                                     d["xT"][dc * 128:(dc + 1) * 128, :])
                last.then_inc(p.sems["w2"], 16)
            for nm, _s, _d in cst_names:
                last = eng.dma_start(cst[nm][:], dcst[nm])
                last.then_inc(p.sems["w2"], 16)
            return None
        p._sem("w2")
        pl.sems["w2"] = p.sems["w2"]
        pl.cnt["w2"] = 0
        pl.emit("gpsimd", xloads)
        nw2 = (NT + ND + len(cst_names)) * 16
        p.cnt["w2"] += nw2
        def w2wait(eng):
            eng.wait_ge(pl.sems["w2"], nw2)
            return None
        pl.emit("gpsimd", w2wait)
        env["xload_tok"] = ("w2", p.val("w2"))
        env["pload"] = pl

        env["truncate"] = truncate
        dbg = None
        if debug:
            dbg = {nm: nc.dram_tensor("dbg_" + nm, shape,
                                      F32R if k == "r" else F32,
                                      kind="ExternalOutput").ap()
                   for nm, (shape, k) in DBG_TAPS.items()}
        ch = stage_a(p, ctx, env, dbg=dbg)

        def stores(eng):
            last = None
            for ci in range(NT):
                last = eng.dma_start(out[ci * 128:(ci + 1) * 128, :],
                                     XT[:, ci * 1024:(ci + 1) * 1024])
                last.then_inc(p.sems["st"], 16)
            eng.wait_ge(p.sems["st"], NT * 16)
            return None
        p._sem("st")
        p.emit("sync", stores, waits=[ch.prev])
        p.cnt["st"] += NT * 16

        pl.play()
        p.play()
    return nc


def make_stage_a_inmaps(inputs):
    com, percore = prep_host(inputs)
    maps = []
    for b in range(8):
        m = {"x": percore[b]["x"], "xT": percore[b]["xT"],
             "wqkvg": com["wqkvg"], "wab": com["wab"], "wo0": com["wo0"],
             "bqkvg": com["bqkvg"], "bab": com["bab"], "bo0": com["bo0"]}
        for nm in CST_ALL:
            m["c_" + nm] = com[nm]
        maps.append(m)
    return maps


CST_ALL = ("ident", "ident_b", "twoI", "trilcum", "smask", "umask",
           "ones1x512", "ones1x512f", "ones1b", "eyeI", "epscol", "onescol2",
           "sel01")
CST_SHAPES = {"ident": ([128, 128], F32), "ident_b": ([128, 128], BF16),
              "twoI": ([128, 128], F32), "trilcum": ([128, 128], F32),
              "smask": ([128, 128], F32), "umask": ([128, 128], F32),
              "ones1x512": ([1, 128], F32R), "ones1x512f": ([1, 128], F32),
              "ones1b": ([1, 128], BF16), "eyeI": ([128, 128], F32),
              "epscol": ([128, 1], F32), "onescol2": ([128, 2], BF16),
              "sel01": ([2, 128], F32)}


def declare_consts(nc):
    return {nm: nc.dram_tensor("c_" + nm, CST_SHAPES[nm][0], CST_SHAPES[nm][1],
                               kind="ExternalInput").ap() for nm in CST_ALL}


def build_stage_b_program(debug=False):
    nc = bass.Bass("TRN2", target_bir_lowering=False)
    env = {}
    d_x = nc.dram_tensor("x", [T, D], F32, kind="ExternalInput").ap()
    d_xT = nc.dram_tensor("xT", [D, T], F32, kind="ExternalInput").ap()
    env["d_rwT"] = nc.dram_tensor("rwT", [128, ND * 32], F32, kind="ExternalInput").ap()
    env["d_rb"] = nc.dram_tensor("rb", [1, 32], F32, kind="ExternalInput").ap()
    env["d_b1c"] = nc.dram_tensor("b1c", [128, 64], F32, kind="ExternalInput").ap()
    env["d_w1"] = nc.dram_tensor("w1", [NE0, 128, 8 * H0], F32, kind="ExternalInput").ap()
    env["d_w2"] = nc.dram_tensor("w2", [NE0, 128, 2 * D], F32, kind="ExternalInput").ap()
    env["d_b2"] = nc.dram_tensor("b2", [NE0, D], F32, kind="ExternalInput").ap()
    dcst = declare_consts(nc)
    out = nc.dram_tensor("xb", [T, D], F32, kind="ExternalOutput").ap()
    dbgb = None
    if debug:
        dbgb = {"msk": nc.dram_tensor("dbg_msk", [128, NT * 32], F32, kind="ExternalOutput").ap(),
                "lgt0": nc.dram_tensor("dbg_lgt0", [128, 32], F32, kind="ExternalOutput").ap(),
                "hfm0": nc.dram_tensor("dbg_hfm0", [128, 1024], BF16, kind="ExternalOutput").ap(),
                "xfmb0": nc.dram_tensor("dbg_xfmb0", [128, 512], BF16, kind="ExternalOutput").ap(),
                "w1b0": nc.dram_tensor("dbg_w1b0", [128, 2048], BF16, kind="ExternalOutput").ap(),
                "acc0": nc.dram_tensor("dbg_acc0", [128, 4096], F32, kind="ExternalOutput").ap()}

    with ExitStack() as ctx:
        XT = ctx.enter_context(nc.sbuf_tensor("XT", [128, NT * 1024], F32))
        XFMB = ctx.enter_context(nc.sbuf_tensor("XFMB", [128, ND * 2048], BF16))
        cst = {nm: ctx.enter_context(nc.sbuf_tensor("s_" + nm, *CST_SHAPES[nm]))
               for nm in CST_ALL}
        ps = alloc_psum(ctx, nc)
        env.update(XT=XT, XFMB=XFMB, cst=cst, ps=ps, dbgb=dbgb)

        pl = Prog(nc)
        p = Prog(nc)
        p._sem("w2")
        pl.sems["w2"] = p.sems["w2"]
        pl.cnt["w2"] = 0

        def xloads(eng):
            for ci in range(NT):
                eng.dma_start(XT[:, ci * 1024:(ci + 1) * 1024],
                              d_x[ci * 128:(ci + 1) * 128, :]).then_inc(p.sems["w2"], 16)
            for nm in CST_ALL:
                eng.dma_start(cst[nm][:], dcst[nm]).then_inc(p.sems["w2"], 16)
            return None
        pl.emit("sync", xloads)

        def xloads2(eng):
            for dc in range(ND):
                eng.dma_start(XFMB[:, dc * 2048:(dc + 1) * 2048],
                              d_xT[dc * 128:(dc + 1) * 128, :]).then_inc(p.sems["w2"], 16)
            return None
        pl.emit("gpsimd", xloads2)
        nw2 = (NT + len(CST_ALL) + ND) * 16
        p.cnt["w2"] += nw2

        def w2wait(eng):
            eng.wait_ge(p.sems["w2"], nw2)
            return None
        pl.emit("gpsimd", w2wait)
        env["xload_tok"] = None

        ch = stage_b(p, ctx, env)

        def stores(eng):
            for ci in range(NT):
                eng.dma_start(out[ci * 128:(ci + 1) * 128, :],
                              XT[:, ci * 1024:(ci + 1) * 1024]).then_inc(p.sems["st"], 16)
            eng.wait_ge(p.sems["st"], NT * 16)
            return None
        p._sem("st")
        p.emit("sync", stores, waits=[ch.prev])
        p.cnt["st"] += NT * 16

        pl.play()
        p.play()
    return nc


def make_stage_b_inmaps(inputs, xs):
    com, _ = prep_host(inputs)
    maps = []
    for b in range(8):
        m = {"x": np.ascontiguousarray(xs[b]), "xT": np.ascontiguousarray(xs[b].T),
             "rwT": com["rwT"], "rb": com["rb"], "b1c": com["b1c"],
             "w1": com["w1"], "w2": com["w2"], "b2": com["b2"]}
        for nm in CST_ALL:
            m["c_" + nm] = com[nm]
        maps.append(m)
    return maps


def emit_refresh_xfmb(p, env, xtmb, prev_tok):
    ps, XT, XFMB, cst = env["ps"], env["XT"], env["XFMB"], env["cst"]
    ch = Chain(p)
    ch.prev = prev_tok
    for ci in range(NT):
        ch.step("vector", lambda e, ci=ci: e.tensor_copy(
            xtmb[:], XT[:, ci * 1024:(ci + 1) * 1024]))
        for dc in range(ND):
            ch.step("tensor", lambda e, dc=dc: e.transpose(
                ps["trb"][:, 0:128], xtmb[:, dc * 128:(dc + 1) * 128],
                cst["ident_b"][:]))
            ch.step("vector", lambda e, ci=ci, dc=dc: e.tensor_copy(
                XFMB[:, dc * 2048 + ci * 128: dc * 2048 + (ci + 1) * 128],
                ps["trb"][:, 0:128]))
    return ch.prev


def build_fused_program():
    nc = bass.Bass("TRN2", target_bir_lowering=False)
    env = {}
    d_x = nc.dram_tensor("x", [T, D], F32, kind="ExternalInput").ap()
    d_xT = nc.dram_tensor("xT", [D, T], F32R, kind="ExternalInput").ap()
    env["d_wqkvg"] = nc.dram_tensor("wqkvg", [D, 512], F32R, kind="ExternalInput").ap()
    env["d_wab"] = nc.dram_tensor("wab", [D, 2], F32R, kind="ExternalInput").ap()
    env["d_wo0"] = nc.dram_tensor("wo0", [C0, D], F32, kind="ExternalInput").ap()
    env["d_bqkvg"] = nc.dram_tensor("bqkvg", [1, 512], F32R, kind="ExternalInput").ap()
    env["d_bab"] = nc.dram_tensor("bab", [1, 2], F32R, kind="ExternalInput").ap()
    env["d_bo0"] = nc.dram_tensor("bo0", [1, D], F32, kind="ExternalInput").ap()
    env["d_rwT"] = nc.dram_tensor("rwT", [128, ND * 32], F32, kind="ExternalInput").ap()
    env["d_rb"] = nc.dram_tensor("rb", [1, 32], F32, kind="ExternalInput").ap()
    env["d_b1c"] = nc.dram_tensor("b1c", [128, 64], F32, kind="ExternalInput").ap()
    env["d_w1"] = nc.dram_tensor("w1", [NE0, 128, 8 * H0], F32, kind="ExternalInput").ap()
    env["d_w2"] = nc.dram_tensor("w2", [NE0, 128, 2 * D], F32, kind="ExternalInput").ap()
    env["d_b2"] = nc.dram_tensor("b2", [NE0, D], F32, kind="ExternalInput").ap()
    env["d_wq1"] = nc.dram_tensor("wq1", [8, 128, 1024], F32, kind="ExternalInput").ap()
    env["d_wk1"] = nc.dram_tensor("wk1", [8, 128, 1024], F32, kind="ExternalInput").ap()
    env["d_wv1"] = nc.dram_tensor("wv1", [8, 128, 1024], F32, kind="ExternalInput").ap()
    env["d_wo1T"] = nc.dram_tensor("wo1T", [D, D], F32, kind="ExternalInput").ap()
    env["d_cmask"] = nc.dram_tensor("cmaskc", [4, 128, 512], BF16, kind="ExternalInput").ap()
    env["d_onescol"] = nc.dram_tensor("onescolc", [128, 1], BF16, kind="ExternalInput").ap()
    env["d_rw1"] = nc.dram_tensor("rw1", [128, ND * 8], F32, kind="ExternalInput").ap()
    env["d_rb1"] = nc.dram_tensor("rb1", [1, 8], F32, kind="ExternalInput").ap()
    env["d_wg1"] = nc.dram_tensor("wg1", [NE1, NF1, 128, 1024], F32, kind="ExternalInput").ap()
    env["d_wu1"] = nc.dram_tensor("wu1", [NE1, NF1, 128, 1024], F32, kind="ExternalInput").ap()
    env["d_wd1"] = nc.dram_tensor("wd1", [NE1, NF1, 128, 1024], F32, kind="ExternalInput").ap()
    dcst = declare_consts(nc)
    out = nc.dram_tensor("out", [T, D], F32, kind="ExternalOutput").ap()

    with ExitStack() as ctx:
        XT = ctx.enter_context(nc.sbuf_tensor("XT", [128, NT * 1024], F32))
        xtmb = ctx.enter_context(nc.sbuf_tensor("xtmb", [128, 1024], BF16))
        cst = {nm: ctx.enter_context(nc.sbuf_tensor("s_" + nm, *CST_SHAPES[nm]))
               for nm in CST_ALL}
        ps = alloc_psum(ctx, nc)
        env.update(XT=XT, cst=cst, ps=ps)

        pl = Prog(nc)
        p = Prog(nc)
        p._sem("w2")
        pl.sems["w2"] = p.sems["w2"]
        pl.cnt["w2"] = 0

        with ExitStack() as ctx_a:
            XFMR = ctx_a.enter_context(nc.sbuf_tensor("XFMR", [128, ND * 2048], F32R))
            env["XFMR"] = XFMR

            def xloads(eng):
                for ci in range(NT):
                    eng.dma_start(XT[:, ci * 1024:(ci + 1) * 1024],
                                  d_x[ci * 128:(ci + 1) * 128, :]).then_inc(p.sems["w2"], 16)
                for dc in range(ND):
                    eng.dma_start(XFMR[:, dc * 2048:(dc + 1) * 2048],
                                  d_xT[dc * 128:(dc + 1) * 128, :]).then_inc(p.sems["w2"], 16)
                for nm in CST_ALL:
                    eng.dma_start(cst[nm][:], dcst[nm]).then_inc(p.sems["w2"], 16)
                return None
            pl.emit("gpsimd", xloads)
            nw2 = (NT + ND + len(CST_ALL)) * 16
            p.cnt["w2"] += nw2

            def w2wait(eng):
                eng.wait_ge(p.sems["w2"], nw2)
                return None
            pl.emit("gpsimd", w2wait)
            env["xload_tok"] = ("w2", p.val("w2"))
            env["pload"] = pl

            ch_a = stage_a(p, ctx_a, env)
            tok = ch_a.prev
        env.pop("pload", None)

        XFMB = ctx.enter_context(nc.sbuf_tensor("XFMB", [128, ND * 2048], BF16))
        env["XFMB"] = XFMB
        tok = emit_refresh_xfmb(p, env, xtmb, tok)
        with ExitStack() as ctx_b:
            env["start_tok"] = tok
            env["xload_tok"] = tok
            ch_b = stage_b(p, ctx_b, env)
            tok = ch_b.prev

        tok = emit_refresh_xfmb(p, env, xtmb, tok)
        with ExitStack() as ctx_c:
            env["start_tok"] = tok
            env["xload_tok"] = tok
            ch_c = stage_c(p, ctx_c, env)
            tok = ch_c.prev

        tok = emit_refresh_xfmb(p, env, xtmb, tok)
        with ExitStack() as ctx_d:
            env["start_tok"] = tok
            env["xload_tok"] = tok
            p._sem("st")

            def mk_store(ch, ci):
                def st_fn(eng, ci=ci):
                    return eng.dma_start(out[ci * 128:(ci + 1) * 128, :],
                                         XT[:, ci * 1024:(ci + 1) * 1024])
                p.emit("sync", st_fn, waits=[ch.prev], incs=[("st", 16)])
            ch_d = stage_d(p, ctx_d, env, final_store=mk_store)

            def stwait(eng):
                eng.wait_ge(p.sems["st"], NT * 16)
                return None
            p.emit("sync", stwait, waits=[ch_d.prev])

        pl.play()
        p.play()
    return nc


def make_fused_inmaps(inputs):
    com, percore = prep_host(inputs)
    names_a = ("wqkvg", "wab", "wo0", "bqkvg", "bab", "bo0")
    maps = []
    for b in range(8):
        m = {"x": percore[b]["x"], "xT": percore[b]["xT"]}
        for nm in names_a:
            m[nm] = com[nm]
        m.update(rwT=com["rwT"], rb=com["rb"], b1c=com["b1c"], w1=com["w1"],
                 w2=com["w2"], b2=com["b2"],
                 wq1=com["wq1"], wk1=com["wk1"], wv1=com["wv1"], wo1T=com["wo1T"],
                 cmaskc=com["cmask"], onescolc=com["onescol_b"],
                 rw1=com["rw1"], rb1=com["rb1"], wg1=com["wg1"], wu1=com["wu1"],
                 wd1=com["wd1"])
        for nm in CST_ALL:
            m["c_" + nm] = com[nm]
        maps.append(m)
    return maps


_PROGS = {}


def kernel_unfused(**inputs):
    if "A" not in _PROGS:
        _PROGS["A"] = build_stage_a_program()
        _PROGS["B"] = build_stage_b_program()
        _PROGS["C"] = build_stage_c_program()
        _PROGS["D"] = build_stage_d_program()
    cores = list(range(8))
    res = run_bass_kernel_spmd(_PROGS["A"], make_stage_a_inmaps(inputs), core_ids=cores)
    xa = np.stack([r["xa"] for r in res.results], axis=0)
    res = run_bass_kernel_spmd(_PROGS["B"], make_stage_b_inmaps(inputs, xa), core_ids=cores)
    xb = np.stack([r["xb"] for r in res.results], axis=0)
    res = run_bass_kernel_spmd(_PROGS["C"], make_stage_c_inmaps(inputs, xb), core_ids=cores)
    xc = np.stack([r["xc"] for r in res.results], axis=0)
    res = run_bass_kernel_spmd(_PROGS["D"], make_stage_d_inmaps(inputs, xc), core_ids=cores)
    return np.stack([r["xd"] for r in res.results], axis=0)


def kernel(**inputs):
    if "F" not in _PROGS:
        _PROGS["F"] = build_fused_program()
    res = run_bass_kernel_spmd(_PROGS["F"], make_fused_inmaps(inputs),
                               core_ids=list(range(8)))
    return np.stack([r["out"] for r in res.results], axis=0)
